# version 01
# baseline (speedup 1.0000x reference)
import numpy as np
import ml_dtypes
import concourse.bass as bass
import concourse.mybir as mybir
from concourse.bass_utils import run_bass_kernel_spmd
from contextlib import ExitStack

F32 = mybir.dt.float32
BF16 = mybir.dt.bfloat16
I32 = mybir.dt.int32
AF = mybir.ActivationFunctionType
ALU = mybir.AluOpType
AX = mybir.AxisListType
BF16NP = ml_dtypes.bfloat16

NT = 2048
DM = 4096
HB = 32
NTILE = 16
NGRP = 8
HPG = 4
DH = 128
IFF = 11008
IB = 86
EPS = 1e-5
INVSQ = float(1.0 / np.sqrt(128.0))
NEGMASK = -30000.0
LN2 = float(np.log(2.0))

ENGINES = ("pe", "act", "dve", "pool", "sp")
DMAQ = {"sp": "sp_dma", "act": "act_dma", "pool": "pool_dma"}


class Ctx:

    def __init__(self, nc):
        self.nc = nc
        self.streams = {e: [] for e in ENGINES}
        self.count = {k: 0 for k in ENGINES}
        self.vc = {e: {} for e in ENGINES}
        self.sems = {}

    def _waits_for(self, e, deps):
        waits = []
        for t in deps:
            if t is None:
                continue
            key, val, snap = t
            if self.vc[e].get(key, 0) < val:
                waits.append((key, val))
                for k2, v2 in snap.items():
                    if self.vc[e].get(k2, 0) < v2:
                        self.vc[e][k2] = v2
        return waits

    STRICT = ("dve", "act", "pool")

    def op(self, e, fn, deps=(), inc=False):
        waits = self._waits_for(e, deps)
        token = None
        semkey = None
        if inc or e in self.STRICT:
            semkey = e
            self.count[e] += 1
            snap = dict(self.vc[e])
            snap[e] = self.count[e]
            token = (e, self.count[e], snap)
            if e not in self.STRICT:
                self.vc[e][e] = self.count[e]
        self.streams[e].append((waits, fn, semkey, 1))
        return token

    def dma(self, e, fn, deps=(), ch="main"):
        waits = self._waits_for(e, deps)
        q = f"{e}:{ch}"
        self.count[q] = self.count.get(q, 0) + 16
        snap = dict(self.vc[e])
        snap[q] = self.count[q]
        token = (q, self.count[q], snap)
        self.vc[e][q] = self.count[q]
        self.streams[e].append((waits, fn, q, 16))
        return token

    def barrier(self):
        toks = []
        for k, v in self.count.items():
            if v > 0:
                toks.append((k, v, {k: v}))
        return toks

    def emit(self, block):
        starters = {"pe": block.tensor, "act": block.scalar, "dve": block.vector,
                    "pool": block.gpsimd, "sp": block.sync}
        final_counts = dict(self.count)
        for k, v in final_counts.items():
            if v > 0 and k not in self.sems:
                self.sems[k] = self.nc.alloc_semaphore(f"sem_{k.replace(':', '_')}")

        def body_for(name):
            def body(eng):
                for waits, fn, semkey, incval in self.streams[name]:
                    for (k, v) in waits:
                        eng.wait_ge(self.sems[k], v)
                    ins = fn(eng)
                    if semkey is not None:
                        ins.then_inc(self.sems[semkey], incval)
                if name == "sp":
                    for k, v in final_counts.items():
                        if v > 0:
                            eng.wait_ge(self.sems[k], v)
            return body

        for name in ENGINES:
            if self.streams[name] or name == "sp":
                starters[name](body_for(name))


def make_const_arrays():
    c = {}
    c["ident"] = np.eye(128, dtype=BF16NP)
    c["lt16"] = np.triu(np.full((128, 128), 1.0 / 16.0)).astype(BF16NP)
    c["ones16_col"] = np.full((128, 1), 1.0 / 16.0, dtype=BF16NP)
    c["ones_col_bf"] = np.ones((128, 1), dtype=BF16NP)
    c["ones_row_bf"] = np.ones((1, 128), dtype=BF16NP)
    c["ones_row_f32"] = np.ones((1, 128), dtype=np.float32)
    c["ones2"] = np.ones((2, 128), dtype=BF16NP)
    c["negln2_col"] = np.full((128, 1), -np.log(2.0), dtype=np.float32)
    m = (np.arange(128)[None, :] >= np.arange(128)[:, None]).astype(np.float32)
    c["gla_mask"] = np.tile(m, (1, 4)).astype(BF16NP)
    r = np.arange(128)[:, None]
    cc = np.arange(256)[None, :]
    valid = (cc >= r + 64) & (cc <= r + 128)
    mask = np.where(valid, 0.0, NEGMASK).astype(np.float32)
    c["swa_mask"] = mask
    mask0 = mask.copy()
    mask0[:, :128] = NEGMASK
    c["swa_mask0"] = mask0
    inv = 10000.0 ** (-np.arange(64) / 64.0)
    ang = np.arange(NT)[:, None] * inv[None, :]
    c["cos_t"] = np.cos(ang).reshape(NTILE, 128, 64).transpose(1, 0, 2).reshape(128, NTILE * 64).astype(BF16NP)
    c["sin_t"] = np.sin(ang).reshape(NTILE, 128, 64).transpose(1, 0, 2).reshape(128, NTILE * 64).astype(BF16NP)
    return c


class Builder:
    def __init__(self, debug=(), phases=("A",)):
        self.debug = set(debug)
        self.phases = set(phases)
        nc = bass.Bass("TRN2", target_bir_lowering=False)
        self.nc = nc
        self.x = nc.dram_tensor("x", [NT, DM], F32, kind="ExternalInput")
        self.wq = nc.dram_tensor("wq", [DM, DM], F32, kind="ExternalInput")
        self.wk = nc.dram_tensor("wk", [DM, 1024], F32, kind="ExternalInput")
        self.wv = nc.dram_tensor("wv", [DM, 1024], F32, kind="ExternalInput")
        self.wo = nc.dram_tensor("wo", [DM, DM], F32, kind="ExternalInput")
        self.gq = nc.dram_tensor("gq", [DM, IFF], I32, kind="ExternalInput")
        self.uq = nc.dram_tensor("uq", [DM, IFF], I32, kind="ExternalInput")
        self.dq = nc.dram_tensor("dq", [IFF, DM], I32, kind="ExternalInput")
        self.gsc = nc.dram_tensor("gsc", [1, IFF], F32, kind="ExternalInput")
        self.usc = nc.dram_tensor("usc", [1, IFF], F32, kind="ExternalInput")
        self.dsc = nc.dram_tensor("dsc", [1, DM], F32, kind="ExternalInput")
        self.ga = nc.dram_tensor("ga", [DM, 32], F32, kind="ExternalInput")
        self.gbm = nc.dram_tensor("gbm", [32, IFF], F32, kind="ExternalInput")
        self.ua = nc.dram_tensor("ua", [DM, 32], F32, kind="ExternalInput")
        self.ubm = nc.dram_tensor("ubm", [32, IFF], F32, kind="ExternalInput")
        self.da = nc.dram_tensor("da", [128, IB, 32], F32, kind="ExternalInput")
        self.dbm = nc.dram_tensor("dbm", [32, DM], F32, kind="ExternalInput")
        self.out = nc.dram_tensor("out", [NT, DM], F32, kind="ExternalOutput")
        self.h_sp = nc.dram_tensor("h_sp", [IFF, NT], BF16, kind="Internal")
        if "ozero" in self.debug:
            self.o_sp = nc.dram_tensor("o_sp", [DM, NT], BF16, kind="ExternalInput")
        else:
            self.o_sp = nc.dram_tensor("o_sp", [DM, NT], BF16, kind="Internal")
        self.o_done = []
        self.c_end = []
        self.gb_sc = nc.dram_tensor("gb_sc", [32, IFF], BF16, kind="Internal")
        self.ub_sc = nc.dram_tensor("ub_sc", [32, IFF], BF16, kind="Internal")
        self.r1_sp = nc.dram_tensor("r1_sp", [1, 2 * NT], BF16, kind="Internal")
        self.cnp = make_const_arrays()
        self.cin = {k: nc.inline_tensor(v, name="c_" + k) for k, v in self.cnp.items()}
        self.dbg = {}
        if "xn" in self.debug:
            self.dbg["xn"] = nc.dram_tensor("dbg_xn", [DM, NT], BF16, kind="ExternalOutput")
        if "hcopy" in self.debug:
            self.dbg["hcopy"] = nc.dram_tensor("dbg_h", [IFF, NT], BF16, kind="ExternalOutput")
        if "astats" in self.debug:
            self.dbg["astats"] = nc.dram_tensor("dbg_astats", [128, 34], F32, kind="ExternalOutput")
        if "ocopy" in self.debug:
            self.dbg["ocopy"] = nc.dram_tensor("dbg_o", [DM, NT], BF16, kind="ExternalOutput")
        if "cg" in self.debug:
            self.dbg["cg"] = nc.dram_tensor("dbg_cg", [128, 2 * 512], F32, kind="ExternalOutput")
            self.dbg["cg2"] = nc.dram_tensor("dbg_cg2", [32, 1152], F32, kind="ExternalOutput")

    def sb(self, stack, name, shape, dtype):
        return stack.enter_context(self.nc.sbuf_tensor(name, shape, dtype))

    def ps(self, stack, name, shape, dtype=F32):
        return stack.enter_context(self.nc.psum_tensor(name, shape, dtype))

    def build(self):
        nc = self.nc
        with ExitStack() as st:
            ctx = Ctx(nc)
            self.ctx = ctx

            self.c_ident = self.sb(st, "sc_ident", [128, 128], BF16)

            t_id = ctx.dma("sp", lambda e: e.dma_start(out=self.c_ident[:], in_=self.cin["ident"][:]), ch="const")
            self.tok_ident = t_id

            self.out_prefill = []
            for i in range(4):
                rows = slice(i * 512, (i + 1) * 512)
                self.out_prefill.append(ctx.dma(
                    "act", lambda e, rows=rows: e.dma_start(out=self.out[rows, :], in_=self.x[rows, :]),
                    ch="prefill"))

            with ExitStack() as st_abc:
                self.xn_fm = self.sb(st_abc, "xn_fm", [128, HB * NT], BF16)
                self.phase_A(st_abc)
                if "B" in self.phases:
                    self.phase_B(st_abc)
                if "C" in self.phases:
                    self.phase_C(st_abc)
                if "xn" in self.debug:
                    xr = self.dbg["xn"][:].rearrange("(hb p) t -> p hb t", p=128)
                    sr = self.xn_fm[:].rearrange("p (hb t) -> p hb t", hb=HB)
                    ctx.dma("sp", lambda e: e.dma_start(out=xr, in_=sr),
                            deps=self.xn_done, ch="dbg")
            if "ocopy" in self.debug and "B" in self.phases:
                for i in range(2):
                    rows = slice(i * 2048, (i + 1) * 2048)
                    ctx.dma("act", lambda e, rows=rows: e.dma_start(
                        out=self.dbg["ocopy"][rows, :], in_=self.o_sp[rows, :]),
                        deps=self.b_end, ch="dbg")
            if "hcopy" in self.debug and "C" in self.phases:
                for i in range(4):
                    rows = slice(i * 2752, (i + 1) * 2752)
                    ctx.dma("act", lambda e, rows=rows: e.dma_start(
                        out=self.dbg["hcopy"][rows, :], in_=self.h_sp[rows, :]),
                        deps=self.c_end, ch="dbg")
            if "D" in self.phases:
                self.phase_D(st)

            with nc.Block() as block:
                ctx.emit(block)
        return nc

    def phase_A(self, st_abc):
        nc, ctx = self.nc, self.ctx
        with ExitStack() as sA:
            xbuf = [self.sb(sA, f"A_x{p}", [128, DM], F32) for p in range(2)]
            xn_tm = [self.sb(sA, f"A_xn{p}", [128, DM], BF16) for p in range(2)]
            sqscr = xn_tm
            s1c = self.sb(sA, "A_s1", [128, NTILE], F32)
            s2c = self.sb(sA, "A_s2", [128, NTILE], F32)
            negmc = self.sb(sA, "A_negm", [128, NTILE], F32)
            msqc = self.sb(sA, "A_msq", [128, NTILE], F32)
            varc = self.sb(sA, "A_var", [128, NTILE], F32)
            sdc = self.sb(sA, "A_sd", [128, NTILE], F32)
            rstdc = self.sb(sA, "A_rstd", [128, NTILE], F32)
            ps_tr = [self.ps(sA, f"A_pstr{b}", [128, 1024], BF16) for b in range(2)]
            ps_sum = self.ps(sA, "A_pssum", [1, 512], F32)

            war_x = [(), ()]
            war_xn = [None, None]
            war_bank = [None, None]
            evac_toks = []
            for t in range(NTILE):
                p = t % 2
                tok = slice(t * 128, (t + 1) * 128)
                L = ctx.dma("sp", lambda e, p=p, tok=tok: e.dma_start(
                    out=xbuf[p][:], in_=self.x[tok, :]), deps=war_x[p], ch=f"x{p}")
                tsq = ctx.op("act", lambda e, p=p, t=t: e.activation(
                    sqscr[p][:], xbuf[p][:], AF.Square, accum_out=s2c[:, t:t + 1]),
                    deps=[L, war_xn[p]], inc=True)
                ts1 = ctx.op("dve", lambda e, p=p, t=t: e.reduce_sum(
                    s1c[:, t:t + 1], xbuf[p][:], axis=AX.X), deps=[L], inc=True)
                negm = negmc[:, t:t + 1]
                msq = msqc[:, t:t + 1]
                varep = varc[:, t:t + 1]
                sd = sdc[:, t:t + 1]
                rstd = rstdc[:, t:t + 1]
                tnm = ctx.op("dve", lambda e, t=t, negm=negm: e.tensor_scalar(
                    negm, s1c[:, t:t + 1], -1.0 / DM, None, ALU.mult), deps=[ts1])
                tmq = ctx.op("dve", lambda e, negm=negm, msq=msq: e.tensor_mul(msq, negm, negm),
                             deps=[tnm])
                tva = ctx.op("dve", lambda e, t=t, varep=varep: e.tensor_scalar(
                    varep, s2c[:, t:t + 1], 1.0 / DM, EPS, ALU.mult, ALU.add),
                    deps=[tsq])
                tv = ctx.op("dve", lambda e, varep=varep, msq=msq: e.tensor_sub(varep, varep, msq),
                            deps=[tva, tmq], inc=True)
                tsd = ctx.op("act", lambda e, sd=sd, varep=varep: e.activation(sd, varep, AF.Sqrt),
                             deps=[tv], inc=True)
                tr = ctx.op("dve", lambda e, rstd=rstd, sd=sd: e.reciprocal(rstd, sd), deps=[tsd], inc=True)
                txn = ctx.op("dve", lambda e, p=p, negm=negm, rstd=rstd: e.tensor_scalar(
                    xn_tm[p][:], xbuf[p][:], negm, rstd, ALU.add, ALU.mult),
                    deps=[war_xn[p], L, tsq, tr, tnm], inc=True)
                war_x[p] = (txn, tsq)
                last_tr = None
                for g8 in range(4):
                    b = g8 % 2
                    grp_tok = None
                    for j in range(8):
                        d = g8 * 8 + j
                        grp_tok = ctx.op("pe", lambda e, p=p, d=d, b=b, j=j: e.transpose(
                            ps_tr[b][:, j * 128:(j + 1) * 128],
                            xn_tm[p][:, d * 128:(d + 1) * 128], self.c_ident[:]),
                            deps=[txn, self.tok_ident, war_bank[b]] if j == 0 else (),
                            inc=(j == 7))
                    dst = self.xn_fm[:].rearrange("p (hb t) -> p hb t", hb=HB)[:, g8 * 8:(g8 + 1) * 8, t * 128:(t + 1) * 128]
                    ev = ctx.op("dve", lambda e, b=b, dst=dst: e.tensor_copy(dst, ps_tr[b][:].rearrange("p (j t) -> p j t", j=8)),
                                deps=[grp_tok], inc=True)
                    war_bank[b] = ev
                    evac_toks.append(ev)
                    last_tr = grp_tok
                war_xn[p] = last_tr

            ones_c = self.sb(sA, "A_ones", [128, 1], BF16)
            xnsum_row = self.sb(sA, "A_xnsum", [1, NT], F32)
            tone = ctx.dma("sp", lambda e: e.dma_start(out=ones_c[:], in_=self.cin["ones_col_bf"][:]), ch="const")
            r1f = self.sb(sA, "A_r1f", [1, NT], F32)
            r1bf = self.sb(sA, "A_r1bf", [1, NT], BF16)
            ev = None
            sum_evs = []
            for ch in range(4):
                cols = slice(ch * 512, (ch + 1) * 512)
                mt = None
                for hb in range(HB):
                    mt = ctx.op("pe", lambda e, hb=hb, cols=cols: e.matmul(
                        ps_sum[:], ones_c[:],
                        self.xn_fm[:, hb * NT + cols.start: hb * NT + cols.stop],
                        start=(hb == 0), stop=(hb == HB - 1)),
                        deps=(evac_toks + [tone, ev]) if hb == 0 else (),
                        inc=(hb == HB - 1))
                ev = ctx.op("dve", lambda e, cols=cols: e.tensor_copy(
                    xnsum_row[0:1, cols], ps_sum[:]), deps=[mt], inc=True)
                sum_evs.append(ev)
            t1 = ctx.op("dve", lambda e: e.tensor_scalar(
                r1f[:], xnsum_row[:], -7.5, None, ALU.mult), deps=sum_evs, inc=True)
            t2 = ctx.op("dve", lambda e: e.tensor_copy(r1bf[:], r1f[:]), deps=[t1], inc=True)
            s1t = ctx.dma("sp", lambda e: e.dma_start(out=self.r1_sp[0:1, 0:NT], in_=r1bf[:]),
                          deps=[t2], ch="r1w")
            t3 = ctx.op("dve", lambda e: e.tensor_sub(r1f[:], r1f[:], r1bf[:]), deps=[t2])
            t4 = ctx.op("dve", lambda e: e.tensor_copy(r1bf[:], r1f[:]), deps=[t3, s1t], inc=True)
            s2t = ctx.dma("sp", lambda e: e.dma_start(out=self.r1_sp[0:1, NT:2 * NT], in_=r1bf[:]),
                          deps=[t4], ch="r1w")
            self.xn_done = evac_toks + [t4, s2t]
            if "astats" in self.debug:
                stat = self.sb(sA, "A_stat", [128, 34], F32)
                bar = ctx.barrier()
                d1 = ctx.op("dve", lambda e: e.tensor_copy(stat[:, 0:16], s1c[:]), deps=bar, inc=True)
                ctx.op("dve", lambda e: e.tensor_copy(stat[:, 16:32], s2c[:]))
                ctx.op("dve", lambda e: e.tensor_copy(stat[:, 32:33], negmc[:, NTILE - 1:NTILE]))
                d2 = ctx.op("dve", lambda e: e.tensor_copy(stat[:, 33:34], rstdc[:, NTILE - 1:NTILE]), inc=True)
                ctx.dma("sp", lambda e: e.dma_start(out=self.dbg["astats"][:], in_=stat[:]), deps=[d1, d2], ch="dbg")
            self.a_end = ctx.barrier()

    def phase_B(self, st_abc):
        nc, ctx = self.nc, self.ctx
        bbar = ctx.barrier()
        with ExitStack() as sB:
            wqb = self.sb(sB, "B_wq", [128, HB * 512], BF16)
            wkvb = self.sb(sB, "B_wkv", [128, HB * 256], BF16)
            v_tm = self.sb(sB, "B_vtm", [128, NTILE * 128], BF16)
            krf = self.sb(sB, "B_krf", [128, NTILE * 128], BF16)
            S_f = self.sb(sB, "B_S", [128, 128], F32)
            S_t1 = self.sb(sB, "B_St1", [128, 128], F32)
            S_bf = self.sb(sB, "B_Sbf", [128, 128], BF16)
            ea_c = self.sb(sB, "B_ea", [128, 1], F32)
            cos_sb = self.sb(sB, "B_cos", [128, NTILE * 64], BF16)
            sin_sb = self.sb(sB, "B_sin", [128, NTILE * 64], BF16)
            lt16_sb = self.sb(sB, "B_lt16", [128, 128], BF16)
            o16_sb = self.sb(sB, "B_o16", [128, 1], BF16)
            glam = self.sb(sB, "B_glam", [128, 512], BF16)
            swam = self.sb(sB, "B_swam", [128, 256], F32)
            nl2 = self.sb(sB, "B_nl2", [128, 1], F32)
            q_t = [self.sb(sB, f"B_q{p}", [128, 512], BF16) for p in range(2)]
            k_t = [self.sb(sB, f"B_k{p}", [128, 128], BF16) for p in range(2)]
            qe = self.sb(sB, "B_qe", [128, 512], BF16)
            ke = self.sb(sB, "B_ke", [128, 128], BF16)
            gsc1 = self.sb(sB, "B_gsc1", [128, 128], BF16)
            gsc2 = self.sb(sB, "B_gsc2", [128, 128], BF16)
            gs = self.sb(sB, "B_gs", [128, 128], BF16)
            eA_sb = self.sb(sB, "B_eA", [128, 128], BF16)
            enA_sb = self.sb(sB, "B_enA", [128, 128], BF16)
            qt_tm = self.sb(sB, "B_qttm", [128, 512], BF16)
            kt_tm = self.sb(sB, "B_kttm", [128, 128], BF16)
            qkfm = self.sb(sB, "B_qkfm", [128, 640], BF16)
            Pbf = self.sb(sB, "B_Pbf", [128, 512], BF16)
            obf = self.sb(sB, "B_obf", [128, 512], BF16)
            qrt = self.sb(sB, "B_qrt", [128, 128], BF16)
            rsc1 = self.sb(sB, "B_rsc1", [128, 64], BF16)
            rsc2 = self.sb(sB, "B_rsc2", [128, 64], BF16)
            qkrfm = self.sb(sB, "B_qkrfm", [128, 640], BF16)
            scs = self.sb(sB, "B_scs", [128, 256], F32)
            pe_b = self.sb(sB, "B_peb", [128, 256], BF16)
            psc = self.sb(sB, "B_psc", [128, 256], BF16)
            pT = self.sb(sB, "B_pT", [128, 256], BF16)
            nmx = self.sb(sB, "B_nmx", [128, 16], F32)
            se_c = self.sb(sB, "B_se", [128, 16], F32)
            rr_c = self.sb(sB, "B_rr", [128, 16], F32)
            ps_q = self.ps(sB, "B_psq", [128, 512], F32)
            ps_kv = self.ps(sB, "B_pskv", [128, 256], F32)
            ps_misc = self.ps(sB, "B_psmisc", [128, 512], F32)
            ps_P = self.ps(sB, "B_psP", [128, 512], F32)
            ps_o = self.ps(sB, "B_pso", [128, 512], F32)
            ps_S = self.ps(sB, "B_psS", [128, 128], F32)
            ps_t1 = self.ps(sB, "B_pst1", [128, 1024], BF16)
            ps_t2 = self.ps(sB, "B_pst2", [128, 1024], BF16)

            cl = bbar
            for nm, dst in (("cos_t", cos_sb), ("sin_t", sin_sb), ("lt16", lt16_sb),
                            ("ones16_col", o16_sb), ("gla_mask", glam), ("swa_mask", swam),
                            ("negln2_col", nl2)):
                tokc = ctx.dma("sp", lambda e, nm=nm, dst=dst: e.dma_start(
                    out=dst[:], in_=self.cin[nm][:]), deps=cl, ch="const")
            consts = [tokc, self.tok_ident]

            wqr = self.wq[:].rearrange("(hb p) o -> p hb o", p=128)
            wkr = self.wk[:].rearrange("(hb p) o -> p hb o", p=128)
            wvr = self.wv[:].rearrange("(hb p) o -> p hb o", p=128)
            wqbr = wqb[:].rearrange("p (hb o) -> p hb o", o=512)
            wkvbr = wkvb[:].rearrange("p (hb o) -> p hb o", o=256)
            ospr = self.o_sp[:].rearrange("(g j p) t -> p g j t", p=128, j=HPG)

            W = {"wq": bbar, "wkv": bbar, "psq": [None], "pskv": [None],
                 "qt": [None, None], "kt": [None, None], "vtm": [None],
                 "qe": None, "ke": None, "gs": None, "misc": [None], "P": [None],
                 "o": [None], "S": [None], "t1": [None], "t2": [None],
                 "qttm": None, "kttm": None, "qkfm": None, "Pbf": None, "obf": None,
                 "qrt": None, "rsc": None, "qkrfm": None, "scs": None, "peb": None,
                 "psc": None, "pT": None, "Sbf_ready": None, "Sf_ready": None,
                 "nmx": [None] * 16, "se": [None] * 16, "rr": [None] * 16,
                 "krf": [None], "ea": None, "eA": None, "enA": None,
                 "osp": [None, None]}
            qkv_toks = {}
            o_states = []
            LQ = {"wq": None, "wkv": None}

            def emit_qkv(g, t):
                par = t % 2
                mq = None
                for hb in range(HB):
                    mq = ctx.op("pe", lambda e, hb=hb, t=t: e.matmul(
                        ps_q[:], self.xn_fm[:, hb * NT + t * 128: hb * NT + (t + 1) * 128],
                        wqb[:, hb * 512:(hb + 1) * 512],
                        start=(hb == 0), stop=(hb == HB - 1)),
                        deps=([LQ["wq"]] + W["psq"]) if hb == 0 else (),
                        inc=(hb == HB - 1))
                mkv = None
                for hb in range(HB):
                    mkv = ctx.op("pe", lambda e, hb=hb, t=t: e.matmul(
                        ps_kv[:], self.xn_fm[:, hb * NT + t * 128: hb * NT + (t + 1) * 128],
                        wkvb[:, hb * 256:(hb + 1) * 256],
                        start=(hb == 0), stop=(hb == HB - 1)),
                        deps=([LQ["wkv"]] + W["pskv"]) if hb == 0 else (),
                        inc=(hb == HB - 1))
                qkv_toks[t] = (None, None, None, mq, mkv)
                return mq

            def emit_qkv_evacs(g, t):
                par = t % 2
                _, _, _, mq, mkv = qkv_toks[t]
                evq = ctx.op("dve", lambda e, par=par: e.tensor_copy(q_t[par][:], ps_q[:]),
                             deps=[mq, W["qt"][par]], inc=True)
                evk = ctx.op("dve", lambda e, par=par: e.tensor_copy(k_t[par][:], ps_kv[:, 0:128]),
                             deps=[mkv, W["kt"][par]], inc=True)
                evv = ctx.op("dve", lambda e, t=t: e.tensor_copy(
                    v_tm[:, t * 128:(t + 1) * 128], ps_kv[:, 128:256]),
                    deps=[mkv] + W["vtm"], inc=True)
                W["psq"] = [evq]
                W["pskv"] = [evk, evv]
                qkv_toks[t] = (evq, evk, evv, mq, mkv)

            def rope(dst, srcap, t, dep_src, extra_dep):
                cs = cos_sb[:, t * 64:(t + 1) * 64]
                sn = sin_sb[:, t * 64:(t + 1) * 64]
                d1 = ctx.op("dve", lambda e: e.tensor_mul(rsc1[:], srcap[:, 0:64], cs),
                            deps=[dep_src, W["rsc"], consts[0]] + ([extra_dep] if extra_dep else []), inc=True)
                d2 = ctx.op("pool", lambda e: e.tensor_mul(rsc2[:], srcap[:, 64:128], sn),
                            deps=[dep_src, W["rsc"], consts[0]] + ([extra_dep] if extra_dep else []), inc=True)
                d3 = ctx.op("dve", lambda e: e.tensor_sub(dst[:, 0:64], rsc1[:], rsc2[:]),
                            deps=[d1, d2], inc=True)
                d4 = ctx.op("dve", lambda e: e.tensor_mul(rsc1[:], srcap[:, 64:128], cs),
                            deps=[d3], inc=True)
                d5 = ctx.op("pool", lambda e: e.tensor_mul(rsc2[:], srcap[:, 0:64], sn),
                            deps=[d3], inc=True)
                d6 = ctx.op("dve", lambda e: e.tensor_add(dst[:, 64:128], rsc1[:], rsc2[:]),
                            deps=[d4, d5], inc=True)
                W["rsc"] = d6
                return d6

            def emit_att(g, t):
                par = t % 2
                evq, evk, evv, _, _ = qkv_toks[t]
                texp = []
                for j in range(4):
                    tmx = ctx.op("dve", lambda e, j=j, par=par: e.reduce_max(
                        nmx[:, j:j + 1], q_t[par][:, j * 128:(j + 1) * 128],
                        axis=AX.X, negate=True), deps=[evq, W["nmx"][j]], inc=True)
                    tx = ctx.op("act", lambda e, j=j, par=par: e.activation(
                        qe[:, j * 128:(j + 1) * 128], q_t[par][:, j * 128:(j + 1) * 128],
                        AF.Exp, bias=nmx[:, j:j + 1], accum_out=se_c[:, j:j + 1]),
                        deps=[tmx, W["qe"], W["se"][j]], inc=True)
                    trc = ctx.op("dve", lambda e, j=j: e.reciprocal(
                        rr_c[:, j:j + 1], se_c[:, j:j + 1]), deps=[tx, W["rr"][j]], inc=True)
                    W["nmx"][j] = tx
                    W["se"][j] = trc
                    texp.append((tx, trc))
                tmxk = ctx.op("dve", lambda e, par=par: e.reduce_max(
                    nmx[:, 4:5], k_t[par][:], axis=AX.X, negate=True),
                    deps=[evk, W["nmx"][4]], inc=True)
                txk = ctx.op("act", lambda e, par=par: e.activation(
                    ke[:], k_t[par][:], AF.Exp, bias=nmx[:, 4:5], accum_out=se_c[:, 4:5]),
                    deps=[tmxk, W["ke"], W["se"][4]], inc=True)
                trck = ctx.op("dve", lambda e: e.reciprocal(rr_c[:, 4:5], se_c[:, 4:5]),
                              deps=[txk, W["rr"][4]], inc=True)
                W["nmx"][4] = txk
                W["se"][4] = trck
                g1 = ctx.op("act", lambda e, par=par: e.activation(gsc1[:], k_t[par][:], AF.Abs),
                            deps=[evk, W["gs"]], inc=True)
                g2 = ctx.op("act", lambda e: e.activation(gsc2[:], gsc1[:], AF.Exp, scale=-1.0),
                            deps=[g1], inc=True)
                g3 = ctx.op("act", lambda e: e.activation(gsc1[:], gsc2[:], AF.Ln, bias=1.0),
                            deps=[g2], inc=True)
                g4 = ctx.op("dve", lambda e, par=par: e.tensor_scalar_min(gsc2[:], k_t[par][:], 0.0),
                            deps=[g3, evk], inc=True)
                g5 = ctx.op("dve", lambda e: e.tensor_sub(gs[:], gsc2[:], gsc1[:]),
                            deps=[g3, g4], inc=True)
                mA = ctx.op("pe", lambda e: e.matmul(
                    ps_misc[:, 0:128], lt16_sb[:], gs[:], start=True, stop=True),
                    deps=[g5, consts[0]] + W["misc"], inc=True)
                mal = ctx.op("pe", lambda e: e.matmul(
                    ps_misc[:, 128:129], gs[:], o16_sb[:], start=True, stop=True),
                    deps=[consts[0]], inc=True)
                W["gs"] = mal
                eAh = ctx.op("act", lambda e: e.activation(
                    eA_sb[:], ps_misc[:, 0:128], AF.Exp, bias=nl2[:, 0:1]),
                    deps=[mA, mal, W["eA"]], inc=True)
                enA = ctx.op("act", lambda e: e.activation(
                    enA_sb[:], ps_misc[:, 0:128], AF.Exp, scale=-1.0),
                    deps=[mA, W["enA"]], inc=True)
                eat = ctx.op("act", lambda e: e.activation(ea_c[:], ps_misc[:, 128:129], AF.Exp),
                             deps=[mal, W["ea"]], inc=True)
                tqs = []
                for j in range(4):
                    tq = ctx.op("dve", lambda e, j=j: e.scalar_tensor_tensor(
                        qt_tm[:, j * 128:(j + 1) * 128], qe[:, j * 128:(j + 1) * 128],
                        rr_c[:, j:j + 1], eA_sb[:], ALU.mult, ALU.mult),
                        deps=[texp[j][0], texp[j][1], eAh, W["qttm"]], inc=True)
                    tqs.append(tq)
                tkt = ctx.op("dve", lambda e: e.scalar_tensor_tensor(
                    kt_tm[:], ke[:], rr_c[:, 4:5], enA_sb[:], ALU.mult, ALU.mult),
                    deps=[txk, trck, enA, W["kttm"]], inc=True)
                W["qe"] = tqs[-1]
                W["ke"] = tkt
                W["eA"] = tqs[-1]
                W["enA"] = tkt
                for j in range(4):
                    W["rr"][j] = tqs[j]
                W["rr"][4] = tkt
                trt = None
                for j in range(4):
                    trt = ctx.op("pe", lambda e, j=j: e.transpose(
                        ps_t1[:, j * 128:(j + 1) * 128], qt_tm[:, j * 128:(j + 1) * 128],
                        self.c_ident[:]),
                        deps=[tqs[j], self.tok_ident] + (W["t1"] if j == 0 else []),
                        inc=False)
                trt = ctx.op("pe", lambda e: e.transpose(
                    ps_t1[:, 512:640], kt_tm[:], self.c_ident[:]), deps=[tkt], inc=True)
                evfm = ctx.op("dve", lambda e: e.tensor_copy(qkfm[:], ps_t1[:, 0:640]),
                              deps=[trt, W["qkfm"]], inc=True)
                W["t1"] = [evfm]
                W["qttm"] = evfm
                W["kttm"] = evfm
                mP = ctx.op("pe", lambda e: e.matmul(
                    ps_P[:], qkfm[:, 512:640], qkfm[:, 0:512], start=True, stop=True),
                    deps=[evfm] + W["P"], inc=True)
                evP = ctx.op("dve", lambda e: e.tensor_mul(Pbf[:], ps_P[:], glam[:]),
                             deps=[mP, W["Pbf"], consts[0]], inc=True)
                W["P"] = [evP]
                mo1 = ctx.op("pe", lambda e: e.matmul(
                    ps_o[:], S_bf[:], qkfm[:, 0:512], start=True, stop=False),
                    deps=[evfm, W["Sbf_ready"]] + W["o"], inc=True)
                mo2 = ctx.op("pe", lambda e: e.matmul(
                    ps_o[:], v_tm[:, t * 128:(t + 1) * 128], Pbf[:], start=False, stop=False),
                    deps=[evP, evv], inc=True)
                W["Pbf"] = mo2
                W["qkfm"] = mo1
                mS = ctx.op("pe", lambda e: e.matmul(
                    ps_S[:], kt_tm[:], v_tm[:, t * 128:(t + 1) * 128], start=True, stop=True),
                    deps=[tkt, evv] + W["S"], inc=True)
                tS1 = ctx.op("dve", lambda e: e.tensor_add(S_t1[:], ps_S[:], S_f[:]),
                             deps=[mS, W["Sf_ready"]], inc=True)
                W["S"] = [tS1]
                tS2 = ctx.op("dve", lambda e: e.tensor_scalar(
                    S_f[:], S_t1[:], ea_c[:, 0:1], None, ALU.mult), deps=[tS1, eat], inc=True)
                W["ea"] = tS2
                W["Sf_ready"] = tS2
                tS3 = ctx.op("dve", lambda e: e.tensor_copy(S_bf[:], S_f[:]),
                             deps=[tS2, mo1], inc=True)
                W["Sbf_ready"] = tS3
                krd = rope(qrt, k_t[par], t, evk, W["qrt"])
                trkr = ctx.op("pe", lambda e: e.transpose(
                    ps_t2[:, 512:640], qrt[:], self.c_ident[:]),
                    deps=[krd, self.tok_ident] + W["t2"], inc=True)
                W["qrt"] = trkr
                evkr = ctx.op("dve", lambda e, t=t: e.tensor_copy(
                    krf[:, t * 128:(t + 1) * 128], ps_t2[:, 512:640]),
                    deps=[trkr] + W["krf"], inc=True)
                trq = None
                qrd_prev = evkr
                for j in range(4):
                    qrd = rope(qrt, q_t[par][:, j * 128:(j + 1) * 128], t, evq, W["qrt"])
                    trq = ctx.op("pe", lambda e, j=j: e.transpose(
                        ps_t2[:, j * 128:(j + 1) * 128], qrt[:], self.c_ident[:]),
                        deps=[qrd, evkr], inc=True)
                    W["qrt"] = trq
                evqr = ctx.op("dve", lambda e: e.tensor_copy(qkrfm[:, 0:512], ps_t2[:, 0:512]),
                              deps=[trq, W["qkrfm"]], inc=True)
                last_sc = None
                last_pT_ev = evqr
                mo_last = None
                last_msc = None
                for j in range(4):
                    if t == 0:
                        msc = ctx.op("pe", lambda e, j=j: e.matmul(
                            ps_misc[:, 384:512], qkrfm[:, j * 128:(j + 1) * 128],
                            krf[:, 0:128], start=True, stop=True),
                            deps=[evqr, evkr, eAh, enA, eat] + ([last_sc] if last_sc else []),
                            inc=True)
                        tsc0 = ctx.op("dve", lambda e: e.memset(scs[:, 0:128], NEGMASK),
                                      deps=[W["scs"]], inc=True)
                        tscs = ctx.op("dve", lambda e: e.scalar_tensor_tensor(
                            scs[:, 128:256], ps_misc[:, 384:512], INVSQ, swam[:, 128:256],
                            ALU.mult, ALU.add), deps=[msc, tsc0, consts[0]], inc=True)
                    else:
                        msc = ctx.op("pe", lambda e, j=j, t=t: e.matmul(
                            ps_misc[:, 256:512], qkrfm[:, j * 128:(j + 1) * 128],
                            krf[:, (t - 1) * 128:(t + 1) * 128], start=True, stop=True),
                            deps=[evqr, evkr, eAh, enA, eat] + ([last_sc] if last_sc else []),
                            inc=True)
                        tscs = ctx.op("dve", lambda e: e.scalar_tensor_tensor(
                            scs[:], ps_misc[:, 256:512], INVSQ, swam[:],
                            ALU.mult, ALU.add), deps=[msc, W["scs"], consts[0]], inc=True)
                    last_sc = tscs
                    last_msc = msc
                    W["scs"] = tscs
                    smx = ctx.op("dve", lambda e, j=j: e.reduce_max(
                        nmx[:, 5 + j:6 + j], scs[:], axis=AX.X, negate=True),
                        deps=[tscs, W["nmx"][5 + j]], inc=True)
                    sxp = ctx.op("act", lambda e, j=j: e.activation(
                        pe_b[:], scs[:], AF.Exp, bias=nmx[:, 5 + j:6 + j],
                        accum_out=se_c[:, 5 + j:6 + j]),
                        deps=[smx, W["peb"], W["se"][5 + j]], inc=True)
                    src = ctx.op("dve", lambda e, j=j: e.reciprocal(
                        rr_c[:, 5 + j:6 + j], se_c[:, 5 + j:6 + j]),
                        deps=[sxp, W["rr"][5 + j]], inc=True)
                    tps = ctx.op("dve", lambda e, j=j: e.tensor_scalar(
                        psc[:], pe_b[:], rr_c[:, 5 + j:6 + j], 0.5, ALU.mult, ALU.mult),
                        deps=[sxp, src, W["psc"]], inc=True)
                    W["peb"] = tps
                    W["nmx"][5 + j] = sxp
                    W["se"][5 + j] = src
                    W["rr"][5 + j] = tps
                    trp = None
                    if t > 0:
                        trp1 = ctx.op("pe", lambda e: e.transpose(
                            ps_t2[:, 640:768], psc[:, 0:128], self.c_ident[:]),
                            deps=[tps, last_pT_ev], inc=False)
                    trp = ctx.op("pe", lambda e: e.transpose(
                        ps_t2[:, 768:896], psc[:, 128:256], self.c_ident[:]),
                        deps=[tps, last_pT_ev], inc=True)
                    evpT = ctx.op("dve", lambda e: e.tensor_copy(
                        pT[:], ps_t2[:, 640:896]), deps=[trp, W["pT"]], inc=True)
                    last_pT_ev = evpT
                    W["psc"] = evpT
                    if t > 0:
                        mo_last = ctx.op("pe", lambda e, j=j, t=t: e.matmul(
                            ps_o[:, j * 128:(j + 1) * 128], v_tm[:, (t - 1) * 128:t * 128],
                            pT[:, 0:128], start=False, stop=False), deps=[evpT], inc=False)
                    mo_last = ctx.op("pe", lambda e, j=j, t=t: e.matmul(
                        ps_o[:, j * 128:(j + 1) * 128], v_tm[:, t * 128:(t + 1) * 128],
                        pT[:, 128:256], start=False, stop=(j == 3)), deps=[evpT],
                        inc=(j == 3))
                    W["pT"] = mo_last
                W["misc"] = [last_sc]
                W["t2"] = [last_pT_ev]
                W["qkrfm"] = last_msc
                evo = ctx.op("dve", lambda e: e.tensor_copy(obf[:], ps_o[:]),
                             deps=[mo_last, W["obf"]], inc=True)
                W["o"] = [evo]
                sto = ctx.dma("sp", lambda e, g=g, t=t: e.dma_start(
                    out=ospr[:, g, :, t * 128:(t + 1) * 128],
                    in_=obf[:].rearrange("p (j q) -> p j q", j=HPG)),
                    deps=[evo], ch=f"osp{par}")
                W["obf"] = sto
                W["qt"][par] = evo
                W["kt"][par] = evo
                o_states.append(sto)
                return sto

            war_wq = bbar
            war_wkv = bbar
            for g in range(NGRP):
                gc = slice(g * 512, (g + 1) * 512)
                LWQ = ctx.dma("pool", lambda e, gc=gc: e.dma_start(
                    out=wqbr, in_=wqr[:, :, gc]), deps=war_wq, ch="wq")
                kc = slice(g * 128, (g + 1) * 128)
                LWK = ctx.dma("pool", lambda e, kc=kc: e.dma_start(
                    out=wkvbr[:, :, 0:128], in_=wkr[:, :, kc]), deps=war_wkv, ch="wkv")
                LWV = ctx.dma("pool", lambda e, kc=kc: e.dma_start(
                    out=wkvbr[:, :, 128:256], in_=wvr[:, :, kc]), ch="wkv")
                LQ["wq"] = LWQ
                LQ["wkv"] = LWV
                ms1 = ctx.op("dve", lambda e: e.memset(S_f[:], 0.0),
                             deps=[W["Sf_ready"], W["S"][0]] if W["Sf_ready"] else bbar, inc=True)
                ms2 = ctx.op("dve", lambda e: e.memset(S_bf[:], 0.0),
                             deps=[ms1, W["Sbf_ready"]], inc=True)
                W["Sf_ready"] = ms1
                W["Sbf_ready"] = ms2
                lastq = None
                for t in range(NTILE + 1):
                    if t < NTILE:
                        lastq = emit_qkv(g, t)
                    if t >= 1:
                        emit_att(g, t - 1)
                    if t < NTILE:
                        emit_qkv_evacs(g, t)
                war_wq = [lastq]
                war_wkv = [qkv_toks[NTILE - 1][4]]
                W["vtm"] = [o_states[-1]]
                W["krf"] = [o_states[-1]]
            self.o_done = o_states[-4:]
        self.b_end = ctx.barrier()

    def phase_C(self, st_abc):
        nc, ctx = self.nc, self.ctx
        cbar = ctx.barrier()
        with ExitStack() as sC:
            self.s_cols = {"g": self.sb(sC, "C_scol_g", [128, IB], F32),
                           "u": self.sb(sC, "C_scol_u", [128, IB], F32)}
            self.s_col_toks = {}
            self.xa_g = self.sb(sC, "C_xag", [32, NT], BF16)
            self.xa_u = self.sb(sC, "C_xau", [32, NT], BF16)
            self.r1_2 = self.sb(sC, "C_r12", [2, NT], BF16)
            ones2 = self.sb(sC, "C_ones2", [2, 128], BF16)
            self.ones2 = ones2
            with ExitStack() as sP:
                srow = self.sb(sP, "C_srow", [1, 2048], F32)
                rrow = self.sb(sP, "C_rrow", [1, 256], F32)
                rrow2 = self.sb(sP, "C_rrow2", [1, 256], F32)
                bchunk = self.sb(sP, "C_bch", [32, 256], F32)
                bsc_ch = self.sb(sP, "C_bsc", [32, 256], BF16)
                onesrf = self.sb(sP, "C_orf", [1, 128], F32)
                gaua = self.sb(sP, "C_gaua", [128, HB * 64], BF16)
                ps_scol = self.ps(sP, "C_psscol", [128, IB], F32)
                ps_bc = self.ps(sP, "C_psbc", [32, 256], F32)
                ps_xa = self.ps(sP, "C_psxa", [64, 512], F32)
                t_orf = ctx.dma("sp", lambda e: e.dma_start(
                    out=onesrf[:], in_=self.cin["ones_row_f32"][:]), deps=cbar, ch="const")

                pro_toks = []
                mm_last = None
                prev_ev = None
                for mname, scin in (("g", self.gsc), ("u", self.usc)):
                    scol = self.s_cols[mname]
                    for ch in range(6):
                        c0 = ch * 2048
                        w = min(2048, IFF - c0)
                        ld = ctx.dma("sp", lambda e, c0=c0, w=w, scin=scin: e.dma_start(
                            out=srow[0:1, 0:w], in_=scin[0:1, c0:c0 + w]),
                            deps=([mm_last] if mm_last else cbar) + [prev_ev], ch="prosc")
                        tsc = ctx.op("dve", lambda e, w=w: e.tensor_scalar(
                            srow[0:1, 0:w], srow[0:1, 0:w], 0.005, None, ALU.mult),
                            deps=[ld], inc=True)
                        for j in range(w // 128):
                            ib = c0 // 128 + j
                            mm_last = ctx.op("pe", lambda e, j=j, ib=ib: e.matmul(
                                ps_scol[:, ib:ib + 1], srow[0:1, j * 128:(j + 1) * 128],
                                onesrf[0:1, 0:1], start=True, stop=True),
                                deps=[tsc, t_orf, prev_ev], inc=True)
                    ev = ctx.op("dve", lambda e, scol=scol: e.tensor_copy(scol[:], ps_scol[:]),
                                deps=[mm_last], inc=True)
                    prev_ev = ev
                    self.s_col_toks[mname] = ev
                    pro_toks.append(ev)

                for mname, bin_, bout, scin in (("g", self.gbm, self.gb_sc, self.gsc),
                                                ("u", self.ubm, self.ub_sc, self.usc)):
                    prev_st = None
                    for ch in range(43):
                        c0 = ch * 256
                        l1 = ctx.dma("sp", lambda e, c0=c0, scin=scin: e.dma_start(
                            out=rrow[0:1, :], in_=scin[0:1, c0:c0 + 256]),
                            deps=[prev_st] if prev_st else cbar, ch="prob1")
                        l2 = ctx.dma("sp", lambda e, c0=c0, bin_=bin_: e.dma_start(
                            out=bchunk[:], in_=bin_[:, c0:c0 + 256]),
                            deps=[prev_st] if prev_st else cbar, ch="prob2")
                        tre = ctx.op("dve", lambda e: e.reciprocal(rrow2[0:1, :], rrow[0:1, :]),
                                     deps=[l1], inc=True)
                        mmb = ctx.op("pe", lambda e: e.matmul(
                            ps_bc[:], onesrf[0:1, 0:32], rrow2[0:1, :],
                            start=True, stop=True), deps=[tre, t_orf], inc=True)
                        tst = ctx.op("dve", lambda e: e.scalar_tensor_tensor(
                            bsc_ch[:], bchunk[:], 4.0, ps_bc[:], ALU.mult, ALU.mult),
                            deps=[mmb, l2], inc=True)
                        prev_st = ctx.dma("sp", lambda e, c0=c0, bout=bout: e.dma_start(
                            out=bout[:, c0:c0 + 256], in_=bsc_ch[:]), deps=[tst], ch="probst")

                gar = self.ga[:].rearrange("(hb p) r -> p hb r", p=128)
                uar = self.ua[:].rearrange("(hb p) r -> p hb r", p=128)
                gr = gaua[:].rearrange("p (hb c) -> p hb c", hb=HB)
                lga = ctx.dma("pool", lambda e: e.dma_start(out=gr[:, :, 0:32], in_=gar), deps=cbar, ch="xa")
                lua = ctx.dma("pool", lambda e: e.dma_start(out=gr[:, :, 32:64], in_=uar), deps=cbar, ch="xa")
                prev_ev = None
                for tc in range(4):
                    cols = slice(tc * 512, (tc + 1) * 512)
                    mm = None
                    for hb in range(HB):
                        mm = ctx.op("pe", lambda e, hb=hb, cols=cols: e.matmul(
                            ps_xa[:], gaua[:, hb * 64:(hb + 1) * 64],
                            self.xn_fm[:, hb * NT + cols.start: hb * NT + cols.stop],
                            start=(hb == 0), stop=(hb == HB - 1)),
                            deps=([lga, lua, prev_ev] + self.xn_done) if hb == 0 else (),
                            inc=(hb == HB - 1))
                    e1 = ctx.op("dve", lambda e, cols=cols: e.tensor_copy(
                        self.xa_g[:, cols], ps_xa[0:32, :]), deps=[mm], inc=True)
                    e2 = ctx.op("dve", lambda e, cols=cols: e.tensor_copy(
                        self.xa_u[:, cols], ps_xa[32:64, :]), deps=[mm], inc=True)
                    prev_ev = e2
                    pro_toks += [e1, e2]

                d1 = ctx.dma("sp", lambda e: e.dma_start(
                    out=self.r1_2[0:1, :], in_=self.r1_sp[0:1, 0:NT]), deps=cbar, ch="r12")
                d2 = ctx.dma("sp", lambda e: e.dma_start(
                    out=self.r1_2[1:2, :], in_=self.r1_sp[0:1, NT:2 * NT]), deps=cbar, ch="r12")
                d3 = ctx.dma("sp", lambda e: e.dma_start(out=ones2[:], in_=self.cin["ones2"][:]), ch="const")
                pro_toks += [d1, d2, d3]

            pro_bar = ctx.barrier()

            gbuf = [self.sb(sC, f"C_G{p}", [128, HB * 128], BF16) for p in range(2)]
            ubuf = [self.sb(sC, f"C_U{p}", [128, HB * 128], BF16) for p in range(2)]
            gsl = [self.sb(sC, f"C_gsl{p}", [32, 128], BF16) for p in range(2)]
            usl = [self.sb(sC, f"C_usl{p}", [32, 128], BF16) for p in range(2)]
            tgb = [self.sb(sC, f"C_tg{p}", [128, 512], F32) for p in range(2)]
            tub = [self.sb(sC, f"C_tu{p}", [128, 512], F32) for p in range(2)]
            sgb = [self.sb(sC, f"C_sg{p}", [128, 512], F32) for p in range(2)]
            hbb = [self.sb(sC, f"C_h{p}", [128, 512], BF16) for p in range(2)]
            ps_g = [self.ps(sC, f"C_psg{j}", [128, 512], F32) for j in range(4)]
            ps_u = [self.ps(sC, f"C_psu{j}", [128, 512], F32) for j in range(4)]

            war_g = [pro_bar, pro_bar]
            war_u = [pro_bar, pro_bar]
            war_gsl = [pro_bar, pro_bar]
            war_usl = [pro_bar, pro_bar]
            war_psg = [[None]] * 4
            war_psu = [[None]] * 4
            war_tg = [None, None]
            war_tu = [None, None]
            war_sg = [None, None]
            war_h = [None, None]
            gqr = self.gq[:].rearrange("(hb p) i -> p hb i", p=128)
            uqr = self.uq[:].rearrange("(hb p) i -> p hb i", p=128)
            h_states = []
            for ib in range(IB):
                par = ib % 2
                i0 = ib * 128
                gdst = gbuf[par][:].rearrange("p (hb c) -> p hb c", hb=HB)
                LG = ctx.dma("pool", lambda e, i0=i0, gdst=gdst: e.dma_start(
                    out=gdst, in_=gqr[:, :, i0:i0 + 128]), deps=war_g[par], ch=f"g{par}")
                udst = ubuf[par][:].rearrange("p (hb c) -> p hb c", hb=HB)
                LU = ctx.dma("pool", lambda e, i0=i0, udst=udst: e.dma_start(
                    out=udst, in_=uqr[:, :, i0:i0 + 128]), deps=war_u[par], ch=f"u{par}")
                LSG = ctx.dma("pool", lambda e, i0=i0, par=par: e.dma_start(
                    out=gsl[par][:], in_=self.gb_sc[:, i0:i0 + 128]), deps=war_gsl[par], ch=f"sg{par}")
                LSU = ctx.dma("pool", lambda e, i0=i0, par=par: e.dma_start(
                    out=usl[par][:], in_=self.ub_sc[:, i0:i0 + 128]), deps=war_usl[par], ch=f"su{par}")
                g_last = None
                u_last = None
                for tc in range(4):
                    idx = (ib * 4 + tc) % 4
                    cols = slice(tc * 512, (tc + 1) * 512)
                    for hb in range(HB):
                        ctx.op("pe", lambda e, hb=hb, par=par, idx=idx, cols=cols: e.matmul(
                            ps_g[idx][:], gbuf[par][:, hb * 128:(hb + 1) * 128],
                            self.xn_fm[:, hb * NT + cols.start: hb * NT + cols.stop],
                            start=(hb == 0), stop=False),
                            deps=[LG] + war_psg[idx] if hb == 0 else ())
                    ctx.op("pe", lambda e, par=par, idx=idx, cols=cols: e.matmul(
                        ps_g[idx][:], gsl[par][:], self.xa_g[:, cols],
                        start=False, stop=False), deps=[LSG])
                    tok_g = ctx.op("pe", lambda e, idx=idx, cols=cols: e.matmul(
                        ps_g[idx][:], self.ones2[:], self.r1_2[:, cols],
                        start=False, stop=True), inc=True)
                    for hb in range(HB):
                        ctx.op("pe", lambda e, hb=hb, par=par, idx=idx, cols=cols: e.matmul(
                            ps_u[idx][:], ubuf[par][:, hb * 128:(hb + 1) * 128],
                            self.xn_fm[:, hb * NT + cols.start: hb * NT + cols.stop],
                            start=(hb == 0), stop=False),
                            deps=[LU] + war_psu[idx] if hb == 0 else ())
                    ctx.op("pe", lambda e, par=par, idx=idx, cols=cols: e.matmul(
                        ps_u[idx][:], usl[par][:], self.xa_u[:, cols],
                        start=False, stop=False), deps=[LSU])
                    tok_u = ctx.op("pe", lambda e, idx=idx, cols=cols: e.matmul(
                        ps_u[idx][:], self.ones2[:], self.r1_2[:, cols],
                        start=False, stop=True), inc=True)
                    if tc == 3:
                        g_last, u_last = tok_g, tok_u
                    p2 = (ib * 4 + tc) % 2
                    ttg = ctx.op("dve", lambda e, p2=p2, ib=ib, idx=idx: e.tensor_scalar(
                        tgb[p2][:], ps_g[idx][:], self.s_cols["g"][:, ib:ib + 1], None, ALU.mult),
                        deps=[tok_g, war_tg[p2], self.s_col_toks["g"]], inc=True)
                    tsg = ctx.op("act", lambda e, p2=p2: e.activation(
                        sgb[p2][:], tgb[p2][:], AF.Silu), deps=[ttg, war_sg[p2]], inc=True)
                    ttu = ctx.op("dve", lambda e, p2=p2, ib=ib, idx=idx: e.tensor_scalar(
                        tub[p2][:], ps_u[idx][:], self.s_cols["u"][:, ib:ib + 1], None, ALU.mult),
                        deps=[tok_u, war_tu[p2], self.s_col_toks["u"]], inc=True)
                    tth = ctx.op("dve", lambda e, p2=p2: e.tensor_mul(
                        hbb[p2][:], sgb[p2][:], tub[p2][:]),
                        deps=[tsg, ttu, war_h[p2]], inc=True)
                    sth = ctx.dma("sp", lambda e, p2=p2, i0=i0, cols=cols: e.dma_start(
                        out=self.h_sp[i0:i0 + 128, cols], in_=hbb[p2][:]), deps=[tth], ch=f"h{p2}")
                    if ib == 0 and tc == 0 and "cg" in self.debug:
                        cgbuf = self.sb(sC, "C_dbgcg", [128, 2 * 512], F32)
                        c1 = ctx.op("dve", lambda e, idx=idx: e.tensor_copy(
                            cgbuf[:, 0:512], ps_g[idx][:]), deps=[tok_g], inc=True)
                        c2 = ctx.op("dve", lambda e, p2=p2: e.tensor_copy(
                            cgbuf[:, 512:1024], tgb[p2][:]), deps=[ttg], inc=True)
                        ctx.dma("sp", lambda e: e.dma_start(
                            out=self.dbg["cg"][:], in_=cgbuf[:]), deps=[c1, c2], ch="dbg")
                        cg2 = self.sb(sC, "C_dbgcg2", [32, 1152], F32)
                        g1 = ctx.op("dve", lambda e, par=par: e.tensor_copy(
                            cg2[:, 0:128], gsl[par][:]), deps=[LSG], inc=True)
                        g2 = ctx.op("dve", lambda e: e.tensor_copy(
                            cg2[:, 128:640], self.xa_g[:, 0:512]), deps=[g1], inc=True)
                        g3 = ctx.op("dve", lambda e: e.tensor_copy(
                            cg2[:, 640:1152], self.r1_2[0:1, 0:512].partition_broadcast(1) if False else self.xa_u[:, 0:512]), deps=[g2], inc=True)
                        ctx.dma("sp", lambda e: e.dma_start(
                            out=self.dbg["cg2"][:], in_=cg2[:]), deps=[g3], ch="dbg")
                    war_psg[idx] = [ttg]
                    war_psu[idx] = [ttu]
                    war_tg[p2] = tsg
                    war_sg[p2] = tth
                    war_tu[p2] = tth
                    war_h[p2] = sth
                    h_states.append(sth)
                war_g[par] = (g_last,)
                war_u[par] = (u_last,)
                war_gsl[par] = (g_last,)
                war_usl[par] = (u_last,)
            self.h_done = h_states[-8:]
        self.c_end = ctx.barrier()

    def phase_D(self, st):
        nc, ctx = self.nc, self.ctx
        dbar = ctx.barrier()
        with ExitStack() as sD:
            hT = self.sb(sD, "D_hT", [128, IB * 512], BF16)
            ostrip = self.sb(sD, "D_o", [128, HB * 512], BF16)
            wdb = [self.sb(sD, f"D_wd{p}", [128, 43 * 256], BF16) for p in range(2)]
            wob = [self.sb(sD, f"D_wo{p}", [128, 16 * 256], BF16) for p in range(2)]
            dao = self.sb(sD, "D_dao", [128, IB * 33], BF16)
            dbsl = [self.sb(sD, f"D_db{p}", [32, 256], BF16) for p in range(2)]
            dbf32 = [self.sb(sD, f"D_dbf{p}", [32, 256], F32) for p in range(2)]
            dscrow = [self.sb(sD, f"D_dsc{p}", [1, 256], F32) for p in range(2)]
            sbdsl = [self.sb(sD, f"D_sbd{p}", [128, 256], F32) for p in range(2)]
            onesrf = self.sb(sD, "D_orf", [1, 128], F32)
            ones64 = self.sb(sD, "D_o64", [64, 1], F32)
            hdaf = self.sb(sD, "D_hdaf", [33, 512], F32)
            hdab = self.sb(sD, "D_hdab", [33, 512], BF16)
            hnegq = self.sb(sD, "D_hneg", [128, 4], F32)
            t1b = [self.sb(sD, f"D_t1{p}", [128, 256], F32) for p in range(2)]
            outb = [self.sb(sD, f"D_out{p}", [128, 256], F32) for p in range(2)]
            ps_ab = [self.ps(sD, f"D_psab{j}", [128, 512], F32) for j in range(4)]
            ps_a = [t[:, 0:256] for t in ps_ab]
            ps_b = [t[:, 256:512] for t in ps_ab]
            self._ps_hda = self.ps(sD, "D_pshda", [33, 512], F32)
            self._ps_hs = self.ps(sD, "D_pshs", [128, 4], F32)
            self._ps_sbd = self.ps(sD, "D_pssbd", [128, 256], F32)

            t_orf = ctx.dma("sp", lambda e: e.dma_start(
                out=onesrf[:], in_=self.cin["ones_row_f32"][:]), deps=dbar, ch="const")
            tm64 = ctx.op("dve", lambda e: e.memset(ones64[:], 1.0), deps=dbar, inc=True)
            daod = dao[:].rearrange("p (ib c) -> p ib c", c=33)
            tda = ctx.dma("pool", lambda e: e.dma_start(out=daod[:, :, 0:32], in_=self.da[:]), deps=dbar, ch="dao")
            tdo = ctx.op("dve", lambda e: e.memset(daod[:, :, 32:33], 1.0), deps=[tda], inc=True)

            dqr = self.dq[:].rearrange("(ib p) o -> p ib o", p=128)
            wor = self.wo[:].rearrange("(hb p) o -> p hb o", p=128)
            osr = self.o_sp[:].rearrange("(hb p) t -> p hb t", p=128)
            hspr = self.h_sp[:].rearrange("(ib p) t -> p ib t", p=128)
            hTd = hT[:].rearrange("p (ib t) -> p ib t", ib=IB)
            wdbr = [w[:].rearrange("p (ib o) -> p ib o", o=256) for w in wdb]
            wobr = [w[:].rearrange("p (hb o) -> p hb o", o=256) for w in wob]

            war_hT = list(dbar)
            war_ostrip = list(dbar)
            war_wd = [list(dbar), list(dbar)]
            war_wo = [list(dbar), list(dbar)]
            war_db = [list(dbar), list(dbar)]
            war_sbd = [[], []]
            war_hda = []
            war_hs = []
            war_t1 = [None] * 2
            war_out = [None] * 2
            war_psa = [[] for _ in range(4)]
            war_psb = [[] for _ in range(4)]
            for Q in range(4):
                q0 = Q * 512
                qcols = slice(q0, q0 + 512)
                hl = []
                for piece in range(4):
                    ibs = slice(piece * 22, min((piece + 1) * 22, IB))
                    hl.append(ctx.dma("sp", lambda e, ibs=ibs, qcols=qcols: e.dma_start(
                        out=hTd[:, ibs, :], in_=hspr[:, ibs, qcols]),
                        deps=war_hT + self.c_end, ch="hT"))
                ol = ctx.dma("sp", lambda e, qcols=qcols: e.dma_start(
                    out=ostrip[:].rearrange("p (hb t) -> p hb t", hb=HB),
                    in_=osr[:, :, qcols]), deps=war_ostrip + self.o_done, ch="ostrip")
                mm = None
                for ib in range(IB):
                    mm = ctx.op("pe", lambda e, ib=ib: e.matmul(
                        self._ps_hda[:], dao[:, ib * 33:(ib + 1) * 33],
                        hT[:, ib * 512:(ib + 1) * 512],
                        start=(ib == 0), stop=(ib == IB - 1)),
                        deps=([tdo] + hl + war_hda) if ib == 0 else (),
                        inc=(ib == IB - 1))
                ehf = ctx.op("dve", lambda e: e.tensor_copy(hdaf[:], self._ps_hda[:]),
                             deps=[mm] + war_hs, inc=True)
                war_hda = [ehf]
                ehb = ctx.op("dve", lambda e: e.tensor_copy(hdab[:], hdaf[:]), deps=[ehf], inc=True)
                hsm = None
                for tt in range(4):
                    hsm = ctx.op("pe", lambda e, tt=tt: e.matmul(
                        self._ps_hs[:, tt:tt + 1], hdaf[32:33, tt * 128:(tt + 1) * 128],
                        ones64[32:33, 0:1], start=True, stop=True),
                        deps=([ehf, tm64] + war_hs + war_sbd[0] + war_sbd[1]) if tt == 0 else (),
                        inc=(tt == 3))
                ehn = ctx.op("dve", lambda e: e.tensor_scalar(
                    hnegq[:], self._ps_hs[:], -7.5, None, ALU.mult), deps=[hsm], inc=True)
                war_hs = [ehn]

                for oc in range(16):
                    o0 = oc * 256
                    ocs = slice(o0, o0 + 256)
                    wdt = []
                    for piece in range(2):
                        pibs = slice(piece * 43, (piece + 1) * 43)
                        wdt.append(ctx.dma("pool", lambda e, pibs=pibs, ocs=ocs, piece=piece: e.dma_start(
                            out=wdbr[piece], in_=dqr[:, pibs, ocs]), deps=war_wd[piece], ch=f"wd{piece}"))
                    wot = []
                    for piece in range(2):
                        phbs = slice(piece * 16, (piece + 1) * 16)
                        wot.append(ctx.dma("pool", lambda e, phbs=phbs, ocs=ocs, piece=piece: e.dma_start(
                            out=wobr[piece], in_=wor[:, phbs, ocs]), deps=war_wo[piece], ch=f"wo{piece}"))
                    pdb = oc % 2
                    ldbf = ctx.dma("sp", lambda e, ocs=ocs, pdb=pdb: e.dma_start(
                        out=dbf32[pdb][:], in_=self.dbm[:, ocs]), deps=war_db[pdb], ch=f"db{pdb}")
                    tdb = ctx.op("dve", lambda e, pdb=pdb: e.tensor_scalar(
                        dbsl[pdb][:], dbf32[pdb][:], 0.02, None, ALU.mult), deps=[ldbf], inc=True)
                    ldsc = ctx.dma("sp", lambda e, ocs=ocs, pdb=pdb: e.dma_start(
                        out=dscrow[pdb][0:1, :], in_=self.dsc[0:1, ocs]), deps=war_sbd[pdb], ch=f"dsc{pdb}")
                    tds = ctx.op("dve", lambda e, pdb=pdb: e.tensor_scalar(
                        dscrow[pdb][0:1, :], dscrow[pdb][0:1, :], 0.005, None, ALU.mult),
                        deps=[ldsc], inc=True)
                    msb = ctx.op("pe", lambda e, pdb=pdb: e.matmul(
                        self._ps_sbd[:], onesrf[:], dscrow[pdb][0:1, :], start=True, stop=True),
                        deps=[tds, t_orf] + war_sbd[pdb] + war_hs, inc=True)
                    esb = ctx.op("dve", lambda e, pdb=pdb: e.tensor_copy(
                        sbdsl[pdb][:], self._ps_sbd[:]), deps=[msb], inc=True)
                    war_sbd[pdb] = [esb]

                    tA_mid = None
                    tA = None
                    for ib in range(IB):
                        piece = 0 if ib < 43 else 1
                        off = ib - 43 * piece
                        for tt in range(4):
                            dep = ()
                            if ib == 0 and tt == 0:
                                dep = [wdt[0]] + [t for w in war_psa for t in w]
                            elif ib == 43 and tt == 0:
                                dep = [wdt[1]]
                            tok = ctx.op("pe", lambda e, ib=ib, tt=tt, piece=piece, off=off: e.matmul(
                                ps_a[tt], hT[:, ib * 512 + tt * 128: ib * 512 + (tt + 1) * 128],
                                wdb[piece][:, off * 256:(off + 1) * 256],
                                start=(ib == 0), stop=(ib == IB - 1)),
                                deps=dep, inc=(tt == 3 and ib in (42, IB - 1)))
                            if ib == 42 and tt == 3:
                                tA_mid = tok
                            if ib == IB - 1 and tt == 3:
                                tA = tok
                    war_wd = [[tA_mid], [tA]]
                    tB_mid = None
                    tB = None
                    for hb in range(HB):
                        piece = 0 if hb < 16 else 1
                        off = hb - 16 * piece
                        for tt in range(4):
                            dep = ()
                            if hb == 0 and tt == 0:
                                dep = [wot[0], ol] + [t for w in war_psb for t in w]
                            elif hb == 16 and tt == 0:
                                dep = [wot[1]]
                            tok = ctx.op("pe", lambda e, hb=hb, tt=tt, piece=piece, off=off: e.matmul(
                                ps_b[tt], ostrip[:, hb * 512 + tt * 128: hb * 512 + (tt + 1) * 128],
                                wob[piece][:, off * 256:(off + 1) * 256],
                                start=(hb == 0), stop=False),
                                deps=dep, inc=(tt == 3 and hb == 15))
                            if hb == 15 and tt == 3:
                                tB_mid = tok
                    tBs = []
                    for tt in range(4):
                        tB = ctx.op("pe", lambda e, tt=tt, pdb=pdb: e.matmul(
                            ps_b[tt], hdab[0:32, tt * 128:(tt + 1) * 128],
                            dbsl[pdb][:], start=False, stop=True),
                            deps=[tdb, ehb] if tt == 0 else (), inc=(tt == 3))
                        tBs.append(tB)
                    war_wo = [[tB_mid], [tB]]
                    war_db[pdb] = [tB]
                    for tt in range(4):
                        b2 = tt % 2
                        et1 = ctx.op("dve", lambda e, tt=tt, b2=b2, pdb=pdb: e.scalar_tensor_tensor(
                            t1b[b2][:], ps_a[tt], hnegq[:, tt:tt + 1], sbdsl[pdb][:],
                            ALU.add, ALU.mult),
                            deps=[tA, tBs[-1], ehn, esb, war_t1[b2]], inc=True)
                        eo = ctx.op("dve", lambda e, tt=tt, b2=b2: e.tensor_add(
                            outb[b2][:], t1b[b2][:], ps_b[tt]),
                            deps=[tBs[-1], et1, war_out[b2]], inc=True)
                        stt = ctx.dma("pool", lambda e, b2=b2, tt=tt, q0=q0, ocs=ocs: e.dma_start(
                            out=self.out[q0 + tt * 128: q0 + (tt + 1) * 128, ocs],
                            in_=outb[b2][:], accum_op=ALU.add),
                            deps=[eo] + self.out_prefill, ch=f"oacc{tt % 2}")
                        war_psa[tt] = [eo]
                        war_psb[tt] = [eo]
                        war_t1[b2] = eo
                        war_out[b2] = stt
                war_hT = [tA]
                war_ostrip = [tB_mid, tB]


def _prep_inputs(inputs):
    x = inputs["L0_attn_hidden_states"]
    common = {
        "wq": np.ascontiguousarray(inputs["L0_attn_Wq"]),
        "wk": np.ascontiguousarray(inputs["L0_attn_Wk"]),
        "wv": np.ascontiguousarray(inputs["L0_attn_Wv"]),
        "wo": np.ascontiguousarray(inputs["L0_attn_Wo"]),
        "gq": np.ascontiguousarray(inputs["L0_ffn_gate_q"].T),
        "uq": np.ascontiguousarray(inputs["L0_ffn_up_q"].T),
        "dq": np.ascontiguousarray(inputs["L0_ffn_down_q"].T),
        "gsc": np.ascontiguousarray(inputs["L0_ffn_gate_scale"].reshape(1, IFF)),
        "usc": np.ascontiguousarray(inputs["L0_ffn_up_scale"].reshape(1, IFF)),
        "dsc": np.ascontiguousarray(inputs["L0_ffn_down_scale"].reshape(1, DM)),
        "ga": np.ascontiguousarray(inputs["L0_ffn_gate_A"]),
        "gbm": np.ascontiguousarray(inputs["L0_ffn_gate_B"]),
        "ua": np.ascontiguousarray(inputs["L0_ffn_up_A"]),
        "ubm": np.ascontiguousarray(inputs["L0_ffn_up_B"]),
        "da": np.ascontiguousarray(inputs["L0_ffn_down_A"].reshape(IB, 128, 32).transpose(1, 0, 2)),
        "dbm": np.ascontiguousarray(inputs["L0_ffn_down_B"]),
    }
    in_maps = []
    for c in range(8):
        m = dict(common)
        m["x"] = np.ascontiguousarray(x[c])
        in_maps.append(m)
    return in_maps


def kernel(**inputs):
    in_maps = _prep_inputs(inputs)
    nc = Builder(phases=("A", "B", "C", "D")).build()
    res = run_bass_kernel_spmd(nc, in_maps, core_ids=list(range(8)))
    outs = [np.asarray(res.results[c]["out"]) for c in range(8)]
    return np.stack(outs, axis=0)


# version 02
# speedup vs baseline: 1.0382x; 1.0382x over previous
import numpy as np
import ml_dtypes
import concourse.bass as bass
import concourse.mybir as mybir
from concourse.bass_utils import run_bass_kernel_spmd
from contextlib import ExitStack

F32 = mybir.dt.float32
BF16 = mybir.dt.bfloat16
I32 = mybir.dt.int32
AF = mybir.ActivationFunctionType
ALU = mybir.AluOpType
AX = mybir.AxisListType
BF16NP = ml_dtypes.bfloat16

NT = 2048
DM = 4096
HB = 32
NTILE = 16
NGRP = 8
HPG = 4
DH = 128
IFF = 11008
IB = 86
EPS = 1e-5
INVSQ = float(1.0 / np.sqrt(128.0))
NEGMASK = -30000.0
LN2 = float(np.log(2.0))

ENGINES = ("pe", "act", "dve", "pool", "sp")
DMAQ = {"sp": "sp_dma", "act": "act_dma", "pool": "pool_dma"}


class Ctx:

    def __init__(self, nc):
        self.nc = nc
        self.streams = {e: [] for e in ENGINES}
        self.count = {k: 0 for k in ENGINES}
        self.vc = {e: {} for e in ENGINES}
        self.sems = {}

    def _waits_for(self, e, deps):
        waits = []
        for t in deps:
            if t is None:
                continue
            key, val, snap = t
            if self.vc[e].get(key, 0) < val:
                waits.append((key, val))
                for k2, v2 in snap.items():
                    if self.vc[e].get(k2, 0) < v2:
                        self.vc[e][k2] = v2
        return waits

    STRICT = ("dve", "act", "pool")

    def op(self, e, fn, deps=(), inc=False):
        waits = self._waits_for(e, deps)
        token = None
        semkey = None
        if inc or e in self.STRICT:
            semkey = e
            self.count[e] += 1
            snap = dict(self.vc[e])
            snap[e] = self.count[e]
            token = (e, self.count[e], snap)
            if e not in self.STRICT:
                self.vc[e][e] = self.count[e]
        self.streams[e].append((waits, fn, semkey, 1))
        return token

    def dma(self, e, fn, deps=(), ch="main"):
        waits = self._waits_for(e, deps)
        q = f"{e}:{ch}"
        self.count[q] = self.count.get(q, 0) + 16
        snap = dict(self.vc[e])
        snap[q] = self.count[q]
        token = (q, self.count[q], snap)
        self.vc[e][q] = self.count[q]
        self.streams[e].append((waits, fn, q, 16))
        return token

    def barrier(self):
        toks = []
        for k, v in self.count.items():
            if v > 0:
                toks.append((k, v, {k: v}))
        return toks

    def emit(self, block):
        starters = {"pe": block.tensor, "act": block.scalar, "dve": block.vector,
                    "pool": block.gpsimd, "sp": block.sync}
        final_counts = dict(self.count)
        for k, v in final_counts.items():
            if v > 0 and k not in self.sems:
                self.sems[k] = self.nc.alloc_semaphore(f"sem_{k.replace(':', '_')}")

        def body_for(name):
            def body(eng):
                for waits, fn, semkey, incval in self.streams[name]:
                    for (k, v) in waits:
                        eng.wait_ge(self.sems[k], v)
                    ins = fn(eng)
                    if semkey is not None:
                        ins.then_inc(self.sems[semkey], incval)
                if name == "sp":
                    for k, v in final_counts.items():
                        if v > 0:
                            eng.wait_ge(self.sems[k], v)
            return body

        for name in ENGINES:
            if self.streams[name] or name == "sp":
                starters[name](body_for(name))


def make_const_arrays():
    c = {}
    c["ident"] = np.eye(128, dtype=BF16NP)
    c["lt16"] = np.triu(np.full((128, 128), 1.0 / 16.0)).astype(BF16NP)
    c["ones16_col"] = np.full((128, 1), 1.0 / 16.0, dtype=BF16NP)
    c["ones_col_bf"] = np.ones((128, 1), dtype=BF16NP)
    c["ones_row_bf"] = np.ones((1, 128), dtype=BF16NP)
    c["ones_row_f32"] = np.ones((1, 128), dtype=np.float32)
    c["ones2"] = np.ones((2, 128), dtype=BF16NP)
    c["negln2_col"] = np.full((128, 1), -np.log(2.0), dtype=np.float32)
    m = (np.arange(128)[None, :] >= np.arange(128)[:, None]).astype(np.float32)
    c["gla_mask"] = np.tile(m, (1, 4)).astype(BF16NP)
    r = np.arange(128)[:, None]
    cc = np.arange(256)[None, :]
    valid = (cc >= r + 64) & (cc <= r + 128)
    mask = np.where(valid, 0.0, NEGMASK).astype(np.float32)
    c["swa_mask"] = mask
    mask0 = mask.copy()
    mask0[:, :128] = NEGMASK
    c["swa_mask0"] = mask0
    inv = 10000.0 ** (-np.arange(64) / 64.0)
    ang = np.arange(NT)[:, None] * inv[None, :]
    c["cos_t"] = np.cos(ang).reshape(NTILE, 128, 64).transpose(1, 0, 2).reshape(128, NTILE * 64).astype(BF16NP)
    c["sin_t"] = np.sin(ang).reshape(NTILE, 128, 64).transpose(1, 0, 2).reshape(128, NTILE * 64).astype(BF16NP)
    return c


class Builder:
    def __init__(self, debug=(), phases=("A",)):
        self.debug = set(debug)
        self.phases = set(phases)
        nc = bass.Bass("TRN2", target_bir_lowering=False)
        self.nc = nc
        self.x = nc.dram_tensor("x", [NT, DM], F32, kind="ExternalInput")
        self.wq = nc.dram_tensor("wq", [DM, DM], F32, kind="ExternalInput")
        self.wk = nc.dram_tensor("wk", [DM, 1024], F32, kind="ExternalInput")
        self.wv = nc.dram_tensor("wv", [DM, 1024], F32, kind="ExternalInput")
        self.wo = nc.dram_tensor("wo", [DM, DM], F32, kind="ExternalInput")
        self.gq = nc.dram_tensor("gq", [DM, IFF], I32, kind="ExternalInput")
        self.uq = nc.dram_tensor("uq", [DM, IFF], I32, kind="ExternalInput")
        self.dq = nc.dram_tensor("dq", [IFF, DM], I32, kind="ExternalInput")
        self.gsc = nc.dram_tensor("gsc", [1, IFF], F32, kind="ExternalInput")
        self.usc = nc.dram_tensor("usc", [1, IFF], F32, kind="ExternalInput")
        self.dsc = nc.dram_tensor("dsc", [1, DM], F32, kind="ExternalInput")
        self.ga = nc.dram_tensor("ga", [DM, 32], F32, kind="ExternalInput")
        self.gbm = nc.dram_tensor("gbm", [32, IFF], F32, kind="ExternalInput")
        self.ua = nc.dram_tensor("ua", [DM, 32], F32, kind="ExternalInput")
        self.ubm = nc.dram_tensor("ubm", [32, IFF], F32, kind="ExternalInput")
        self.da = nc.dram_tensor("da", [128, IB, 32], F32, kind="ExternalInput")
        self.dbm = nc.dram_tensor("dbm", [32, DM], F32, kind="ExternalInput")
        self.out = nc.dram_tensor("out", [NT, DM], F32, kind="ExternalOutput")
        self.h_sp = nc.dram_tensor("h_sp", [IFF, NT], BF16, kind="Internal")
        if "ozero" in self.debug:
            self.o_sp = nc.dram_tensor("o_sp", [DM, NT], BF16, kind="ExternalInput")
        else:
            self.o_sp = nc.dram_tensor("o_sp", [DM, NT], BF16, kind="Internal")
        self.o_done = []
        self.c_end = []
        self.gb_sc = nc.dram_tensor("gb_sc", [32, IFF], BF16, kind="Internal")
        self.ub_sc = nc.dram_tensor("ub_sc", [32, IFF], BF16, kind="Internal")
        self.r1_sp = nc.dram_tensor("r1_sp", [1, 2 * NT], BF16, kind="Internal")
        self.dq_bf = nc.dram_tensor("dq_bf", [IFF, DM], BF16, kind="Internal")
        self.wo_bf = nc.dram_tensor("wo_bf", [DM, DM], BF16, kind="Internal")
        self.cnp = make_const_arrays()
        self.cin = {k: nc.inline_tensor(v, name="c_" + k) for k, v in self.cnp.items()}
        self.dbg = {}
        if "xn" in self.debug:
            self.dbg["xn"] = nc.dram_tensor("dbg_xn", [DM, NT], BF16, kind="ExternalOutput")
        if "hcopy" in self.debug:
            self.dbg["hcopy"] = nc.dram_tensor("dbg_h", [IFF, NT], BF16, kind="ExternalOutput")
        if "astats" in self.debug:
            self.dbg["astats"] = nc.dram_tensor("dbg_astats", [128, 34], F32, kind="ExternalOutput")
        if "ocopy" in self.debug:
            self.dbg["ocopy"] = nc.dram_tensor("dbg_o", [DM, NT], BF16, kind="ExternalOutput")
        if "cg" in self.debug:
            self.dbg["cg"] = nc.dram_tensor("dbg_cg", [128, 2 * 512], F32, kind="ExternalOutput")
            self.dbg["cg2"] = nc.dram_tensor("dbg_cg2", [32, 1152], F32, kind="ExternalOutput")

    def sb(self, stack, name, shape, dtype):
        return stack.enter_context(self.nc.sbuf_tensor(name, shape, dtype))

    def ps(self, stack, name, shape, dtype=F32):
        return stack.enter_context(self.nc.psum_tensor(name, shape, dtype))

    def build(self):
        nc = self.nc
        with ExitStack() as st:
            ctx = Ctx(nc)
            self.ctx = ctx

            self.c_ident = self.sb(st, "sc_ident", [128, 128], BF16)

            t_id = ctx.dma("sp", lambda e: e.dma_start(out=self.c_ident[:], in_=self.cin["ident"][:]), ch="const")
            self.tok_ident = t_id

            self.out_prefill = []
            for i in range(4):
                rows = slice(i * 512, (i + 1) * 512)
                self.out_prefill.append(ctx.dma(
                    "act", lambda e, rows=rows: e.dma_start(out=self.out[rows, :], in_=self.x[rows, :]),
                    ch="prefill"))

            with ExitStack() as st_abc:
                self.xn_fm = self.sb(st_abc, "xn_fm", [128, HB * NT], BF16)
                self.phase_A(st_abc)
                if "B" in self.phases:
                    self.phase_B(st_abc)
                if "C" in self.phases:
                    self.phase_C(st_abc)
                if "xn" in self.debug:
                    xr = self.dbg["xn"][:].rearrange("(hb p) t -> p hb t", p=128)
                    sr = self.xn_fm[:].rearrange("p (hb t) -> p hb t", hb=HB)
                    ctx.dma("sp", lambda e: e.dma_start(out=xr, in_=sr),
                            deps=self.xn_done, ch="dbg")
            if "ocopy" in self.debug and "B" in self.phases:
                for i in range(2):
                    rows = slice(i * 2048, (i + 1) * 2048)
                    ctx.dma("act", lambda e, rows=rows: e.dma_start(
                        out=self.dbg["ocopy"][rows, :], in_=self.o_sp[rows, :]),
                        deps=self.b_end, ch="dbg")
            if "hcopy" in self.debug and "C" in self.phases:
                for i in range(4):
                    rows = slice(i * 2752, (i + 1) * 2752)
                    ctx.dma("act", lambda e, rows=rows: e.dma_start(
                        out=self.dbg["hcopy"][rows, :], in_=self.h_sp[rows, :]),
                        deps=self.c_end, ch="dbg")
            if "D" in self.phases:
                self.phase_D(st)

            with nc.Block() as block:
                ctx.emit(block)
        return nc

    def phase_A(self, st_abc):
        nc, ctx = self.nc, self.ctx
        with ExitStack() as sA:
            xbuf = [self.sb(sA, f"A_x{p}", [128, DM], F32) for p in range(2)]
            xn_tm = [self.sb(sA, f"A_xn{p}", [128, DM], BF16) for p in range(2)]
            sqscr = xn_tm
            s1c = self.sb(sA, "A_s1", [128, NTILE], F32)
            s2c = self.sb(sA, "A_s2", [128, NTILE], F32)
            negmc = self.sb(sA, "A_negm", [128, NTILE], F32)
            msqc = self.sb(sA, "A_msq", [128, NTILE], F32)
            varc = self.sb(sA, "A_var", [128, NTILE], F32)
            sdc = self.sb(sA, "A_sd", [128, NTILE], F32)
            rstdc = self.sb(sA, "A_rstd", [128, NTILE], F32)
            ps_tr = [self.ps(sA, f"A_pstr{b}", [128, 1024], BF16) for b in range(2)]
            ps_sum = self.ps(sA, "A_pssum", [1, 512], F32)

            war_x = [(), ()]
            war_xn = [None, None]
            war_bank = [None, None]
            evac_toks = []
            for t in range(NTILE):
                p = t % 2
                tok = slice(t * 128, (t + 1) * 128)
                L = ctx.dma("sp", lambda e, p=p, tok=tok: e.dma_start(
                    out=xbuf[p][:], in_=self.x[tok, :]), deps=war_x[p], ch=f"x{p}")
                tsq = ctx.op("act", lambda e, p=p, t=t: e.activation(
                    sqscr[p][:], xbuf[p][:], AF.Square, accum_out=s2c[:, t:t + 1]),
                    deps=[L, war_xn[p]], inc=True)
                ts1 = ctx.op("dve", lambda e, p=p, t=t: e.reduce_sum(
                    s1c[:, t:t + 1], xbuf[p][:], axis=AX.X), deps=[L], inc=True)
                negm = negmc[:, t:t + 1]
                msq = msqc[:, t:t + 1]
                varep = varc[:, t:t + 1]
                sd = sdc[:, t:t + 1]
                rstd = rstdc[:, t:t + 1]
                tnm = ctx.op("dve", lambda e, t=t, negm=negm: e.tensor_scalar(
                    negm, s1c[:, t:t + 1], -1.0 / DM, None, ALU.mult), deps=[ts1])
                tmq = ctx.op("dve", lambda e, negm=negm, msq=msq: e.tensor_mul(msq, negm, negm),
                             deps=[tnm])
                tva = ctx.op("dve", lambda e, t=t, varep=varep: e.tensor_scalar(
                    varep, s2c[:, t:t + 1], 1.0 / DM, EPS, ALU.mult, ALU.add),
                    deps=[tsq])
                tv = ctx.op("dve", lambda e, varep=varep, msq=msq: e.tensor_sub(varep, varep, msq),
                            deps=[tva, tmq], inc=True)
                tsd = ctx.op("act", lambda e, sd=sd, varep=varep: e.activation(sd, varep, AF.Sqrt),
                             deps=[tv], inc=True)
                tr = ctx.op("dve", lambda e, rstd=rstd, sd=sd: e.reciprocal(rstd, sd), deps=[tsd], inc=True)
                txn = ctx.op("dve", lambda e, p=p, negm=negm, rstd=rstd: e.tensor_scalar(
                    xn_tm[p][:], xbuf[p][:], negm, rstd, ALU.add, ALU.mult),
                    deps=[war_xn[p], L, tsq, tr, tnm], inc=True)
                war_x[p] = (txn, tsq)
                last_tr = None
                for g8 in range(4):
                    b = g8 % 2
                    grp_tok = None
                    for j in range(8):
                        d = g8 * 8 + j
                        grp_tok = ctx.op("pe", lambda e, p=p, d=d, b=b, j=j: e.transpose(
                            ps_tr[b][:, j * 128:(j + 1) * 128],
                            xn_tm[p][:, d * 128:(d + 1) * 128], self.c_ident[:]),
                            deps=[txn, self.tok_ident, war_bank[b]] if j == 0 else (),
                            inc=(j == 7))
                    dst = self.xn_fm[:].rearrange("p (hb t) -> p hb t", hb=HB)[:, g8 * 8:(g8 + 1) * 8, t * 128:(t + 1) * 128]
                    ev = ctx.op("dve", lambda e, b=b, dst=dst: e.tensor_copy(dst, ps_tr[b][:].rearrange("p (j t) -> p j t", j=8)),
                                deps=[grp_tok], inc=True)
                    war_bank[b] = ev
                    evac_toks.append(ev)
                    last_tr = grp_tok
                war_xn[p] = last_tr

            wob2 = [self.sb(sA, f"A_wob{p}", [128, 2048], BF16) for p in range(2)]
            wor2 = self.wo[:].rearrange("(b p) o -> p b o", p=128)
            wobf2 = self.wo_bf[:].rearrange("(b p) o -> p b o", p=128)
            war_wob = [None, None]
            for i in range(64):
                pw = i % 2
                blk, half = i // 2, (i % 2) * 0
                blk = i // 2
                oc0 = (i % 2) * 2048
                lw = ctx.dma("pool", lambda e, blk=blk, oc0=oc0, pw=pw: e.dma_start(
                    out=wob2[pw][:], in_=wor2[:, blk, oc0:oc0 + 2048]),
                    deps=[war_wob[pw]], ch=f"woc{pw}")
                war_wob[pw] = ctx.dma("act", lambda e, blk=blk, oc0=oc0, pw=pw: e.dma_start(
                    out=wobf2[:, blk, oc0:oc0 + 2048], in_=wob2[pw][:]),
                    deps=[lw], ch=f"wocs{pw}")
            self.wo_bf_done = list(war_wob)

            ones_c = self.sb(sA, "A_ones", [128, 1], BF16)
            xnsum_row = self.sb(sA, "A_xnsum", [1, NT], F32)
            tone = ctx.dma("sp", lambda e: e.dma_start(out=ones_c[:], in_=self.cin["ones_col_bf"][:]), ch="const")
            r1f = self.sb(sA, "A_r1f", [1, NT], F32)
            r1bf = self.sb(sA, "A_r1bf", [1, NT], BF16)
            ev = None
            sum_evs = []
            for ch in range(4):
                cols = slice(ch * 512, (ch + 1) * 512)
                mt = None
                for hb in range(HB):
                    mt = ctx.op("pe", lambda e, hb=hb, cols=cols: e.matmul(
                        ps_sum[:], ones_c[:],
                        self.xn_fm[:, hb * NT + cols.start: hb * NT + cols.stop],
                        start=(hb == 0), stop=(hb == HB - 1)),
                        deps=(evac_toks + [tone, ev]) if hb == 0 else (),
                        inc=(hb == HB - 1))
                ev = ctx.op("dve", lambda e, cols=cols: e.tensor_copy(
                    xnsum_row[0:1, cols], ps_sum[:]), deps=[mt], inc=True)
                sum_evs.append(ev)
            t1 = ctx.op("dve", lambda e: e.tensor_scalar(
                r1f[:], xnsum_row[:], -7.5, None, ALU.mult), deps=sum_evs, inc=True)
            t2 = ctx.op("dve", lambda e: e.tensor_copy(r1bf[:], r1f[:]), deps=[t1], inc=True)
            s1t = ctx.dma("sp", lambda e: e.dma_start(out=self.r1_sp[0:1, 0:NT], in_=r1bf[:]),
                          deps=[t2], ch="r1w")
            t3 = ctx.op("dve", lambda e: e.tensor_sub(r1f[:], r1f[:], r1bf[:]), deps=[t2])
            t4 = ctx.op("dve", lambda e: e.tensor_copy(r1bf[:], r1f[:]), deps=[t3, s1t], inc=True)
            s2t = ctx.dma("sp", lambda e: e.dma_start(out=self.r1_sp[0:1, NT:2 * NT], in_=r1bf[:]),
                          deps=[t4], ch="r1w")
            self.xn_done = evac_toks + [t4, s2t]
            if "astats" in self.debug:
                stat = self.sb(sA, "A_stat", [128, 34], F32)
                bar = ctx.barrier()
                d1 = ctx.op("dve", lambda e: e.tensor_copy(stat[:, 0:16], s1c[:]), deps=bar, inc=True)
                ctx.op("dve", lambda e: e.tensor_copy(stat[:, 16:32], s2c[:]))
                ctx.op("dve", lambda e: e.tensor_copy(stat[:, 32:33], negmc[:, NTILE - 1:NTILE]))
                d2 = ctx.op("dve", lambda e: e.tensor_copy(stat[:, 33:34], rstdc[:, NTILE - 1:NTILE]), inc=True)
                ctx.dma("sp", lambda e: e.dma_start(out=self.dbg["astats"][:], in_=stat[:]), deps=[d1, d2], ch="dbg")
            self.a_end = ctx.barrier()

    def phase_B(self, st_abc):
        nc, ctx = self.nc, self.ctx
        bbar = ctx.barrier()
        with ExitStack() as sB:
            wqb = self.sb(sB, "B_wq", [128, HB * 512], BF16)
            wkvb = self.sb(sB, "B_wkv", [128, HB * 256], BF16)
            v_tm = self.sb(sB, "B_vtm", [128, NTILE * 128], BF16)
            krf = self.sb(sB, "B_krf", [128, NTILE * 128], BF16)
            S_f = self.sb(sB, "B_S", [128, 128], F32)
            S_t1 = self.sb(sB, "B_St1", [128, 128], F32)
            S_bf = self.sb(sB, "B_Sbf", [128, 128], BF16)
            ea_c = self.sb(sB, "B_ea", [128, 1], F32)
            cos_sb = self.sb(sB, "B_cos", [128, NTILE * 64], BF16)
            sin_sb = self.sb(sB, "B_sin", [128, NTILE * 64], BF16)
            lt16_sb = self.sb(sB, "B_lt16", [128, 128], BF16)
            o16_sb = self.sb(sB, "B_o16", [128, 1], BF16)
            glam = self.sb(sB, "B_glam", [128, 512], BF16)
            swam = self.sb(sB, "B_swam", [128, 256], F32)
            nl2 = self.sb(sB, "B_nl2", [128, 1], F32)
            q_t = [self.sb(sB, f"B_q{p}", [128, 512], BF16) for p in range(2)]
            k_t = [self.sb(sB, f"B_k{p}", [128, 128], BF16) for p in range(2)]
            qe = self.sb(sB, "B_qe", [128, 512], BF16)
            ke = self.sb(sB, "B_ke", [128, 128], BF16)
            gsc1 = self.sb(sB, "B_gsc1", [128, 128], BF16)
            gsc2 = self.sb(sB, "B_gsc2", [128, 128], BF16)
            gs = self.sb(sB, "B_gs", [128, 128], BF16)
            eA_sb = self.sb(sB, "B_eA", [128, 128], BF16)
            enA_sb = self.sb(sB, "B_enA", [128, 128], BF16)
            qt_tm = self.sb(sB, "B_qttm", [128, 512], BF16)
            kt_tm = self.sb(sB, "B_kttm", [128, 128], BF16)
            qkfm = self.sb(sB, "B_qkfm", [128, 640], BF16)
            Pbf = self.sb(sB, "B_Pbf", [128, 512], BF16)
            obf = self.sb(sB, "B_obf", [128, 512], BF16)
            qrt = self.sb(sB, "B_qrt", [128, 128], BF16)
            rsc1 = self.sb(sB, "B_rsc1", [128, 64], BF16)
            rsc2 = self.sb(sB, "B_rsc2", [128, 64], BF16)
            qkrfm = self.sb(sB, "B_qkrfm", [128, 640], BF16)
            scs = self.sb(sB, "B_scs", [128, 256], F32)
            pe_b = self.sb(sB, "B_peb", [128, 256], BF16)
            psc = self.sb(sB, "B_psc", [128, 256], BF16)
            pT = self.sb(sB, "B_pT", [128, 256], BF16)
            nmx = self.sb(sB, "B_nmx", [128, 16], F32)
            se_c = self.sb(sB, "B_se", [128, 16], F32)
            rr_c = self.sb(sB, "B_rr", [128, 16], F32)
            ps_q = self.ps(sB, "B_psq", [128, 512], F32)
            ps_kv = self.ps(sB, "B_pskv", [128, 256], F32)
            ps_misc = self.ps(sB, "B_psmisc", [128, 512], F32)
            ps_P = self.ps(sB, "B_psP", [128, 512], F32)
            ps_o = self.ps(sB, "B_pso", [128, 512], F32)
            ps_S = self.ps(sB, "B_psS", [128, 128], F32)
            ps_t1 = self.ps(sB, "B_pst1", [128, 1024], BF16)
            ps_t2 = self.ps(sB, "B_pst2", [128, 1024], BF16)

            cl = bbar
            for nm, dst in (("cos_t", cos_sb), ("sin_t", sin_sb), ("lt16", lt16_sb),
                            ("ones16_col", o16_sb), ("gla_mask", glam), ("swa_mask", swam),
                            ("negln2_col", nl2)):
                tokc = ctx.dma("sp", lambda e, nm=nm, dst=dst: e.dma_start(
                    out=dst[:], in_=self.cin[nm][:]), deps=cl, ch="const")
            consts = [tokc, self.tok_ident]

            wqr = self.wq[:].rearrange("(hb p) o -> p hb o", p=128)
            wkr = self.wk[:].rearrange("(hb p) o -> p hb o", p=128)
            wvr = self.wv[:].rearrange("(hb p) o -> p hb o", p=128)
            wqbr = wqb[:].rearrange("p (hb o) -> p hb o", o=512)
            wkvbr = wkvb[:].rearrange("p (hb o) -> p hb o", o=256)
            ospr = self.o_sp[:].rearrange("(g j p) t -> p g j t", p=128, j=HPG)

            W = {"wq": bbar, "wkv": bbar, "psq": [None], "pskv": [None],
                 "qt": [None, None], "kt": [None, None], "vtm": [None],
                 "qe": None, "ke": None, "gs": None, "misc": [None], "P": [None],
                 "o": [None], "S": [None], "t1": [None], "t2": [None],
                 "qttm": None, "kttm": None, "qkfm": None, "Pbf": None, "obf": None,
                 "qrt": None, "rsc": None, "qkrfm": None, "scs": None, "peb": None,
                 "psc": None, "pT": None, "Sbf_ready": None, "Sf_ready": None,
                 "nmx": [None] * 16, "se": [None] * 16, "rr": [None] * 16,
                 "krf": [None], "ea": None, "eA": None, "enA": None,
                 "osp": [None, None]}
            qkv_toks = {}
            o_states = []
            LQ = {"wq": None, "wkv": None}

            def emit_qkv(g, t):
                par = t % 2
                mq = None
                for hb in range(HB):
                    mq = ctx.op("pe", lambda e, hb=hb, t=t: e.matmul(
                        ps_q[:], self.xn_fm[:, hb * NT + t * 128: hb * NT + (t + 1) * 128],
                        wqb[:, hb * 512:(hb + 1) * 512],
                        start=(hb == 0), stop=(hb == HB - 1)),
                        deps=([LQ["wq"]] + W["psq"]) if hb == 0 else (),
                        inc=(hb == HB - 1))
                mkv = None
                for hb in range(HB):
                    mkv = ctx.op("pe", lambda e, hb=hb, t=t: e.matmul(
                        ps_kv[:], self.xn_fm[:, hb * NT + t * 128: hb * NT + (t + 1) * 128],
                        wkvb[:, hb * 256:(hb + 1) * 256],
                        start=(hb == 0), stop=(hb == HB - 1)),
                        deps=([LQ["wkv"]] + W["pskv"]) if hb == 0 else (),
                        inc=(hb == HB - 1))
                qkv_toks[t] = (None, None, None, mq, mkv)
                return mq

            def emit_qkv_evacs(g, t):
                par = t % 2
                _, _, _, mq, mkv = qkv_toks[t]
                evq = ctx.op("dve", lambda e, par=par: e.tensor_copy(q_t[par][:], ps_q[:]),
                             deps=[mq, W["qt"][par]], inc=True)
                evk = ctx.op("dve", lambda e, par=par: e.tensor_copy(k_t[par][:], ps_kv[:, 0:128]),
                             deps=[mkv, W["kt"][par]], inc=True)
                evv = ctx.op("dve", lambda e, t=t: e.tensor_copy(
                    v_tm[:, t * 128:(t + 1) * 128], ps_kv[:, 128:256]),
                    deps=[mkv] + W["vtm"], inc=True)
                W["psq"] = [evq]
                W["pskv"] = [evk, evv]
                qkv_toks[t] = (evq, evk, evv, mq, mkv)

            def rope(dst, srcap, t, dep_src, extra_dep):
                cs = cos_sb[:, t * 64:(t + 1) * 64]
                sn = sin_sb[:, t * 64:(t + 1) * 64]
                d1 = ctx.op("dve", lambda e: e.tensor_mul(rsc1[:], srcap[:, 0:64], cs),
                            deps=[dep_src, W["rsc"], consts[0]] + ([extra_dep] if extra_dep else []), inc=True)
                d2 = ctx.op("pool", lambda e: e.tensor_mul(rsc2[:], srcap[:, 64:128], sn),
                            deps=[dep_src, W["rsc"], consts[0]] + ([extra_dep] if extra_dep else []), inc=True)
                d3 = ctx.op("dve", lambda e: e.tensor_sub(dst[:, 0:64], rsc1[:], rsc2[:]),
                            deps=[d1, d2], inc=True)
                d4 = ctx.op("dve", lambda e: e.tensor_mul(rsc1[:], srcap[:, 64:128], cs),
                            deps=[d3], inc=True)
                d5 = ctx.op("pool", lambda e: e.tensor_mul(rsc2[:], srcap[:, 0:64], sn),
                            deps=[d3], inc=True)
                d6 = ctx.op("dve", lambda e: e.tensor_add(dst[:, 64:128], rsc1[:], rsc2[:]),
                            deps=[d4, d5], inc=True)
                W["rsc"] = d6
                return d6

            def emit_att(g, t):
                par = t % 2
                evq, evk, evv, _, _ = qkv_toks[t]
                texp = []
                for j in range(4):
                    tmx = ctx.op("dve", lambda e, j=j, par=par: e.reduce_max(
                        nmx[:, j:j + 1], q_t[par][:, j * 128:(j + 1) * 128],
                        axis=AX.X, negate=True), deps=[evq, W["nmx"][j]], inc=True)
                    tx = ctx.op("act", lambda e, j=j, par=par: e.activation(
                        qe[:, j * 128:(j + 1) * 128], q_t[par][:, j * 128:(j + 1) * 128],
                        AF.Exp, bias=nmx[:, j:j + 1], accum_out=se_c[:, j:j + 1]),
                        deps=[tmx, W["qe"], W["se"][j]], inc=True)
                    trc = ctx.op("dve", lambda e, j=j: e.reciprocal(
                        rr_c[:, j:j + 1], se_c[:, j:j + 1]), deps=[tx, W["rr"][j]], inc=True)
                    W["nmx"][j] = tx
                    W["se"][j] = trc
                    texp.append((tx, trc))
                tmxk = ctx.op("dve", lambda e, par=par: e.reduce_max(
                    nmx[:, 4:5], k_t[par][:], axis=AX.X, negate=True),
                    deps=[evk, W["nmx"][4]], inc=True)
                txk = ctx.op("act", lambda e, par=par: e.activation(
                    ke[:], k_t[par][:], AF.Exp, bias=nmx[:, 4:5], accum_out=se_c[:, 4:5]),
                    deps=[tmxk, W["ke"], W["se"][4]], inc=True)
                trck = ctx.op("dve", lambda e: e.reciprocal(rr_c[:, 4:5], se_c[:, 4:5]),
                              deps=[txk, W["rr"][4]], inc=True)
                W["nmx"][4] = txk
                W["se"][4] = trck
                g1 = ctx.op("act", lambda e, par=par: e.activation(gsc1[:], k_t[par][:], AF.Abs),
                            deps=[evk, W["gs"]], inc=True)
                g2 = ctx.op("act", lambda e: e.activation(gsc2[:], gsc1[:], AF.Exp, scale=-1.0),
                            deps=[g1], inc=True)
                g3 = ctx.op("act", lambda e: e.activation(gsc1[:], gsc2[:], AF.Ln, bias=1.0),
                            deps=[g2], inc=True)
                g4 = ctx.op("dve", lambda e, par=par: e.tensor_scalar_min(gsc2[:], k_t[par][:], 0.0),
                            deps=[g3, evk], inc=True)
                g5 = ctx.op("dve", lambda e: e.tensor_sub(gs[:], gsc2[:], gsc1[:]),
                            deps=[g3, g4], inc=True)
                mA = ctx.op("pe", lambda e: e.matmul(
                    ps_misc[:, 0:128], lt16_sb[:], gs[:], start=True, stop=True),
                    deps=[g5, consts[0]] + W["misc"], inc=True)
                mal = ctx.op("pe", lambda e: e.matmul(
                    ps_misc[:, 128:129], gs[:], o16_sb[:], start=True, stop=True),
                    deps=[consts[0]], inc=True)
                W["gs"] = mal
                eAh = ctx.op("act", lambda e: e.activation(
                    eA_sb[:], ps_misc[:, 0:128], AF.Exp, bias=nl2[:, 0:1]),
                    deps=[mA, mal, W["eA"]], inc=True)
                enA = ctx.op("act", lambda e: e.activation(
                    enA_sb[:], ps_misc[:, 0:128], AF.Exp, scale=-1.0),
                    deps=[mA, W["enA"]], inc=True)
                eat = ctx.op("act", lambda e: e.activation(ea_c[:], ps_misc[:, 128:129], AF.Exp),
                             deps=[mal, W["ea"]], inc=True)
                tqs = []
                for j in range(4):
                    tq = ctx.op("dve", lambda e, j=j: e.scalar_tensor_tensor(
                        qt_tm[:, j * 128:(j + 1) * 128], qe[:, j * 128:(j + 1) * 128],
                        rr_c[:, j:j + 1], eA_sb[:], ALU.mult, ALU.mult),
                        deps=[texp[j][0], texp[j][1], eAh, W["qttm"]], inc=True)
                    tqs.append(tq)
                tkt = ctx.op("dve", lambda e: e.scalar_tensor_tensor(
                    kt_tm[:], ke[:], rr_c[:, 4:5], enA_sb[:], ALU.mult, ALU.mult),
                    deps=[txk, trck, enA, W["kttm"]], inc=True)
                W["qe"] = tqs[-1]
                W["ke"] = tkt
                W["eA"] = tqs[-1]
                W["enA"] = tkt
                for j in range(4):
                    W["rr"][j] = tqs[j]
                W["rr"][4] = tkt
                trt = None
                for j in range(4):
                    trt = ctx.op("pe", lambda e, j=j: e.transpose(
                        ps_t1[:, j * 128:(j + 1) * 128], qt_tm[:, j * 128:(j + 1) * 128],
                        self.c_ident[:]),
                        deps=[tqs[j], self.tok_ident] + (W["t1"] if j == 0 else []),
                        inc=False)
                trt = ctx.op("pe", lambda e: e.transpose(
                    ps_t1[:, 512:640], kt_tm[:], self.c_ident[:]), deps=[tkt], inc=True)
                evfm = ctx.op("dve", lambda e: e.tensor_copy(qkfm[:], ps_t1[:, 0:640]),
                              deps=[trt, W["qkfm"]], inc=True)
                W["t1"] = [evfm]
                W["qttm"] = evfm
                W["kttm"] = evfm
                mP = ctx.op("pe", lambda e: e.matmul(
                    ps_P[:], qkfm[:, 512:640], qkfm[:, 0:512], start=True, stop=True),
                    deps=[evfm] + W["P"], inc=True)
                evP = ctx.op("dve", lambda e: e.tensor_mul(Pbf[:], ps_P[:], glam[:]),
                             deps=[mP, W["Pbf"], consts[0]], inc=True)
                W["P"] = [evP]
                mo1 = ctx.op("pe", lambda e: e.matmul(
                    ps_o[:], S_bf[:], qkfm[:, 0:512], start=True, stop=False),
                    deps=[evfm, W["Sbf_ready"]] + W["o"], inc=True)
                mo2 = ctx.op("pe", lambda e: e.matmul(
                    ps_o[:], v_tm[:, t * 128:(t + 1) * 128], Pbf[:], start=False, stop=False),
                    deps=[evP, evv], inc=True)
                W["Pbf"] = mo2
                W["qkfm"] = mo1
                mS = ctx.op("pe", lambda e: e.matmul(
                    ps_S[:], kt_tm[:], v_tm[:, t * 128:(t + 1) * 128], start=True, stop=True),
                    deps=[tkt, evv] + W["S"], inc=True)
                tS1 = ctx.op("dve", lambda e: e.tensor_add(S_t1[:], ps_S[:], S_f[:]),
                             deps=[mS, W["Sf_ready"]], inc=True)
                W["S"] = [tS1]
                tS2 = ctx.op("dve", lambda e: e.tensor_scalar(
                    S_f[:], S_t1[:], ea_c[:, 0:1], None, ALU.mult), deps=[tS1, eat], inc=True)
                W["ea"] = tS2
                W["Sf_ready"] = tS2
                tS3 = ctx.op("dve", lambda e: e.tensor_copy(S_bf[:], S_f[:]),
                             deps=[tS2, mo1], inc=True)
                W["Sbf_ready"] = tS3
                krd = rope(qrt, k_t[par], t, evk, W["qrt"])
                trkr = ctx.op("pe", lambda e: e.transpose(
                    ps_t2[:, 512:640], qrt[:], self.c_ident[:]),
                    deps=[krd, self.tok_ident] + W["t2"], inc=True)
                W["qrt"] = trkr
                evkr = ctx.op("dve", lambda e, t=t: e.tensor_copy(
                    krf[:, t * 128:(t + 1) * 128], ps_t2[:, 512:640]),
                    deps=[trkr] + W["krf"], inc=True)
                trq = None
                qrd_prev = evkr
                for j in range(4):
                    qrd = rope(qrt, q_t[par][:, j * 128:(j + 1) * 128], t, evq, W["qrt"])
                    trq = ctx.op("pe", lambda e, j=j: e.transpose(
                        ps_t2[:, j * 128:(j + 1) * 128], qrt[:], self.c_ident[:]),
                        deps=[qrd, evkr], inc=True)
                    W["qrt"] = trq
                evqr = ctx.op("dve", lambda e: e.tensor_copy(qkrfm[:, 0:512], ps_t2[:, 0:512]),
                              deps=[trq, W["qkrfm"]], inc=True)
                last_sc = None
                last_pT_ev = evqr
                mo_last = None
                last_msc = None
                for j in range(4):
                    if t == 0:
                        msc = ctx.op("pe", lambda e, j=j: e.matmul(
                            ps_misc[:, 384:512], qkrfm[:, j * 128:(j + 1) * 128],
                            krf[:, 0:128], start=True, stop=True),
                            deps=[evqr, evkr, eAh, enA, eat] + ([last_sc] if last_sc else []),
                            inc=True)
                        tsc0 = ctx.op("dve", lambda e: e.memset(scs[:, 0:128], NEGMASK),
                                      deps=[W["scs"]], inc=True)
                        tscs = ctx.op("dve", lambda e: e.scalar_tensor_tensor(
                            scs[:, 128:256], ps_misc[:, 384:512], INVSQ, swam[:, 128:256],
                            ALU.mult, ALU.add), deps=[msc, tsc0, consts[0]], inc=True)
                    else:
                        msc = ctx.op("pe", lambda e, j=j, t=t: e.matmul(
                            ps_misc[:, 256:512], qkrfm[:, j * 128:(j + 1) * 128],
                            krf[:, (t - 1) * 128:(t + 1) * 128], start=True, stop=True),
                            deps=[evqr, evkr, eAh, enA, eat] + ([last_sc] if last_sc else []),
                            inc=True)
                        tscs = ctx.op("dve", lambda e: e.scalar_tensor_tensor(
                            scs[:], ps_misc[:, 256:512], INVSQ, swam[:],
                            ALU.mult, ALU.add), deps=[msc, W["scs"], consts[0]], inc=True)
                    last_sc = tscs
                    last_msc = msc
                    W["scs"] = tscs
                    smx = ctx.op("dve", lambda e, j=j: e.reduce_max(
                        nmx[:, 5 + j:6 + j], scs[:], axis=AX.X, negate=True),
                        deps=[tscs, W["nmx"][5 + j]], inc=True)
                    sxp = ctx.op("act", lambda e, j=j: e.activation(
                        pe_b[:], scs[:], AF.Exp, bias=nmx[:, 5 + j:6 + j],
                        accum_out=se_c[:, 5 + j:6 + j]),
                        deps=[smx, W["peb"], W["se"][5 + j]], inc=True)
                    src = ctx.op("dve", lambda e, j=j: e.reciprocal(
                        rr_c[:, 5 + j:6 + j], se_c[:, 5 + j:6 + j]),
                        deps=[sxp, W["rr"][5 + j]], inc=True)
                    tps = ctx.op("dve", lambda e, j=j: e.tensor_scalar(
                        psc[:], pe_b[:], rr_c[:, 5 + j:6 + j], 0.5, ALU.mult, ALU.mult),
                        deps=[sxp, src, W["psc"]], inc=True)
                    W["peb"] = tps
                    W["nmx"][5 + j] = sxp
                    W["se"][5 + j] = src
                    W["rr"][5 + j] = tps
                    trp = None
                    if t > 0:
                        trp1 = ctx.op("pe", lambda e: e.transpose(
                            ps_t2[:, 640:768], psc[:, 0:128], self.c_ident[:]),
                            deps=[tps, last_pT_ev], inc=False)
                    trp = ctx.op("pe", lambda e: e.transpose(
                        ps_t2[:, 768:896], psc[:, 128:256], self.c_ident[:]),
                        deps=[tps, last_pT_ev], inc=True)
                    evpT = ctx.op("dve", lambda e: e.tensor_copy(
                        pT[:], ps_t2[:, 640:896]), deps=[trp, W["pT"]], inc=True)
                    last_pT_ev = evpT
                    W["psc"] = evpT
                    if t > 0:
                        mo_last = ctx.op("pe", lambda e, j=j, t=t: e.matmul(
                            ps_o[:, j * 128:(j + 1) * 128], v_tm[:, (t - 1) * 128:t * 128],
                            pT[:, 0:128], start=False, stop=False), deps=[evpT], inc=False)
                    mo_last = ctx.op("pe", lambda e, j=j, t=t: e.matmul(
                        ps_o[:, j * 128:(j + 1) * 128], v_tm[:, t * 128:(t + 1) * 128],
                        pT[:, 128:256], start=False, stop=(j == 3)), deps=[evpT],
                        inc=(j == 3))
                    W["pT"] = mo_last
                W["misc"] = [last_sc]
                W["t2"] = [last_pT_ev]
                W["qkrfm"] = last_msc
                evo = ctx.op("dve", lambda e: e.tensor_copy(obf[:], ps_o[:]),
                             deps=[mo_last, W["obf"]], inc=True)
                W["o"] = [evo]
                sto = ctx.dma("sp", lambda e, g=g, t=t: e.dma_start(
                    out=ospr[:, g, :, t * 128:(t + 1) * 128],
                    in_=obf[:].rearrange("p (j q) -> p j q", j=HPG)),
                    deps=[evo], ch=f"osp{par}")
                W["obf"] = sto
                W["qt"][par] = evo
                W["kt"][par] = evo
                o_states.append(sto)
                return sto

            war_wq = bbar
            war_wkv = bbar
            for g in range(NGRP):
                gc = slice(g * 512, (g + 1) * 512)
                LWQ = ctx.dma("pool", lambda e, gc=gc: e.dma_start(
                    out=wqbr, in_=wqr[:, :, gc]), deps=war_wq, ch="wq")
                kc = slice(g * 128, (g + 1) * 128)
                LWK = ctx.dma("pool", lambda e, kc=kc: e.dma_start(
                    out=wkvbr[:, :, 0:128], in_=wkr[:, :, kc]), deps=war_wkv, ch="wkv")
                LWV = ctx.dma("pool", lambda e, kc=kc: e.dma_start(
                    out=wkvbr[:, :, 128:256], in_=wvr[:, :, kc]), ch="wkv")
                LQ["wq"] = LWQ
                LQ["wkv"] = LWV
                ms1 = ctx.op("dve", lambda e: e.memset(S_f[:], 0.0),
                             deps=[W["Sf_ready"], W["S"][0]] if W["Sf_ready"] else bbar, inc=True)
                ms2 = ctx.op("dve", lambda e: e.memset(S_bf[:], 0.0),
                             deps=[ms1, W["Sbf_ready"]], inc=True)
                W["Sf_ready"] = ms1
                W["Sbf_ready"] = ms2
                lastq = None
                for t in range(NTILE + 1):
                    if t < NTILE:
                        lastq = emit_qkv(g, t)
                    if t >= 1:
                        emit_att(g, t - 1)
                    if t < NTILE:
                        emit_qkv_evacs(g, t)
                war_wq = [lastq]
                war_wkv = [qkv_toks[NTILE - 1][4]]
                W["vtm"] = [o_states[-1]]
                W["krf"] = [o_states[-1]]
            self.o_done = o_states[-4:]
        self.b_end = ctx.barrier()

    def phase_C(self, st_abc):
        nc, ctx = self.nc, self.ctx
        cbar = ctx.barrier()
        with ExitStack() as sC:
            self.s_cols = {"g": self.sb(sC, "C_scol_g", [128, IB], F32),
                           "u": self.sb(sC, "C_scol_u", [128, IB], F32)}
            self.s_col_toks = {}
            self.xa_g = self.sb(sC, "C_xag", [32, NT], BF16)
            self.xa_u = self.sb(sC, "C_xau", [32, NT], BF16)
            self.r1_2 = self.sb(sC, "C_r12", [2, NT], BF16)
            ones2 = self.sb(sC, "C_ones2", [2, 128], BF16)
            self.ones2 = ones2
            with ExitStack() as sP:
                srow = self.sb(sP, "C_srow", [1, 2048], F32)
                rrow = self.sb(sP, "C_rrow", [1, 256], F32)
                rrow2 = self.sb(sP, "C_rrow2", [1, 256], F32)
                bchunk = self.sb(sP, "C_bch", [32, 256], F32)
                bsc_ch = self.sb(sP, "C_bsc", [32, 256], BF16)
                onesrf = self.sb(sP, "C_orf", [1, 128], F32)
                gaua = self.sb(sP, "C_gaua", [128, HB * 64], BF16)
                ps_scol = self.ps(sP, "C_psscol", [128, IB], F32)
                ps_bc = self.ps(sP, "C_psbc", [32, 256], F32)
                ps_xa = self.ps(sP, "C_psxa", [64, 512], F32)
                t_orf = ctx.dma("sp", lambda e: e.dma_start(
                    out=onesrf[:], in_=self.cin["ones_row_f32"][:]), deps=cbar, ch="const")

                pro_toks = []
                mm_last = None
                prev_ev = None
                for mname, scin in (("g", self.gsc), ("u", self.usc)):
                    scol = self.s_cols[mname]
                    for ch in range(6):
                        c0 = ch * 2048
                        w = min(2048, IFF - c0)
                        ld = ctx.dma("sp", lambda e, c0=c0, w=w, scin=scin: e.dma_start(
                            out=srow[0:1, 0:w], in_=scin[0:1, c0:c0 + w]),
                            deps=([mm_last] if mm_last else cbar) + [prev_ev], ch="prosc")
                        tsc = ctx.op("dve", lambda e, w=w: e.tensor_scalar(
                            srow[0:1, 0:w], srow[0:1, 0:w], 0.005, None, ALU.mult),
                            deps=[ld], inc=True)
                        for j in range(w // 128):
                            ib = c0 // 128 + j
                            mm_last = ctx.op("pe", lambda e, j=j, ib=ib: e.matmul(
                                ps_scol[:, ib:ib + 1], srow[0:1, j * 128:(j + 1) * 128],
                                onesrf[0:1, 0:1], start=True, stop=True),
                                deps=[tsc, t_orf, prev_ev], inc=True)
                    ev = ctx.op("dve", lambda e, scol=scol: e.tensor_copy(scol[:], ps_scol[:]),
                                deps=[mm_last], inc=True)
                    prev_ev = ev
                    self.s_col_toks[mname] = ev
                    pro_toks.append(ev)

                for mname, bin_, bout, scin in (("g", self.gbm, self.gb_sc, self.gsc),
                                                ("u", self.ubm, self.ub_sc, self.usc)):
                    prev_st = None
                    for ch in range(43):
                        c0 = ch * 256
                        l1 = ctx.dma("sp", lambda e, c0=c0, scin=scin: e.dma_start(
                            out=rrow[0:1, :], in_=scin[0:1, c0:c0 + 256]),
                            deps=[prev_st] if prev_st else cbar, ch="prob1")
                        l2 = ctx.dma("sp", lambda e, c0=c0, bin_=bin_: e.dma_start(
                            out=bchunk[:], in_=bin_[:, c0:c0 + 256]),
                            deps=[prev_st] if prev_st else cbar, ch="prob2")
                        tre = ctx.op("dve", lambda e: e.reciprocal(rrow2[0:1, :], rrow[0:1, :]),
                                     deps=[l1], inc=True)
                        mmb = ctx.op("pe", lambda e: e.matmul(
                            ps_bc[:], onesrf[0:1, 0:32], rrow2[0:1, :],
                            start=True, stop=True), deps=[tre, t_orf], inc=True)
                        tst = ctx.op("dve", lambda e: e.scalar_tensor_tensor(
                            bsc_ch[:], bchunk[:], 4.0, ps_bc[:], ALU.mult, ALU.mult),
                            deps=[mmb, l2], inc=True)
                        prev_st = ctx.dma("sp", lambda e, c0=c0, bout=bout: e.dma_start(
                            out=bout[:, c0:c0 + 256], in_=bsc_ch[:]), deps=[tst], ch="probst")

                gar = self.ga[:].rearrange("(hb p) r -> p hb r", p=128)
                uar = self.ua[:].rearrange("(hb p) r -> p hb r", p=128)
                gr = gaua[:].rearrange("p (hb c) -> p hb c", hb=HB)
                lga = ctx.dma("pool", lambda e: e.dma_start(out=gr[:, :, 0:32], in_=gar), deps=cbar, ch="xa")
                lua = ctx.dma("pool", lambda e: e.dma_start(out=gr[:, :, 32:64], in_=uar), deps=cbar, ch="xa")
                prev_ev = None
                for tc in range(4):
                    cols = slice(tc * 512, (tc + 1) * 512)
                    mm = None
                    for hb in range(HB):
                        mm = ctx.op("pe", lambda e, hb=hb, cols=cols: e.matmul(
                            ps_xa[:], gaua[:, hb * 64:(hb + 1) * 64],
                            self.xn_fm[:, hb * NT + cols.start: hb * NT + cols.stop],
                            start=(hb == 0), stop=(hb == HB - 1)),
                            deps=([lga, lua, prev_ev] + self.xn_done) if hb == 0 else (),
                            inc=(hb == HB - 1))
                    e1 = ctx.op("dve", lambda e, cols=cols: e.tensor_copy(
                        self.xa_g[:, cols], ps_xa[0:32, :]), deps=[mm], inc=True)
                    e2 = ctx.op("dve", lambda e, cols=cols: e.tensor_copy(
                        self.xa_u[:, cols], ps_xa[32:64, :]), deps=[mm], inc=True)
                    prev_ev = e2
                    pro_toks += [e1, e2]

                d1 = ctx.dma("sp", lambda e: e.dma_start(
                    out=self.r1_2[0:1, :], in_=self.r1_sp[0:1, 0:NT]), deps=cbar, ch="r12")
                d2 = ctx.dma("sp", lambda e: e.dma_start(
                    out=self.r1_2[1:2, :], in_=self.r1_sp[0:1, NT:2 * NT]), deps=cbar, ch="r12")
                d3 = ctx.dma("sp", lambda e: e.dma_start(out=ones2[:], in_=self.cin["ones2"][:]), ch="const")
                pro_toks += [d1, d2, d3]

            pro_bar = ctx.barrier()

            gbuf = [self.sb(sC, f"C_G{p}", [128, HB * 128], BF16) for p in range(2)]
            ubuf = [self.sb(sC, f"C_U{p}", [128, HB * 128], BF16) for p in range(2)]
            gsl = [self.sb(sC, f"C_gsl{p}", [32, 128], BF16) for p in range(2)]
            usl = [self.sb(sC, f"C_usl{p}", [32, 128], BF16) for p in range(2)]
            tgb = [self.sb(sC, f"C_tg{p}", [128, 512], F32) for p in range(2)]
            tub = [self.sb(sC, f"C_tu{p}", [128, 512], F32) for p in range(2)]
            sgb = [self.sb(sC, f"C_sg{p}", [128, 512], F32) for p in range(2)]
            hbb = [self.sb(sC, f"C_h{p}", [128, 512], BF16) for p in range(2)]
            ps_g = [self.ps(sC, f"C_psg{j}", [128, 512], F32) for j in range(4)]
            ps_u = [self.ps(sC, f"C_psu{j}", [128, 512], F32) for j in range(4)]

            dqb2 = [self.sb(sC, f"C_dqb{p}", [128, 4096], BF16) for p in range(2)]
            dqr2 = self.dq[:].rearrange("(b p) o -> p b o", p=128)
            dqbf2 = self.dq_bf[:].rearrange("(b p) o -> p b o", p=128)
            war_dqb = [pro_bar, pro_bar]
            for i in range(IB):
                pw = i % 2
                lw = ctx.dma("pool", lambda e, i=i, pw=pw: e.dma_start(
                    out=dqb2[pw][:], in_=dqr2[:, i, :]),
                    deps=war_dqb[pw] if isinstance(war_dqb[pw], list) else [war_dqb[pw]], ch=f"dqc{pw}")
                war_dqb[pw] = ctx.dma("act", lambda e, i=i, pw=pw: e.dma_start(
                    out=dqbf2[:, i, :], in_=dqb2[pw][:]), deps=[lw], ch=f"dqcs{pw}")
            self.dq_bf_done = list(war_dqb)

            war_g = [pro_bar, pro_bar]
            war_u = [pro_bar, pro_bar]
            war_gsl = [pro_bar, pro_bar]
            war_usl = [pro_bar, pro_bar]
            war_psg = [[None]] * 4
            war_psu = [[None]] * 4
            war_tg = [None, None]
            war_tu = [None, None]
            war_sg = [None, None]
            war_h = [None, None]
            gqr = self.gq[:].rearrange("(hb p) i -> p hb i", p=128)
            uqr = self.uq[:].rearrange("(hb p) i -> p hb i", p=128)
            h_states = []
            for ib in range(IB):
                par = ib % 2
                i0 = ib * 128
                gdst = gbuf[par][:].rearrange("p (hb c) -> p hb c", hb=HB)
                LG = ctx.dma("pool", lambda e, i0=i0, gdst=gdst: e.dma_start(
                    out=gdst, in_=gqr[:, :, i0:i0 + 128]), deps=war_g[par], ch=f"g{par}")
                udst = ubuf[par][:].rearrange("p (hb c) -> p hb c", hb=HB)
                LU = ctx.dma("pool", lambda e, i0=i0, udst=udst: e.dma_start(
                    out=udst, in_=uqr[:, :, i0:i0 + 128]), deps=war_u[par], ch=f"u{par}")
                LSG = ctx.dma("pool", lambda e, i0=i0, par=par: e.dma_start(
                    out=gsl[par][:], in_=self.gb_sc[:, i0:i0 + 128]), deps=war_gsl[par], ch=f"sg{par}")
                LSU = ctx.dma("pool", lambda e, i0=i0, par=par: e.dma_start(
                    out=usl[par][:], in_=self.ub_sc[:, i0:i0 + 128]), deps=war_usl[par], ch=f"su{par}")
                g_last = None
                u_last = None
                for tc in range(4):
                    idx = (ib * 4 + tc) % 4
                    cols = slice(tc * 512, (tc + 1) * 512)
                    for hb in range(HB):
                        ctx.op("pe", lambda e, hb=hb, par=par, idx=idx, cols=cols: e.matmul(
                            ps_g[idx][:], gbuf[par][:, hb * 128:(hb + 1) * 128],
                            self.xn_fm[:, hb * NT + cols.start: hb * NT + cols.stop],
                            start=(hb == 0), stop=False),
                            deps=[LG] + war_psg[idx] if hb == 0 else ())
                    ctx.op("pe", lambda e, par=par, idx=idx, cols=cols: e.matmul(
                        ps_g[idx][:], gsl[par][:], self.xa_g[:, cols],
                        start=False, stop=False), deps=[LSG])
                    tok_g = ctx.op("pe", lambda e, idx=idx, cols=cols: e.matmul(
                        ps_g[idx][:], self.ones2[:], self.r1_2[:, cols],
                        start=False, stop=True), inc=True)
                    for hb in range(HB):
                        ctx.op("pe", lambda e, hb=hb, par=par, idx=idx, cols=cols: e.matmul(
                            ps_u[idx][:], ubuf[par][:, hb * 128:(hb + 1) * 128],
                            self.xn_fm[:, hb * NT + cols.start: hb * NT + cols.stop],
                            start=(hb == 0), stop=False),
                            deps=[LU] + war_psu[idx] if hb == 0 else ())
                    ctx.op("pe", lambda e, par=par, idx=idx, cols=cols: e.matmul(
                        ps_u[idx][:], usl[par][:], self.xa_u[:, cols],
                        start=False, stop=False), deps=[LSU])
                    tok_u = ctx.op("pe", lambda e, idx=idx, cols=cols: e.matmul(
                        ps_u[idx][:], self.ones2[:], self.r1_2[:, cols],
                        start=False, stop=True), inc=True)
                    if tc == 3:
                        g_last, u_last = tok_g, tok_u
                    p2 = (ib * 4 + tc) % 2
                    ttg = ctx.op("dve", lambda e, p2=p2, ib=ib, idx=idx: e.tensor_scalar(
                        tgb[p2][:], ps_g[idx][:], self.s_cols["g"][:, ib:ib + 1], None, ALU.mult),
                        deps=[tok_g, war_tg[p2], self.s_col_toks["g"]], inc=True)
                    tsg = ctx.op("act", lambda e, p2=p2: e.activation(
                        sgb[p2][:], tgb[p2][:], AF.Silu), deps=[ttg, war_sg[p2]], inc=True)
                    ttu = ctx.op("dve", lambda e, p2=p2, ib=ib, idx=idx: e.tensor_scalar(
                        tub[p2][:], ps_u[idx][:], self.s_cols["u"][:, ib:ib + 1], None, ALU.mult),
                        deps=[tok_u, war_tu[p2], self.s_col_toks["u"]], inc=True)
                    tth = ctx.op("dve", lambda e, p2=p2: e.tensor_mul(
                        hbb[p2][:], sgb[p2][:], tub[p2][:]),
                        deps=[tsg, ttu, war_h[p2]], inc=True)
                    sth = ctx.dma("sp", lambda e, p2=p2, i0=i0, cols=cols: e.dma_start(
                        out=self.h_sp[i0:i0 + 128, cols], in_=hbb[p2][:]), deps=[tth], ch=f"h{p2}")
                    if ib == 0 and tc == 0 and "cg" in self.debug:
                        cgbuf = self.sb(sC, "C_dbgcg", [128, 2 * 512], F32)
                        c1 = ctx.op("dve", lambda e, idx=idx: e.tensor_copy(
                            cgbuf[:, 0:512], ps_g[idx][:]), deps=[tok_g], inc=True)
                        c2 = ctx.op("dve", lambda e, p2=p2: e.tensor_copy(
                            cgbuf[:, 512:1024], tgb[p2][:]), deps=[ttg], inc=True)
                        ctx.dma("sp", lambda e: e.dma_start(
                            out=self.dbg["cg"][:], in_=cgbuf[:]), deps=[c1, c2], ch="dbg")
                        cg2 = self.sb(sC, "C_dbgcg2", [32, 1152], F32)
                        g1 = ctx.op("dve", lambda e, par=par: e.tensor_copy(
                            cg2[:, 0:128], gsl[par][:]), deps=[LSG], inc=True)
                        g2 = ctx.op("dve", lambda e: e.tensor_copy(
                            cg2[:, 128:640], self.xa_g[:, 0:512]), deps=[g1], inc=True)
                        g3 = ctx.op("dve", lambda e: e.tensor_copy(
                            cg2[:, 640:1152], self.r1_2[0:1, 0:512].partition_broadcast(1) if False else self.xa_u[:, 0:512]), deps=[g2], inc=True)
                        ctx.dma("sp", lambda e: e.dma_start(
                            out=self.dbg["cg2"][:], in_=cg2[:]), deps=[g3], ch="dbg")
                    war_psg[idx] = [ttg]
                    war_psu[idx] = [ttu]
                    war_tg[p2] = tsg
                    war_sg[p2] = tth
                    war_tu[p2] = tth
                    war_h[p2] = sth
                    h_states.append(sth)
                war_g[par] = (g_last,)
                war_u[par] = (u_last,)
                war_gsl[par] = (g_last,)
                war_usl[par] = (u_last,)
            self.h_done = h_states[-8:]
        self.c_end = ctx.barrier()

    def phase_D(self, st):
        nc, ctx = self.nc, self.ctx
        dbar = ctx.barrier()
        with ExitStack() as sD:
            hT = self.sb(sD, "D_hT", [128, IB * 512], BF16)
            ostrip = self.sb(sD, "D_o", [128, HB * 512], BF16)
            wdb = [self.sb(sD, f"D_wd{p}", [128, 43 * 256], BF16) for p in range(2)]
            wob = [self.sb(sD, f"D_wo{p}", [128, 16 * 256], BF16) for p in range(2)]
            dao = self.sb(sD, "D_dao", [128, IB * 33], BF16)
            dbsl = [self.sb(sD, f"D_db{p}", [32, 256], BF16) for p in range(2)]
            dbf32 = [self.sb(sD, f"D_dbf{p}", [32, 256], F32) for p in range(2)]
            dscrow = [self.sb(sD, f"D_dsc{p}", [1, 256], F32) for p in range(2)]
            sbdsl = [self.sb(sD, f"D_sbd{p}", [128, 256], F32) for p in range(2)]
            onesrf = self.sb(sD, "D_orf", [1, 128], F32)
            ones64 = self.sb(sD, "D_o64", [64, 1], F32)
            hdaf = self.sb(sD, "D_hdaf", [33, 512], F32)
            hdab = self.sb(sD, "D_hdab", [33, 512], BF16)
            hnegq = self.sb(sD, "D_hneg", [128, 4], F32)
            t1b = [self.sb(sD, f"D_t1{p}", [128, 256], F32) for p in range(2)]
            outb = [self.sb(sD, f"D_out{p}", [128, 256], F32) for p in range(2)]
            ps_ab = [self.ps(sD, f"D_psab{j}", [128, 512], F32) for j in range(4)]
            ps_a = [t[:, 0:256] for t in ps_ab]
            ps_b = [t[:, 256:512] for t in ps_ab]
            self._ps_hda = self.ps(sD, "D_pshda", [33, 512], F32)
            self._ps_hs = self.ps(sD, "D_pshs", [128, 4], F32)
            self._ps_sbd = self.ps(sD, "D_pssbd", [128, 256], F32)

            t_orf = ctx.dma("sp", lambda e: e.dma_start(
                out=onesrf[:], in_=self.cin["ones_row_f32"][:]), deps=dbar, ch="const")
            tm64 = ctx.op("dve", lambda e: e.memset(ones64[:], 1.0), deps=dbar, inc=True)
            daod = dao[:].rearrange("p (ib c) -> p ib c", c=33)
            tda = ctx.dma("pool", lambda e: e.dma_start(out=daod[:, :, 0:32], in_=self.da[:]), deps=dbar, ch="dao")
            tdo = ctx.op("dve", lambda e: e.memset(daod[:, :, 32:33], 1.0), deps=[tda], inc=True)

            dqr = self.dq_bf[:].rearrange("(ib p) o -> p ib o", p=128)
            wor = self.wo_bf[:].rearrange("(hb p) o -> p hb o", p=128)
            osr = self.o_sp[:].rearrange("(hb p) t -> p hb t", p=128)
            hspr = self.h_sp[:].rearrange("(ib p) t -> p ib t", p=128)
            hTd = hT[:].rearrange("p (ib t) -> p ib t", ib=IB)
            wdbr = [w[:].rearrange("p (ib o) -> p ib o", o=256) for w in wdb]
            wobr = [w[:].rearrange("p (hb o) -> p hb o", o=256) for w in wob]

            war_hT = list(dbar)
            war_ostrip = list(dbar)
            war_wd = [list(dbar), list(dbar)]
            war_wo = [list(dbar), list(dbar)]
            war_db = [list(dbar), list(dbar)]
            war_sbd = [[], []]
            war_hda = []
            war_hs = []
            war_t1 = [None] * 2
            war_out = [None] * 2
            war_psa = [[] for _ in range(4)]
            war_psb = [[] for _ in range(4)]
            for Q in range(4):
                q0 = Q * 512
                qcols = slice(q0, q0 + 512)
                hl = []
                for piece in range(4):
                    ibs = slice(piece * 22, min((piece + 1) * 22, IB))
                    hl.append(ctx.dma("sp", lambda e, ibs=ibs, qcols=qcols: e.dma_start(
                        out=hTd[:, ibs, :], in_=hspr[:, ibs, qcols]),
                        deps=war_hT + self.c_end, ch="hT"))
                ol = ctx.dma("sp", lambda e, qcols=qcols: e.dma_start(
                    out=ostrip[:].rearrange("p (hb t) -> p hb t", hb=HB),
                    in_=osr[:, :, qcols]), deps=war_ostrip + self.o_done, ch="ostrip")
                mm = None
                for ib in range(IB):
                    mm = ctx.op("pe", lambda e, ib=ib: e.matmul(
                        self._ps_hda[:], dao[:, ib * 33:(ib + 1) * 33],
                        hT[:, ib * 512:(ib + 1) * 512],
                        start=(ib == 0), stop=(ib == IB - 1)),
                        deps=([tdo] + hl + war_hda) if ib == 0 else (),
                        inc=(ib == IB - 1))
                ehf = ctx.op("dve", lambda e: e.tensor_copy(hdaf[:], self._ps_hda[:]),
                             deps=[mm] + war_hs, inc=True)
                war_hda = [ehf]
                ehb = ctx.op("dve", lambda e: e.tensor_copy(hdab[:], hdaf[:]), deps=[ehf], inc=True)
                hsm = None
                for tt in range(4):
                    hsm = ctx.op("pe", lambda e, tt=tt: e.matmul(
                        self._ps_hs[:, tt:tt + 1], hdaf[32:33, tt * 128:(tt + 1) * 128],
                        ones64[32:33, 0:1], start=True, stop=True),
                        deps=([ehf, tm64] + war_hs + war_sbd[0] + war_sbd[1]) if tt == 0 else (),
                        inc=(tt == 3))
                ehn = ctx.op("dve", lambda e: e.tensor_scalar(
                    hnegq[:], self._ps_hs[:], -7.5, None, ALU.mult), deps=[hsm], inc=True)
                war_hs = [ehn]

                for oc in range(16):
                    o0 = oc * 256
                    ocs = slice(o0, o0 + 256)
                    wdt = []
                    for piece in range(2):
                        pibs = slice(piece * 43, (piece + 1) * 43)
                        wdt.append(ctx.dma("sp", lambda e, pibs=pibs, ocs=ocs, piece=piece: e.dma_start(
                            out=wdbr[piece], in_=dqr[:, pibs, ocs]),
                            deps=war_wd[piece] + (self.dq_bf_done if Q == 0 and oc == 0 else []),
                            ch=f"wd{piece}"))
                    wot = []
                    for piece in range(2):
                        phbs = slice(piece * 16, (piece + 1) * 16)
                        wot.append(ctx.dma("act", lambda e, phbs=phbs, ocs=ocs, piece=piece: e.dma_start(
                            out=wobr[piece], in_=wor[:, phbs, ocs]),
                            deps=war_wo[piece] + (self.wo_bf_done if Q == 0 and oc == 0 else []),
                            ch=f"wo{piece}"))
                    pdb = oc % 2
                    ldbf = ctx.dma("sp", lambda e, ocs=ocs, pdb=pdb: e.dma_start(
                        out=dbf32[pdb][:], in_=self.dbm[:, ocs]), deps=war_db[pdb], ch=f"db{pdb}")
                    tdb = ctx.op("dve", lambda e, pdb=pdb: e.tensor_scalar(
                        dbsl[pdb][:], dbf32[pdb][:], 0.02, None, ALU.mult), deps=[ldbf], inc=True)
                    ldsc = ctx.dma("sp", lambda e, ocs=ocs, pdb=pdb: e.dma_start(
                        out=dscrow[pdb][0:1, :], in_=self.dsc[0:1, ocs]), deps=war_sbd[pdb], ch=f"dsc{pdb}")
                    tds = ctx.op("dve", lambda e, pdb=pdb: e.tensor_scalar(
                        dscrow[pdb][0:1, :], dscrow[pdb][0:1, :], 0.005, None, ALU.mult),
                        deps=[ldsc], inc=True)
                    msb = ctx.op("pe", lambda e, pdb=pdb: e.matmul(
                        self._ps_sbd[:], onesrf[:], dscrow[pdb][0:1, :], start=True, stop=True),
                        deps=[tds, t_orf] + war_sbd[pdb] + war_hs, inc=True)
                    esb = ctx.op("dve", lambda e, pdb=pdb: e.tensor_copy(
                        sbdsl[pdb][:], self._ps_sbd[:]), deps=[msb], inc=True)
                    war_sbd[pdb] = [esb]

                    tA_mid = None
                    tA = None
                    for ib in range(IB):
                        piece = 0 if ib < 43 else 1
                        off = ib - 43 * piece
                        for tt in range(4):
                            dep = ()
                            if ib == 0 and tt == 0:
                                dep = [wdt[0]] + [t for w in war_psa for t in w]
                            elif ib == 43 and tt == 0:
                                dep = [wdt[1]]
                            tok = ctx.op("pe", lambda e, ib=ib, tt=tt, piece=piece, off=off: e.matmul(
                                ps_a[tt], hT[:, ib * 512 + tt * 128: ib * 512 + (tt + 1) * 128],
                                wdb[piece][:, off * 256:(off + 1) * 256],
                                start=(ib == 0), stop=(ib == IB - 1)),
                                deps=dep, inc=(tt == 3 and ib in (42, IB - 1)))
                            if ib == 42 and tt == 3:
                                tA_mid = tok
                            if ib == IB - 1 and tt == 3:
                                tA = tok
                    war_wd = [[tA_mid], [tA]]
                    tB_mid = None
                    tB = None
                    for hb in range(HB):
                        piece = 0 if hb < 16 else 1
                        off = hb - 16 * piece
                        for tt in range(4):
                            dep = ()
                            if hb == 0 and tt == 0:
                                dep = [wot[0], ol] + [t for w in war_psb for t in w]
                            elif hb == 16 and tt == 0:
                                dep = [wot[1]]
                            tok = ctx.op("pe", lambda e, hb=hb, tt=tt, piece=piece, off=off: e.matmul(
                                ps_b[tt], ostrip[:, hb * 512 + tt * 128: hb * 512 + (tt + 1) * 128],
                                wob[piece][:, off * 256:(off + 1) * 256],
                                start=(hb == 0), stop=False),
                                deps=dep, inc=(tt == 3 and hb == 15))
                            if hb == 15 and tt == 3:
                                tB_mid = tok
                    tBs = []
                    for tt in range(4):
                        tB = ctx.op("pe", lambda e, tt=tt, pdb=pdb: e.matmul(
                            ps_b[tt], hdab[0:32, tt * 128:(tt + 1) * 128],
                            dbsl[pdb][:], start=False, stop=True),
                            deps=[tdb, ehb] if tt == 0 else (), inc=(tt == 3))
                        tBs.append(tB)
                    war_wo = [[tB_mid], [tB]]
                    war_db[pdb] = [tB]
                    for tt in range(4):
                        b2 = tt % 2
                        et1 = ctx.op("dve", lambda e, tt=tt, b2=b2, pdb=pdb: e.scalar_tensor_tensor(
                            t1b[b2][:], ps_a[tt], hnegq[:, tt:tt + 1], sbdsl[pdb][:],
                            ALU.add, ALU.mult),
                            deps=[tA, tBs[-1], ehn, esb, war_t1[b2]], inc=True)
                        eo = ctx.op("dve", lambda e, tt=tt, b2=b2: e.tensor_add(
                            outb[b2][:], t1b[b2][:], ps_b[tt]),
                            deps=[tBs[-1], et1, war_out[b2]], inc=True)
                        stt = ctx.dma("pool", lambda e, b2=b2, tt=tt, q0=q0, ocs=ocs: e.dma_start(
                            out=self.out[q0 + tt * 128: q0 + (tt + 1) * 128, ocs],
                            in_=outb[b2][:], accum_op=ALU.add),
                            deps=[eo] + self.out_prefill, ch=f"oacc{tt % 2}")
                        war_psa[tt] = [eo]
                        war_psb[tt] = [eo]
                        war_t1[b2] = eo
                        war_out[b2] = stt
                war_hT = [tA]
                war_ostrip = [tB_mid, tB]


def _prep_inputs(inputs):
    x = inputs["L0_attn_hidden_states"]
    common = {
        "wq": np.ascontiguousarray(inputs["L0_attn_Wq"]),
        "wk": np.ascontiguousarray(inputs["L0_attn_Wk"]),
        "wv": np.ascontiguousarray(inputs["L0_attn_Wv"]),
        "wo": np.ascontiguousarray(inputs["L0_attn_Wo"]),
        "gq": np.ascontiguousarray(inputs["L0_ffn_gate_q"].T),
        "uq": np.ascontiguousarray(inputs["L0_ffn_up_q"].T),
        "dq": np.ascontiguousarray(inputs["L0_ffn_down_q"].T),
        "gsc": np.ascontiguousarray(inputs["L0_ffn_gate_scale"].reshape(1, IFF)),
        "usc": np.ascontiguousarray(inputs["L0_ffn_up_scale"].reshape(1, IFF)),
        "dsc": np.ascontiguousarray(inputs["L0_ffn_down_scale"].reshape(1, DM)),
        "ga": np.ascontiguousarray(inputs["L0_ffn_gate_A"]),
        "gbm": np.ascontiguousarray(inputs["L0_ffn_gate_B"]),
        "ua": np.ascontiguousarray(inputs["L0_ffn_up_A"]),
        "ubm": np.ascontiguousarray(inputs["L0_ffn_up_B"]),
        "da": np.ascontiguousarray(inputs["L0_ffn_down_A"].reshape(IB, 128, 32).transpose(1, 0, 2)),
        "dbm": np.ascontiguousarray(inputs["L0_ffn_down_B"]),
    }
    in_maps = []
    for c in range(8):
        m = dict(common)
        m["x"] = np.ascontiguousarray(x[c])
        in_maps.append(m)
    return in_maps


def kernel(**inputs):
    in_maps = _prep_inputs(inputs)
    nc = Builder(phases=("A", "B", "C", "D")).build()
    res = run_bass_kernel_spmd(nc, in_maps, core_ids=list(range(8)))
    outs = [np.asarray(res.results[c]["out"]) for c in range(8)]
    return np.stack(outs, axis=0)


# version 07
# speedup vs baseline: 1.0771x; 1.0374x over previous
import numpy as np
import ml_dtypes
import concourse.bass as bass
import concourse.mybir as mybir
from concourse.bass_utils import run_bass_kernel_spmd
from contextlib import ExitStack

F32 = mybir.dt.float32
BF16 = mybir.dt.bfloat16
I32 = mybir.dt.int32
AF = mybir.ActivationFunctionType
ALU = mybir.AluOpType
AX = mybir.AxisListType
BF16NP = ml_dtypes.bfloat16

NT = 2048
DM = 4096
HB = 32
NTILE = 16
NGRP = 8
HPG = 4
DH = 128
IFF = 11008
IB = 86
EPS = 1e-5
INVSQ = float(1.0 / np.sqrt(128.0))
NEGMASK = -30000.0
LN2 = float(np.log(2.0))

ENGINES = ("pe", "act", "dve", "pool", "sp")
DMAQ = {"sp": "sp_dma", "act": "act_dma", "pool": "pool_dma"}


class Ctx:

    def __init__(self, nc):
        self.nc = nc
        self.streams = {e: [] for e in ENGINES}
        self.count = {k: 0 for k in ENGINES}
        self.vc = {e: {} for e in ENGINES}
        self.sems = {}

    def _waits_for(self, e, deps):
        waits = []
        for t in deps:
            if t is None:
                continue
            key, val, snap = t
            if self.vc[e].get(key, 0) < val:
                waits.append((key, val))
                for k2, v2 in snap.items():
                    if self.vc[e].get(k2, 0) < v2:
                        self.vc[e][k2] = v2
        return waits

    STRICT = ("dve", "act", "pool")

    def op(self, e, fn, deps=(), inc=False):
        waits = self._waits_for(e, deps)
        token = None
        semkey = None
        if inc or e in self.STRICT:
            semkey = e
            self.count[e] += 1
            snap = dict(self.vc[e])
            snap[e] = self.count[e]
            token = (e, self.count[e], snap)
            if e not in self.STRICT:
                self.vc[e][e] = self.count[e]
        self.streams[e].append((waits, fn, semkey, 1))
        return token

    def dma(self, e, fn, deps=(), ch="main"):
        waits = self._waits_for(e, deps)
        q = f"{e}:{ch}"
        self.count[q] = self.count.get(q, 0) + 16
        snap = dict(self.vc[e])
        snap[q] = self.count[q]
        token = (q, self.count[q], snap)
        self.vc[e][q] = self.count[q]
        self.streams[e].append((waits, fn, q, 16))
        return token

    def barrier(self):
        toks = []
        for k, v in self.count.items():
            if v > 0:
                toks.append((k, v, {k: v}))
        return toks

    def emit(self, block):
        starters = {"pe": block.tensor, "act": block.scalar, "dve": block.vector,
                    "pool": block.gpsimd, "sp": block.sync}
        final_counts = dict(self.count)
        for k, v in final_counts.items():
            if v > 0 and k not in self.sems:
                self.sems[k] = self.nc.alloc_semaphore(f"sem_{k.replace(':', '_')}")

        def body_for(name):
            def body(eng):
                for waits, fn, semkey, incval in self.streams[name]:
                    for (k, v) in waits:
                        eng.wait_ge(self.sems[k], v)
                    ins = fn(eng)
                    if semkey is not None:
                        ins.then_inc(self.sems[semkey], incval)
                if name == "sp":
                    for k, v in final_counts.items():
                        if v > 0:
                            eng.wait_ge(self.sems[k], v)
            return body

        for name in ENGINES:
            if self.streams[name] or name == "sp":
                starters[name](body_for(name))


def make_const_arrays():
    c = {}
    c["ident"] = np.eye(128, dtype=BF16NP)
    c["lt16"] = np.triu(np.full((128, 128), 1.0 / 16.0)).astype(BF16NP)
    c["ones16_col"] = np.full((128, 1), 1.0 / 16.0, dtype=BF16NP)
    c["ones_col_bf"] = np.ones((128, 1), dtype=BF16NP)
    c["ones_row_bf"] = np.ones((1, 128), dtype=BF16NP)
    c["ones_row_f32"] = np.ones((1, 128), dtype=np.float32)
    c["ones2"] = np.ones((2, 128), dtype=BF16NP)
    c["negln2_col"] = np.full((128, 1), -np.log(2.0), dtype=np.float32)
    m = (np.arange(128)[None, :] >= np.arange(128)[:, None]).astype(np.float32)
    c["gla_mask"] = np.tile(m, (1, 4)).astype(BF16NP)
    r = np.arange(128)[:, None]
    cc = np.arange(256)[None, :]
    valid = (cc >= r + 64) & (cc <= r + 128)
    mask = np.where(valid, 0.0, NEGMASK).astype(np.float32)
    c["swa_mask"] = mask
    mask0 = mask.copy()
    mask0[:, :128] = NEGMASK
    c["swa_mask0"] = mask0
    inv = 10000.0 ** (-np.arange(64) / 64.0)
    ang = np.arange(NT)[:, None] * inv[None, :]
    c["cos_t"] = np.cos(ang).reshape(NTILE, 128, 64).transpose(1, 0, 2).reshape(128, NTILE * 64).astype(BF16NP)
    c["sin_t"] = np.sin(ang).reshape(NTILE, 128, 64).transpose(1, 0, 2).reshape(128, NTILE * 64).astype(BF16NP)
    return c


class Builder:
    def __init__(self, debug=(), phases=("A",)):
        self.debug = set(debug)
        self.phases = set(phases)
        nc = bass.Bass("TRN2", target_bir_lowering=False)
        self.nc = nc
        self.x = nc.dram_tensor("x", [NT, DM], F32, kind="ExternalInput")
        self.wq = nc.dram_tensor("wq", [DM, DM], F32, kind="ExternalInput")
        self.wk = nc.dram_tensor("wk", [DM, 1024], F32, kind="ExternalInput")
        self.wv = nc.dram_tensor("wv", [DM, 1024], F32, kind="ExternalInput")
        self.wo = nc.dram_tensor("wo", [DM, DM], F32, kind="ExternalInput")
        self.gq = nc.dram_tensor("gq", [DM, IFF], I32, kind="ExternalInput")
        self.uq = nc.dram_tensor("uq", [DM, IFF], I32, kind="ExternalInput")
        self.dq = nc.dram_tensor("dq", [IFF, DM], I32, kind="ExternalInput")
        self.gsc = nc.dram_tensor("gsc", [1, IFF], F32, kind="ExternalInput")
        self.usc = nc.dram_tensor("usc", [1, IFF], F32, kind="ExternalInput")
        self.dsc = nc.dram_tensor("dsc", [1, DM], F32, kind="ExternalInput")
        self.ga = nc.dram_tensor("ga", [DM, 32], F32, kind="ExternalInput")
        self.gbm = nc.dram_tensor("gbm", [32, IFF], F32, kind="ExternalInput")
        self.ua = nc.dram_tensor("ua", [DM, 32], F32, kind="ExternalInput")
        self.ubm = nc.dram_tensor("ubm", [32, IFF], F32, kind="ExternalInput")
        self.da = nc.dram_tensor("da", [128, IB, 32], F32, kind="ExternalInput")
        self.dbm = nc.dram_tensor("dbm", [32, DM], F32, kind="ExternalInput")
        self.out = nc.dram_tensor("out", [NT, DM], F32, kind="ExternalOutput")
        self.h_sp = nc.dram_tensor("h_sp", [IFF, NT], BF16, kind="Internal")
        if "ozero" in self.debug:
            self.o_sp = nc.dram_tensor("o_sp", [DM, NT], BF16, kind="ExternalInput")
        else:
            self.o_sp = nc.dram_tensor("o_sp", [DM, NT], BF16, kind="Internal")
        self.o_done = []
        self.c_end = []
        self.gb_sc = nc.dram_tensor("gb_sc", [32, IFF], BF16, kind="Internal")
        self.ub_sc = nc.dram_tensor("ub_sc", [32, IFF], BF16, kind="Internal")
        self.r1_sp = nc.dram_tensor("r1_sp", [1, 2 * NT], BF16, kind="Internal")
        self.dq_bf = nc.dram_tensor("dq_bf", [IFF, DM], BF16, kind="Internal")
        self.wo_bf = nc.dram_tensor("wo_bf", [DM, DM], BF16, kind="Internal")
        self.cnp = make_const_arrays()
        self.cin = {k: nc.inline_tensor(v, name="c_" + k) for k, v in self.cnp.items()}
        self.dbg = {}
        if "xn" in self.debug:
            self.dbg["xn"] = nc.dram_tensor("dbg_xn", [DM, NT], BF16, kind="ExternalOutput")
        if "hcopy" in self.debug:
            self.dbg["hcopy"] = nc.dram_tensor("dbg_h", [IFF, NT], BF16, kind="ExternalOutput")
        if "astats" in self.debug:
            self.dbg["astats"] = nc.dram_tensor("dbg_astats", [128, 34], F32, kind="ExternalOutput")
        if "ocopy" in self.debug:
            self.dbg["ocopy"] = nc.dram_tensor("dbg_o", [DM, NT], BF16, kind="ExternalOutput")
        if "cg" in self.debug:
            self.dbg["cg"] = nc.dram_tensor("dbg_cg", [128, 2 * 512], F32, kind="ExternalOutput")
            self.dbg["cg2"] = nc.dram_tensor("dbg_cg2", [32, 1152], F32, kind="ExternalOutput")

    def sb(self, stack, name, shape, dtype):
        return stack.enter_context(self.nc.sbuf_tensor(name, shape, dtype))

    def ps(self, stack, name, shape, dtype=F32):
        return stack.enter_context(self.nc.psum_tensor(name, shape, dtype))

    def build(self):
        nc = self.nc
        with ExitStack() as st:
            ctx = Ctx(nc)
            self.ctx = ctx

            self.c_ident = self.sb(st, "sc_ident", [128, 128], BF16)

            t_id = ctx.dma("sp", lambda e: e.dma_start(out=self.c_ident[:], in_=self.cin["ident"][:]), ch="const")
            self.tok_ident = t_id

            self.out_prefill = []
            for i in range(4):
                rows = slice(i * 512, (i + 1) * 512)
                self.out_prefill.append(ctx.dma(
                    "act", lambda e, rows=rows: e.dma_start(out=self.out[rows, :], in_=self.x[rows, :]),
                    ch="prefill"))

            with ExitStack() as st_abc:
                self.xn_fm = self.sb(st_abc, "xn_fm", [128, HB * NT], BF16)
                self.phase_A(st_abc)
                if "B" in self.phases:
                    self.phase_B(st_abc)
                if "C" in self.phases:
                    self.phase_C(st_abc)
                if "xn" in self.debug:
                    xr = self.dbg["xn"][:].rearrange("(hb p) t -> p hb t", p=128)
                    sr = self.xn_fm[:].rearrange("p (hb t) -> p hb t", hb=HB)
                    ctx.dma("sp", lambda e: e.dma_start(out=xr, in_=sr),
                            deps=self.xn_done, ch="dbg")
            if "ocopy" in self.debug and "B" in self.phases:
                for i in range(2):
                    rows = slice(i * 2048, (i + 1) * 2048)
                    ctx.dma("act", lambda e, rows=rows: e.dma_start(
                        out=self.dbg["ocopy"][rows, :], in_=self.o_sp[rows, :]),
                        deps=self.b_end, ch="dbg")
            if "hcopy" in self.debug and "C" in self.phases:
                for i in range(4):
                    rows = slice(i * 2752, (i + 1) * 2752)
                    ctx.dma("act", lambda e, rows=rows: e.dma_start(
                        out=self.dbg["hcopy"][rows, :], in_=self.h_sp[rows, :]),
                        deps=self.c_end, ch="dbg")
            if "D" in self.phases:
                self.phase_D(st)

            with nc.Block() as block:
                ctx.emit(block)
        return nc

    def phase_A(self, st_abc):
        nc, ctx = self.nc, self.ctx
        with ExitStack() as sA:
            xbuf = [self.sb(sA, f"A_x{p}", [128, DM], F32) for p in range(2)]
            xn_tm = [self.sb(sA, f"A_xn{p}", [128, DM], BF16) for p in range(2)]
            sqscr = xn_tm
            s1c = self.sb(sA, "A_s1", [128, NTILE], F32)
            s2c = self.sb(sA, "A_s2", [128, NTILE], F32)
            negmc = self.sb(sA, "A_negm", [128, NTILE], F32)
            msqc = self.sb(sA, "A_msq", [128, NTILE], F32)
            varc = self.sb(sA, "A_var", [128, NTILE], F32)
            sdc = self.sb(sA, "A_sd", [128, NTILE], F32)
            rstdc = self.sb(sA, "A_rstd", [128, NTILE], F32)
            ps_tr = [self.ps(sA, f"A_pstr{b}", [128, 1024], BF16) for b in range(2)]
            ps_sum = self.ps(sA, "A_pssum", [1, 512], F32)

            war_x = [(), ()]
            war_xn = [None, None]
            war_bank = [None, None]
            evac_toks = []
            for t in range(NTILE):
                p = t % 2
                tok = slice(t * 128, (t + 1) * 128)
                L = ctx.dma("sp", lambda e, p=p, tok=tok: e.dma_start(
                    out=xbuf[p][:], in_=self.x[tok, :]), deps=war_x[p], ch=f"x{p}")
                tsq = ctx.op("act", lambda e, p=p, t=t: e.activation(
                    sqscr[p][:], xbuf[p][:], AF.Square, accum_out=s2c[:, t:t + 1]),
                    deps=[L, war_xn[p]], inc=True)
                ts1 = ctx.op("dve", lambda e, p=p, t=t: e.reduce_sum(
                    s1c[:, t:t + 1], xbuf[p][:], axis=AX.X), deps=[L], inc=True)
                negm = negmc[:, t:t + 1]
                msq = msqc[:, t:t + 1]
                varep = varc[:, t:t + 1]
                sd = sdc[:, t:t + 1]
                rstd = rstdc[:, t:t + 1]
                tnm = ctx.op("dve", lambda e, t=t, negm=negm: e.tensor_scalar(
                    negm, s1c[:, t:t + 1], -1.0 / DM, None, ALU.mult), deps=[ts1])
                tmq = ctx.op("dve", lambda e, negm=negm, msq=msq: e.tensor_mul(msq, negm, negm),
                             deps=[tnm])
                tva = ctx.op("dve", lambda e, t=t, varep=varep: e.tensor_scalar(
                    varep, s2c[:, t:t + 1], 1.0 / DM, EPS, ALU.mult, ALU.add),
                    deps=[tsq])
                tv = ctx.op("dve", lambda e, varep=varep, msq=msq: e.tensor_sub(varep, varep, msq),
                            deps=[tva, tmq], inc=True)
                tsd = ctx.op("act", lambda e, sd=sd, varep=varep: e.activation(sd, varep, AF.Sqrt),
                             deps=[tv], inc=True)
                tr = ctx.op("dve", lambda e, rstd=rstd, sd=sd: e.reciprocal(rstd, sd), deps=[tsd], inc=True)
                txn = ctx.op("dve", lambda e, p=p, negm=negm, rstd=rstd: e.tensor_scalar(
                    xn_tm[p][:], xbuf[p][:], negm, rstd, ALU.add, ALU.mult),
                    deps=[war_xn[p], L, tsq, tr, tnm], inc=True)
                war_x[p] = (txn, tsq)
                last_tr = None
                for g8 in range(4):
                    b = g8 % 2
                    grp_tok = None
                    for j in range(8):
                        d = g8 * 8 + j
                        grp_tok = ctx.op("pe", lambda e, p=p, d=d, b=b, j=j: e.transpose(
                            ps_tr[b][:, j * 128:(j + 1) * 128],
                            xn_tm[p][:, d * 128:(d + 1) * 128], self.c_ident[:]),
                            deps=[txn, self.tok_ident, war_bank[b]] if j == 0 else (),
                            inc=(j == 7))
                    dst = self.xn_fm[:].rearrange("p (hb t) -> p hb t", hb=HB)[:, g8 * 8:(g8 + 1) * 8, t * 128:(t + 1) * 128]
                    ev = ctx.op("dve", lambda e, b=b, dst=dst: e.tensor_copy(dst, ps_tr[b][:].rearrange("p (j t) -> p j t", j=8)),
                                deps=[grp_tok], inc=True)
                    war_bank[b] = ev
                    evac_toks.append(ev)
                    last_tr = grp_tok
                war_xn[p] = last_tr

            wob2 = [self.sb(sA, f"A_wob{p}", [128, 2048], BF16) for p in range(2)]
            wor2 = self.wo[:].rearrange("(b p) o -> p b o", p=128)
            wobf2 = self.wo_bf[:].rearrange("(b p) o -> p b o", p=128)
            war_wob = [None, None]
            for i in range(64):
                pw = i % 2
                blk, half = i // 2, (i % 2) * 0
                blk = i // 2
                oc0 = (i % 2) * 2048
                lw = ctx.dma("pool", lambda e, blk=blk, oc0=oc0, pw=pw: e.dma_start(
                    out=wob2[pw][:], in_=wor2[:, blk, oc0:oc0 + 2048]),
                    deps=[war_wob[pw]], ch=f"woc{pw}")
                war_wob[pw] = ctx.dma("act", lambda e, blk=blk, oc0=oc0, pw=pw: e.dma_start(
                    out=wobf2[:, blk, oc0:oc0 + 2048], in_=wob2[pw][:]),
                    deps=[lw], ch=f"wocs{pw}")
            self.wo_bf_done = list(war_wob)

            ones_c = self.sb(sA, "A_ones", [128, 1], BF16)
            xnsum_row = self.sb(sA, "A_xnsum", [1, NT], F32)
            tone = ctx.dma("sp", lambda e: e.dma_start(out=ones_c[:], in_=self.cin["ones_col_bf"][:]), ch="const")
            r1f = self.sb(sA, "A_r1f", [1, NT], F32)
            r1bf = self.sb(sA, "A_r1bf", [1, NT], BF16)
            ev = None
            sum_evs = []
            for ch in range(4):
                cols = slice(ch * 512, (ch + 1) * 512)
                mt = None
                for hb in range(HB):
                    mt = ctx.op("pe", lambda e, hb=hb, cols=cols: e.matmul(
                        ps_sum[:], ones_c[:],
                        self.xn_fm[:, hb * NT + cols.start: hb * NT + cols.stop],
                        start=(hb == 0), stop=(hb == HB - 1)),
                        deps=(evac_toks + [tone, ev]) if hb == 0 else (),
                        inc=(hb == HB - 1))
                ev = ctx.op("dve", lambda e, cols=cols: e.tensor_copy(
                    xnsum_row[0:1, cols], ps_sum[:]), deps=[mt], inc=True)
                sum_evs.append(ev)
            t1 = ctx.op("dve", lambda e: e.tensor_scalar(
                r1f[:], xnsum_row[:], -7.5, None, ALU.mult), deps=sum_evs, inc=True)
            t2 = ctx.op("dve", lambda e: e.tensor_copy(r1bf[:], r1f[:]), deps=[t1], inc=True)
            s1t = ctx.dma("sp", lambda e: e.dma_start(out=self.r1_sp[0:1, 0:NT], in_=r1bf[:]),
                          deps=[t2], ch="r1w")
            t3 = ctx.op("dve", lambda e: e.tensor_sub(r1f[:], r1f[:], r1bf[:]), deps=[t2])
            t4 = ctx.op("dve", lambda e: e.tensor_copy(r1bf[:], r1f[:]), deps=[t3, s1t], inc=True)
            s2t = ctx.dma("sp", lambda e: e.dma_start(out=self.r1_sp[0:1, NT:2 * NT], in_=r1bf[:]),
                          deps=[t4], ch="r1w")
            self.xn_done = evac_toks + [t4, s2t]
            if "astats" in self.debug:
                stat = self.sb(sA, "A_stat", [128, 34], F32)
                bar = ctx.barrier()
                d1 = ctx.op("dve", lambda e: e.tensor_copy(stat[:, 0:16], s1c[:]), deps=bar, inc=True)
                ctx.op("dve", lambda e: e.tensor_copy(stat[:, 16:32], s2c[:]))
                ctx.op("dve", lambda e: e.tensor_copy(stat[:, 32:33], negmc[:, NTILE - 1:NTILE]))
                d2 = ctx.op("dve", lambda e: e.tensor_copy(stat[:, 33:34], rstdc[:, NTILE - 1:NTILE]), inc=True)
                ctx.dma("sp", lambda e: e.dma_start(out=self.dbg["astats"][:], in_=stat[:]), deps=[d1, d2], ch="dbg")
            self.a_end = ctx.barrier()

    def phase_B(self, st_abc):
        nc, ctx = self.nc, self.ctx
        bbar = ctx.barrier()
        with ExitStack() as sB:
            wqb = self.sb(sB, "B_wq", [128, HB * 512], BF16)
            wkvb = self.sb(sB, "B_wkv", [128, HB * 256], BF16)
            v_tm = self.sb(sB, "B_vtm", [128, NTILE * 128], BF16)
            krf = self.sb(sB, "B_krf", [128, NTILE * 128], BF16)
            S_f = self.sb(sB, "B_S", [128, 128], F32)
            S_t1 = self.sb(sB, "B_St1", [128, 128], F32)
            S_bf = self.sb(sB, "B_Sbf", [128, 128], BF16)
            ea_c = self.sb(sB, "B_ea", [128, 1], F32)
            cos_sb = self.sb(sB, "B_cos", [128, NTILE * 64], BF16)
            sin_sb = self.sb(sB, "B_sin", [128, NTILE * 64], BF16)
            lt16_sb = self.sb(sB, "B_lt16", [128, 128], BF16)
            o16_sb = self.sb(sB, "B_o16", [128, 1], BF16)
            glam = self.sb(sB, "B_glam", [128, 512], BF16)
            swam = self.sb(sB, "B_swam", [128, 256], F32)
            nl2 = self.sb(sB, "B_nl2", [128, 1], F32)
            q_t = [self.sb(sB, f"B_q{p}", [128, 512], BF16) for p in range(2)]
            k_t = [self.sb(sB, f"B_k{p}", [128, 128], BF16) for p in range(2)]
            qe = self.sb(sB, "B_qe", [128, 512], BF16)
            ke = self.sb(sB, "B_ke", [128, 128], BF16)
            gsc1 = self.sb(sB, "B_gsc1", [128, 128], BF16)
            gsc2 = self.sb(sB, "B_gsc2", [128, 128], BF16)
            gs = self.sb(sB, "B_gs", [128, 128], BF16)
            eA_sb = self.sb(sB, "B_eA", [128, 128], BF16)
            enA_sb = self.sb(sB, "B_enA", [128, 128], BF16)
            qt_tm = self.sb(sB, "B_qttm", [128, 512], BF16)
            kt_tm = self.sb(sB, "B_kttm", [128, 128], BF16)
            qkfm = self.sb(sB, "B_qkfm", [128, 640], BF16)
            Pbf = self.sb(sB, "B_Pbf", [128, 512], BF16)
            obf = self.sb(sB, "B_obf", [128, 512], BF16)
            qrt = self.sb(sB, "B_qrt", [128, 128], BF16)
            rsc1 = self.sb(sB, "B_rsc1", [128, 64], BF16)
            rsc2 = self.sb(sB, "B_rsc2", [128, 64], BF16)
            qkrfm = self.sb(sB, "B_qkrfm", [128, 640], BF16)
            scs = self.sb(sB, "B_scs", [128, 256], F32)
            pe_b = self.sb(sB, "B_peb", [128, 256], BF16)
            psc = self.sb(sB, "B_psc", [128, 256], BF16)
            pT = self.sb(sB, "B_pT", [128, 256], BF16)
            nmx = self.sb(sB, "B_nmx", [128, 16], F32)
            se_c = self.sb(sB, "B_se", [128, 16], F32)
            rr_c = self.sb(sB, "B_rr", [128, 16], F32)
            ps_q = self.ps(sB, "B_psq", [128, 512], F32)
            ps_kv = self.ps(sB, "B_pskv", [128, 256], F32)
            ps_misc = self.ps(sB, "B_psmisc", [128, 512], F32)
            ps_P = self.ps(sB, "B_psP", [128, 512], F32)
            ps_o = self.ps(sB, "B_pso", [128, 512], F32)
            ps_S = self.ps(sB, "B_psS", [128, 128], F32)
            ps_t1 = self.ps(sB, "B_pst1", [128, 1024], BF16)
            ps_t2 = self.ps(sB, "B_pst2", [128, 1024], BF16)

            cl = bbar
            for nm, dst in (("cos_t", cos_sb), ("sin_t", sin_sb), ("lt16", lt16_sb),
                            ("ones16_col", o16_sb), ("gla_mask", glam), ("swa_mask", swam),
                            ("negln2_col", nl2)):
                tokc = ctx.dma("sp", lambda e, nm=nm, dst=dst: e.dma_start(
                    out=dst[:], in_=self.cin[nm][:]), deps=cl, ch="const")
            consts = [tokc, self.tok_ident]

            wqr = self.wq[:].rearrange("(hb p) o -> p hb o", p=128)
            wkr = self.wk[:].rearrange("(hb p) o -> p hb o", p=128)
            wvr = self.wv[:].rearrange("(hb p) o -> p hb o", p=128)
            wqbr = wqb[:].rearrange("p (hb o) -> p hb o", o=512)
            wkvbr = wkvb[:].rearrange("p (hb o) -> p hb o", o=256)
            ospr = self.o_sp[:].rearrange("(g j p) t -> p g j t", p=128, j=HPG)

            W = {"wq": bbar, "wkv": bbar, "psq": [None], "pskv": [None],
                 "qt": [None, None], "kt": [None, None], "vtm": [None],
                 "qe": None, "ke": None, "gs": None, "misc": [None], "P": [None],
                 "o": [None], "S": [None], "t1": [None], "t2": [None],
                 "qttm": None, "kttm": None, "qkfm": None, "Pbf": None, "obf": None,
                 "qrt": None, "rsc": None, "qkrfm": None, "scs": None, "peb": None,
                 "psc": None, "pT": None, "Sbf_ready": None, "Sf_ready": None,
                 "nmx": [None] * 16, "se": [None] * 16, "rr": [None] * 16,
                 "krf": [None], "ea": None, "eA": None, "enA": None,
                 "osp": [None, None]}
            qkv_toks = {}
            o_states = []
            LQ = {"wq": None, "wkv": None}

            def emit_qkv(g, t):
                par = t % 2
                mq = None
                for hb in range(HB):
                    mq = ctx.op("pe", lambda e, hb=hb, t=t: e.matmul(
                        ps_q[:], self.xn_fm[:, hb * NT + t * 128: hb * NT + (t + 1) * 128],
                        wqb[:, hb * 512:(hb + 1) * 512],
                        start=(hb == 0), stop=(hb == HB - 1)),
                        deps=([LQ["wq"]] + W["psq"]) if hb == 0 else (),
                        inc=(hb == HB - 1))
                mkv = None
                for hb in range(HB):
                    mkv = ctx.op("pe", lambda e, hb=hb, t=t: e.matmul(
                        ps_kv[:], self.xn_fm[:, hb * NT + t * 128: hb * NT + (t + 1) * 128],
                        wkvb[:, hb * 256:(hb + 1) * 256],
                        start=(hb == 0), stop=(hb == HB - 1)),
                        deps=([LQ["wkv"]] + W["pskv"]) if hb == 0 else (),
                        inc=(hb == HB - 1))
                qkv_toks[t] = (None, None, None, mq, mkv)
                return mq

            def emit_qkv_evacs(g, t):
                par = t % 2
                _, _, _, mq, mkv = qkv_toks[t]
                evq = ctx.op("dve", lambda e, par=par: e.tensor_copy(q_t[par][:], ps_q[:]),
                             deps=[mq, W["qt"][par]], inc=True)
                evk = ctx.op("dve", lambda e, par=par: e.tensor_copy(k_t[par][:], ps_kv[:, 0:128]),
                             deps=[mkv, W["kt"][par]], inc=True)
                evv = ctx.op("dve", lambda e, t=t: e.tensor_copy(
                    v_tm[:, t * 128:(t + 1) * 128], ps_kv[:, 128:256]),
                    deps=[mkv] + W["vtm"], inc=True)
                W["psq"] = [evq]
                W["pskv"] = [evk, evv]
                qkv_toks[t] = (evq, evk, evv, mq, mkv)

            def rope(dst, srcap, t, dep_src, extra_dep):
                cs = cos_sb[:, t * 64:(t + 1) * 64]
                sn = sin_sb[:, t * 64:(t + 1) * 64]
                d1 = ctx.op("dve", lambda e: e.tensor_mul(rsc1[:], srcap[:, 0:64], cs),
                            deps=[dep_src, W["rsc"], consts[0]] + ([extra_dep] if extra_dep else []), inc=True)
                d2 = ctx.op("pool", lambda e: e.tensor_mul(rsc2[:], srcap[:, 64:128], sn),
                            deps=[dep_src, W["rsc"], consts[0]] + ([extra_dep] if extra_dep else []), inc=True)
                d3 = ctx.op("dve", lambda e: e.tensor_sub(dst[:, 0:64], rsc1[:], rsc2[:]),
                            deps=[d1, d2], inc=True)
                d4 = ctx.op("dve", lambda e: e.tensor_mul(rsc1[:], srcap[:, 64:128], cs),
                            deps=[d3], inc=True)
                d5 = ctx.op("pool", lambda e: e.tensor_mul(rsc2[:], srcap[:, 0:64], sn),
                            deps=[d3], inc=True)
                d6 = ctx.op("dve", lambda e: e.tensor_add(dst[:, 64:128], rsc1[:], rsc2[:]),
                            deps=[d4, d5], inc=True)
                W["rsc"] = d6
                return d6

            def emit_att(g, t):
                par = t % 2
                evq, evk, evv, _, _ = qkv_toks[t]
                texp = []
                for j in range(4):
                    tmx = ctx.op("dve", lambda e, j=j, par=par: e.reduce_max(
                        nmx[:, j:j + 1], q_t[par][:, j * 128:(j + 1) * 128],
                        axis=AX.X, negate=True), deps=[evq, W["nmx"][j]], inc=True)
                    tx = ctx.op("act", lambda e, j=j, par=par: e.activation(
                        qe[:, j * 128:(j + 1) * 128], q_t[par][:, j * 128:(j + 1) * 128],
                        AF.Exp, bias=nmx[:, j:j + 1], accum_out=se_c[:, j:j + 1]),
                        deps=[tmx, W["qe"], W["se"][j]], inc=True)
                    trc = ctx.op("dve", lambda e, j=j: e.reciprocal(
                        rr_c[:, j:j + 1], se_c[:, j:j + 1]), deps=[tx, W["rr"][j]], inc=True)
                    W["nmx"][j] = tx
                    W["se"][j] = trc
                    texp.append((tx, trc))
                tmxk = ctx.op("dve", lambda e, par=par: e.reduce_max(
                    nmx[:, 4:5], k_t[par][:], axis=AX.X, negate=True),
                    deps=[evk, W["nmx"][4]], inc=True)
                txk = ctx.op("act", lambda e, par=par: e.activation(
                    ke[:], k_t[par][:], AF.Exp, bias=nmx[:, 4:5], accum_out=se_c[:, 4:5]),
                    deps=[tmxk, W["ke"], W["se"][4]], inc=True)
                trck = ctx.op("dve", lambda e: e.reciprocal(rr_c[:, 4:5], se_c[:, 4:5]),
                              deps=[txk, W["rr"][4]], inc=True)
                W["nmx"][4] = txk
                W["se"][4] = trck
                g1 = ctx.op("act", lambda e, par=par: e.activation(gsc1[:], k_t[par][:], AF.Abs),
                            deps=[evk, W["gs"]], inc=True)
                g2 = ctx.op("act", lambda e: e.activation(gsc2[:], gsc1[:], AF.Exp, scale=-1.0),
                            deps=[g1], inc=True)
                g3 = ctx.op("act", lambda e: e.activation(gsc1[:], gsc2[:], AF.Ln, bias=1.0),
                            deps=[g2], inc=True)
                g4 = ctx.op("dve", lambda e, par=par: e.tensor_scalar_min(gsc2[:], k_t[par][:], 0.0),
                            deps=[g3, evk], inc=True)
                g5 = ctx.op("dve", lambda e: e.tensor_sub(gs[:], gsc2[:], gsc1[:]),
                            deps=[g3, g4], inc=True)
                mA = ctx.op("pe", lambda e: e.matmul(
                    ps_misc[:, 0:128], lt16_sb[:], gs[:], start=True, stop=True),
                    deps=[g5, consts[0]] + W["misc"], inc=True)
                mal = ctx.op("pe", lambda e: e.matmul(
                    ps_misc[:, 128:129], gs[:], o16_sb[:], start=True, stop=True),
                    deps=[consts[0]], inc=True)
                W["gs"] = mal
                eAh = ctx.op("act", lambda e: e.activation(
                    eA_sb[:], ps_misc[:, 0:128], AF.Exp, bias=nl2[:, 0:1]),
                    deps=[mA, mal, W["eA"]], inc=True)
                enA = ctx.op("act", lambda e: e.activation(
                    enA_sb[:], ps_misc[:, 0:128], AF.Exp, scale=-1.0),
                    deps=[mA, W["enA"]], inc=True)
                eat = ctx.op("act", lambda e: e.activation(ea_c[:], ps_misc[:, 128:129], AF.Exp),
                             deps=[mal, W["ea"]], inc=True)
                tqs = []
                for j in range(4):
                    tq = ctx.op("dve", lambda e, j=j: e.scalar_tensor_tensor(
                        qt_tm[:, j * 128:(j + 1) * 128], qe[:, j * 128:(j + 1) * 128],
                        rr_c[:, j:j + 1], eA_sb[:], ALU.mult, ALU.mult),
                        deps=[texp[j][0], texp[j][1], eAh, W["qttm"]], inc=True)
                    tqs.append(tq)
                tkt = ctx.op("dve", lambda e: e.scalar_tensor_tensor(
                    kt_tm[:], ke[:], rr_c[:, 4:5], enA_sb[:], ALU.mult, ALU.mult),
                    deps=[txk, trck, enA, W["kttm"]], inc=True)
                W["qe"] = tqs[-1]
                W["ke"] = tkt
                W["eA"] = tqs[-1]
                W["enA"] = tkt
                for j in range(4):
                    W["rr"][j] = tqs[j]
                W["rr"][4] = tkt
                trt = None
                for j in range(4):
                    trt = ctx.op("pe", lambda e, j=j: e.transpose(
                        ps_t1[:, j * 128:(j + 1) * 128], qt_tm[:, j * 128:(j + 1) * 128],
                        self.c_ident[:]),
                        deps=[tqs[j], self.tok_ident] + (W["t1"] if j == 0 else []),
                        inc=False)
                trt = ctx.op("pe", lambda e: e.transpose(
                    ps_t1[:, 512:640], kt_tm[:], self.c_ident[:]), deps=[tkt], inc=True)
                evfm = ctx.op("dve", lambda e: e.tensor_copy(qkfm[:], ps_t1[:, 0:640]),
                              deps=[trt, W["qkfm"]], inc=True)
                W["t1"] = [evfm]
                W["qttm"] = evfm
                W["kttm"] = evfm
                mP = ctx.op("pe", lambda e: e.matmul(
                    ps_P[:], qkfm[:, 512:640], qkfm[:, 0:512], start=True, stop=True),
                    deps=[evfm] + W["P"], inc=True)
                evP = ctx.op("dve", lambda e: e.tensor_mul(Pbf[:], ps_P[:], glam[:]),
                             deps=[mP, W["Pbf"], consts[0]], inc=True)
                W["P"] = [evP]
                mo1 = ctx.op("pe", lambda e: e.matmul(
                    ps_o[:], S_bf[:], qkfm[:, 0:512], start=True, stop=False),
                    deps=[evfm, W["Sbf_ready"]] + W["o"], inc=True)
                mo2 = ctx.op("pe", lambda e: e.matmul(
                    ps_o[:], v_tm[:, t * 128:(t + 1) * 128], Pbf[:], start=False, stop=False),
                    deps=[evP, evv], inc=True)
                W["Pbf"] = mo2
                W["qkfm"] = mo1
                mS = ctx.op("pe", lambda e: e.matmul(
                    ps_S[:], kt_tm[:], v_tm[:, t * 128:(t + 1) * 128], start=True, stop=True),
                    deps=[tkt, evv] + W["S"], inc=True)
                tS1 = ctx.op("dve", lambda e: e.tensor_add(S_t1[:], ps_S[:], S_f[:]),
                             deps=[mS, W["Sf_ready"]], inc=True)
                W["S"] = [tS1]
                tS2 = ctx.op("dve", lambda e: e.tensor_scalar(
                    S_f[:], S_t1[:], ea_c[:, 0:1], None, ALU.mult), deps=[tS1, eat], inc=True)
                W["ea"] = tS2
                W["Sf_ready"] = tS2
                tS3 = ctx.op("dve", lambda e: e.tensor_copy(S_bf[:], S_f[:]),
                             deps=[tS2, mo1], inc=True)
                W["Sbf_ready"] = tS3
                krd = rope(qrt, k_t[par], t, evk, W["qrt"])
                trkr = ctx.op("pe", lambda e: e.transpose(
                    ps_t2[:, 512:640], qrt[:], self.c_ident[:]),
                    deps=[krd, self.tok_ident] + W["t2"], inc=True)
                W["qrt"] = trkr
                evkr = ctx.op("dve", lambda e, t=t: e.tensor_copy(
                    krf[:, t * 128:(t + 1) * 128], ps_t2[:, 512:640]),
                    deps=[trkr] + W["krf"], inc=True)
                trq = None
                qrd_prev = evkr
                for j in range(4):
                    qrd = rope(qrt, q_t[par][:, j * 128:(j + 1) * 128], t, evq, W["qrt"])
                    trq = ctx.op("pe", lambda e, j=j: e.transpose(
                        ps_t2[:, j * 128:(j + 1) * 128], qrt[:], self.c_ident[:]),
                        deps=[qrd, evkr], inc=True)
                    W["qrt"] = trq
                evqr = ctx.op("dve", lambda e: e.tensor_copy(qkrfm[:, 0:512], ps_t2[:, 0:512]),
                              deps=[trq, W["qkrfm"]], inc=True)
                last_sc = None
                last_pT_ev = evqr
                mo_last = None
                last_msc = None
                for j in range(4):
                    if t == 0:
                        msc = ctx.op("pe", lambda e, j=j: e.matmul(
                            ps_misc[:, 384:512], qkrfm[:, j * 128:(j + 1) * 128],
                            krf[:, 0:128], start=True, stop=True),
                            deps=[evqr, evkr, eAh, enA, eat] + ([last_sc] if last_sc else []),
                            inc=True)
                        tsc0 = ctx.op("dve", lambda e: e.memset(scs[:, 0:128], NEGMASK),
                                      deps=[W["scs"]], inc=True)
                        tscs = ctx.op("dve", lambda e: e.scalar_tensor_tensor(
                            scs[:, 128:256], ps_misc[:, 384:512], INVSQ, swam[:, 128:256],
                            ALU.mult, ALU.add), deps=[msc, tsc0, consts[0]], inc=True)
                    else:
                        msc = ctx.op("pe", lambda e, j=j, t=t: e.matmul(
                            ps_misc[:, 256:512], qkrfm[:, j * 128:(j + 1) * 128],
                            krf[:, (t - 1) * 128:(t + 1) * 128], start=True, stop=True),
                            deps=[evqr, evkr, eAh, enA, eat] + ([last_sc] if last_sc else []),
                            inc=True)
                        tscs = ctx.op("dve", lambda e: e.scalar_tensor_tensor(
                            scs[:], ps_misc[:, 256:512], INVSQ, swam[:],
                            ALU.mult, ALU.add), deps=[msc, W["scs"], consts[0]], inc=True)
                    last_sc = tscs
                    last_msc = msc
                    W["scs"] = tscs
                    smx = ctx.op("dve", lambda e, j=j: e.reduce_max(
                        nmx[:, 5 + j:6 + j], scs[:], axis=AX.X, negate=True),
                        deps=[tscs, W["nmx"][5 + j]], inc=True)
                    sxp = ctx.op("act", lambda e, j=j: e.activation(
                        pe_b[:], scs[:], AF.Exp, bias=nmx[:, 5 + j:6 + j],
                        accum_out=se_c[:, 5 + j:6 + j]),
                        deps=[smx, W["peb"], W["se"][5 + j]], inc=True)
                    src = ctx.op("dve", lambda e, j=j: e.reciprocal(
                        rr_c[:, 5 + j:6 + j], se_c[:, 5 + j:6 + j]),
                        deps=[sxp, W["rr"][5 + j]], inc=True)
                    tps = ctx.op("dve", lambda e, j=j: e.tensor_scalar(
                        psc[:], pe_b[:], rr_c[:, 5 + j:6 + j], 0.5, ALU.mult, ALU.mult),
                        deps=[sxp, src, W["psc"]], inc=True)
                    W["peb"] = tps
                    W["nmx"][5 + j] = sxp
                    W["se"][5 + j] = src
                    W["rr"][5 + j] = tps
                    trp = None
                    if t > 0:
                        trp1 = ctx.op("pe", lambda e: e.transpose(
                            ps_t2[:, 640:768], psc[:, 0:128], self.c_ident[:]),
                            deps=[tps, last_pT_ev], inc=False)
                    trp = ctx.op("pe", lambda e: e.transpose(
                        ps_t2[:, 768:896], psc[:, 128:256], self.c_ident[:]),
                        deps=[tps, last_pT_ev], inc=True)
                    evpT = ctx.op("dve", lambda e: e.tensor_copy(
                        pT[:], ps_t2[:, 640:896]), deps=[trp, W["pT"]], inc=True)
                    last_pT_ev = evpT
                    W["psc"] = evpT
                    if t > 0:
                        mo_last = ctx.op("pe", lambda e, j=j, t=t: e.matmul(
                            ps_o[:, j * 128:(j + 1) * 128], v_tm[:, (t - 1) * 128:t * 128],
                            pT[:, 0:128], start=False, stop=False), deps=[evpT], inc=False)
                    mo_last = ctx.op("pe", lambda e, j=j, t=t: e.matmul(
                        ps_o[:, j * 128:(j + 1) * 128], v_tm[:, t * 128:(t + 1) * 128],
                        pT[:, 128:256], start=False, stop=(j == 3)), deps=[evpT],
                        inc=(j == 3))
                    W["pT"] = mo_last
                W["misc"] = [last_sc]
                W["t2"] = [last_pT_ev]
                W["qkrfm"] = last_msc
                evo = ctx.op("dve", lambda e: e.tensor_copy(obf[:], ps_o[:]),
                             deps=[mo_last, W["obf"]], inc=True)
                W["o"] = [evo]
                sto = ctx.dma("sp", lambda e, g=g, t=t: e.dma_start(
                    out=ospr[:, g, :, t * 128:(t + 1) * 128],
                    in_=obf[:].rearrange("p (j q) -> p j q", j=HPG)),
                    deps=[evo], ch=f"osp{par}")
                W["obf"] = sto
                W["qt"][par] = evo
                W["kt"][par] = evo
                o_states.append(sto)
                return sto

            war_wq = bbar
            war_wkv = bbar
            for g in range(NGRP):
                gc = slice(g * 512, (g + 1) * 512)
                LWQ = ctx.dma("pool", lambda e, gc=gc: e.dma_start(
                    out=wqbr, in_=wqr[:, :, gc]), deps=war_wq, ch="wq")
                kc = slice(g * 128, (g + 1) * 128)
                LWK = ctx.dma("pool", lambda e, kc=kc: e.dma_start(
                    out=wkvbr[:, :, 0:128], in_=wkr[:, :, kc]), deps=war_wkv, ch="wkv")
                LWV = ctx.dma("pool", lambda e, kc=kc: e.dma_start(
                    out=wkvbr[:, :, 128:256], in_=wvr[:, :, kc]), ch="wkv")
                LQ["wq"] = LWQ
                LQ["wkv"] = LWV
                ms1 = ctx.op("dve", lambda e: e.memset(S_f[:], 0.0),
                             deps=[W["Sf_ready"], W["S"][0]] if W["Sf_ready"] else bbar, inc=True)
                ms2 = ctx.op("dve", lambda e: e.memset(S_bf[:], 0.0),
                             deps=[ms1, W["Sbf_ready"]], inc=True)
                W["Sf_ready"] = ms1
                W["Sbf_ready"] = ms2
                lastq = None
                for t in range(NTILE + 1):
                    if t < NTILE:
                        lastq = emit_qkv(g, t)
                    if t >= 1:
                        emit_att(g, t - 1)
                    if t < NTILE:
                        emit_qkv_evacs(g, t)
                war_wq = [lastq]
                war_wkv = [qkv_toks[NTILE - 1][4]]
                W["vtm"] = [o_states[-1]]
                W["krf"] = [o_states[-1]]
            self.o_done = o_states[-4:]
        self.b_end = ctx.barrier()

    def phase_C(self, st_abc):
        nc, ctx = self.nc, self.ctx
        cbar = ctx.barrier()
        with ExitStack() as sC:
            self.s_cols = {"g": self.sb(sC, "C_scol_g", [128, IB], F32),
                           "u": self.sb(sC, "C_scol_u", [128, IB], F32)}
            self.s_col_toks = {}
            self.xa_g = self.sb(sC, "C_xag", [32, NT], BF16)
            self.xa_u = self.sb(sC, "C_xau", [32, NT], BF16)
            self.r1_2 = self.sb(sC, "C_r12", [2, NT], BF16)
            ones2 = self.sb(sC, "C_ones2", [2, 128], BF16)
            self.ones2 = ones2
            with ExitStack() as sP:
                srow = self.sb(sP, "C_srow", [1, 2048], F32)
                rrow = self.sb(sP, "C_rrow", [1, 256], F32)
                rrow2 = self.sb(sP, "C_rrow2", [1, 256], F32)
                bchunk = self.sb(sP, "C_bch", [32, 256], F32)
                bsc_ch = self.sb(sP, "C_bsc", [32, 256], BF16)
                onesrf = self.sb(sP, "C_orf", [1, 128], F32)
                gaua = self.sb(sP, "C_gaua", [128, HB * 64], BF16)
                ps_scol = self.ps(sP, "C_psscol", [128, IB], F32)
                ps_bc = self.ps(sP, "C_psbc", [32, 256], F32)
                ps_xa = self.ps(sP, "C_psxa", [64, 512], F32)
                t_orf = ctx.dma("sp", lambda e: e.dma_start(
                    out=onesrf[:], in_=self.cin["ones_row_f32"][:]), deps=cbar, ch="const")

                pro_toks = []
                mm_last = None
                prev_ev = None
                for mname, scin in (("g", self.gsc), ("u", self.usc)):
                    scol = self.s_cols[mname]
                    for ch in range(6):
                        c0 = ch * 2048
                        w = min(2048, IFF - c0)
                        ld = ctx.dma("sp", lambda e, c0=c0, w=w, scin=scin: e.dma_start(
                            out=srow[0:1, 0:w], in_=scin[0:1, c0:c0 + w]),
                            deps=([mm_last] if mm_last else cbar) + [prev_ev], ch="prosc")
                        tsc = ctx.op("dve", lambda e, w=w: e.tensor_scalar(
                            srow[0:1, 0:w], srow[0:1, 0:w], 0.005, None, ALU.mult),
                            deps=[ld], inc=True)
                        for j in range(w // 128):
                            ib = c0 // 128 + j
                            mm_last = ctx.op("pe", lambda e, j=j, ib=ib: e.matmul(
                                ps_scol[:, ib:ib + 1], srow[0:1, j * 128:(j + 1) * 128],
                                onesrf[0:1, 0:1], start=True, stop=True),
                                deps=[tsc, t_orf, prev_ev], inc=True)
                    ev = ctx.op("dve", lambda e, scol=scol: e.tensor_copy(scol[:], ps_scol[:]),
                                deps=[mm_last], inc=True)
                    prev_ev = ev
                    self.s_col_toks[mname] = ev
                    pro_toks.append(ev)

                for mname, bin_, bout, scin in (("g", self.gbm, self.gb_sc, self.gsc),
                                                ("u", self.ubm, self.ub_sc, self.usc)):
                    prev_st = None
                    for ch in range(43):
                        c0 = ch * 256
                        l1 = ctx.dma("sp", lambda e, c0=c0, scin=scin: e.dma_start(
                            out=rrow[0:1, :], in_=scin[0:1, c0:c0 + 256]),
                            deps=[prev_st] if prev_st else cbar, ch="prob1")
                        l2 = ctx.dma("sp", lambda e, c0=c0, bin_=bin_: e.dma_start(
                            out=bchunk[:], in_=bin_[:, c0:c0 + 256]),
                            deps=[prev_st] if prev_st else cbar, ch="prob2")
                        tre = ctx.op("dve", lambda e: e.reciprocal(rrow2[0:1, :], rrow[0:1, :]),
                                     deps=[l1], inc=True)
                        mmb = ctx.op("pe", lambda e: e.matmul(
                            ps_bc[:], onesrf[0:1, 0:32], rrow2[0:1, :],
                            start=True, stop=True), deps=[tre, t_orf], inc=True)
                        tst = ctx.op("dve", lambda e: e.scalar_tensor_tensor(
                            bsc_ch[:], bchunk[:], 4.0, ps_bc[:], ALU.mult, ALU.mult),
                            deps=[mmb, l2], inc=True)
                        prev_st = ctx.dma("sp", lambda e, c0=c0, bout=bout: e.dma_start(
                            out=bout[:, c0:c0 + 256], in_=bsc_ch[:]), deps=[tst], ch="probst")

                gar = self.ga[:].rearrange("(hb p) r -> p hb r", p=128)
                uar = self.ua[:].rearrange("(hb p) r -> p hb r", p=128)
                gr = gaua[:].rearrange("p (hb c) -> p hb c", hb=HB)
                lga = ctx.dma("pool", lambda e: e.dma_start(out=gr[:, :, 0:32], in_=gar), deps=cbar, ch="xa")
                lua = ctx.dma("pool", lambda e: e.dma_start(out=gr[:, :, 32:64], in_=uar), deps=cbar, ch="xa")
                prev_ev = None
                for tc in range(4):
                    cols = slice(tc * 512, (tc + 1) * 512)
                    mm = None
                    for hb in range(HB):
                        mm = ctx.op("pe", lambda e, hb=hb, cols=cols: e.matmul(
                            ps_xa[:], gaua[:, hb * 64:(hb + 1) * 64],
                            self.xn_fm[:, hb * NT + cols.start: hb * NT + cols.stop],
                            start=(hb == 0), stop=(hb == HB - 1)),
                            deps=([lga, lua, prev_ev] + self.xn_done) if hb == 0 else (),
                            inc=(hb == HB - 1))
                    e1 = ctx.op("dve", lambda e, cols=cols: e.tensor_copy(
                        self.xa_g[:, cols], ps_xa[0:32, :]), deps=[mm], inc=True)
                    e2 = ctx.op("dve", lambda e, cols=cols: e.tensor_copy(
                        self.xa_u[:, cols], ps_xa[32:64, :]), deps=[mm], inc=True)
                    prev_ev = e2
                    pro_toks += [e1, e2]

                d1 = ctx.dma("sp", lambda e: e.dma_start(
                    out=self.r1_2[0:1, :], in_=self.r1_sp[0:1, 0:NT]), deps=cbar, ch="r12")
                d2 = ctx.dma("sp", lambda e: e.dma_start(
                    out=self.r1_2[1:2, :], in_=self.r1_sp[0:1, NT:2 * NT]), deps=cbar, ch="r12")
                d3 = ctx.dma("sp", lambda e: e.dma_start(out=ones2[:], in_=self.cin["ones2"][:]), ch="const")
                pro_toks += [d1, d2, d3]

            pro_bar = ctx.barrier()

            gbuf = [self.sb(sC, f"C_G{p}", [128, HB * 128], BF16) for p in range(2)]
            ubuf = [self.sb(sC, f"C_U{p}", [128, HB * 128], BF16) for p in range(2)]
            gsl = [self.sb(sC, f"C_gsl{p}", [32, 128], BF16) for p in range(2)]
            usl = [self.sb(sC, f"C_usl{p}", [32, 128], BF16) for p in range(2)]
            tgb = [self.sb(sC, f"C_tg{p}", [128, 512], F32) for p in range(2)]
            tub = [self.sb(sC, f"C_tu{p}", [128, 512], F32) for p in range(2)]
            sgb = [self.sb(sC, f"C_sg{p}", [128, 512], F32) for p in range(2)]
            hbb = [self.sb(sC, f"C_h{p}", [128, 512], BF16) for p in range(2)]
            ps_g = [self.ps(sC, f"C_psg{j}", [128, 512], F32) for j in range(4)]
            ps_u = [self.ps(sC, f"C_psu{j}", [128, 512], F32) for j in range(4)]

            dqb2 = [self.sb(sC, f"C_dqb{p}", [128, 4096], BF16) for p in range(2)]
            dqr2 = self.dq[:].rearrange("(b p) o -> p b o", p=128)
            dqbf2 = self.dq_bf[:].rearrange("(b p) o -> p b o", p=128)
            war_dqb = [pro_bar, pro_bar]
            for i in range(IB):
                pw = i % 2
                lw = ctx.dma("pool", lambda e, i=i, pw=pw: e.dma_start(
                    out=dqb2[pw][:], in_=dqr2[:, i, :]),
                    deps=war_dqb[pw] if isinstance(war_dqb[pw], list) else [war_dqb[pw]], ch=f"dqc{pw}")
                war_dqb[pw] = ctx.dma("act", lambda e, i=i, pw=pw: e.dma_start(
                    out=dqbf2[:, i, :], in_=dqb2[pw][:]), deps=[lw], ch=f"dqcs{pw}")
            self.dq_bf_done = list(war_dqb)

            war_g = [pro_bar, pro_bar]
            war_u = [pro_bar, pro_bar]
            war_gsl = [pro_bar, pro_bar]
            war_usl = [pro_bar, pro_bar]
            war_psg = [[None]] * 4
            war_psu = [[None]] * 4
            war_tg = [None, None]
            war_tu = [None, None]
            war_sg = [None, None]
            war_h = [None, None]
            gqr = self.gq[:].rearrange("(hb p) i -> p hb i", p=128)
            uqr = self.uq[:].rearrange("(hb p) i -> p hb i", p=128)
            h_states = []
            for ib in range(IB):
                par = ib % 2
                i0 = ib * 128
                gdst = gbuf[par][:].rearrange("p (hb c) -> p hb c", hb=HB)
                LG = ctx.dma("pool", lambda e, i0=i0, gdst=gdst: e.dma_start(
                    out=gdst, in_=gqr[:, :, i0:i0 + 128]), deps=war_g[par], ch=f"g{par}")
                udst = ubuf[par][:].rearrange("p (hb c) -> p hb c", hb=HB)
                LU = ctx.dma("pool", lambda e, i0=i0, udst=udst: e.dma_start(
                    out=udst, in_=uqr[:, :, i0:i0 + 128]), deps=war_u[par], ch=f"u{par}")
                LSG = ctx.dma("pool", lambda e, i0=i0, par=par: e.dma_start(
                    out=gsl[par][:], in_=self.gb_sc[:, i0:i0 + 128]), deps=war_gsl[par], ch=f"sg{par}")
                LSU = ctx.dma("pool", lambda e, i0=i0, par=par: e.dma_start(
                    out=usl[par][:], in_=self.ub_sc[:, i0:i0 + 128]), deps=war_usl[par], ch=f"su{par}")
                g_last = None
                u_last = None
                for tc in range(4):
                    idx = (ib * 4 + tc) % 4
                    cols = slice(tc * 512, (tc + 1) * 512)
                    for hb in range(HB):
                        ctx.op("pe", lambda e, hb=hb, par=par, idx=idx, cols=cols: e.matmul(
                            ps_g[idx][:], gbuf[par][:, hb * 128:(hb + 1) * 128],
                            self.xn_fm[:, hb * NT + cols.start: hb * NT + cols.stop],
                            start=(hb == 0), stop=False),
                            deps=[LG] + war_psg[idx] if hb == 0 else ())
                    ctx.op("pe", lambda e, par=par, idx=idx, cols=cols: e.matmul(
                        ps_g[idx][:], gsl[par][:], self.xa_g[:, cols],
                        start=False, stop=False), deps=[LSG])
                    tok_g = ctx.op("pe", lambda e, idx=idx, cols=cols: e.matmul(
                        ps_g[idx][:], self.ones2[:], self.r1_2[:, cols],
                        start=False, stop=True), inc=True)
                    for hb in range(HB):
                        ctx.op("pe", lambda e, hb=hb, par=par, idx=idx, cols=cols: e.matmul(
                            ps_u[idx][:], ubuf[par][:, hb * 128:(hb + 1) * 128],
                            self.xn_fm[:, hb * NT + cols.start: hb * NT + cols.stop],
                            start=(hb == 0), stop=False),
                            deps=[LU] + war_psu[idx] if hb == 0 else ())
                    ctx.op("pe", lambda e, par=par, idx=idx, cols=cols: e.matmul(
                        ps_u[idx][:], usl[par][:], self.xa_u[:, cols],
                        start=False, stop=False), deps=[LSU])
                    tok_u = ctx.op("pe", lambda e, idx=idx, cols=cols: e.matmul(
                        ps_u[idx][:], self.ones2[:], self.r1_2[:, cols],
                        start=False, stop=True), inc=True)
                    if tc == 3:
                        g_last, u_last = tok_g, tok_u
                    p2 = (ib * 4 + tc) % 2
                    ttg = ctx.op("dve", lambda e, p2=p2, ib=ib, idx=idx: e.tensor_scalar(
                        tgb[p2][:], ps_g[idx][:], self.s_cols["g"][:, ib:ib + 1], None, ALU.mult),
                        deps=[tok_g, war_tg[p2], self.s_col_toks["g"]], inc=True)
                    tsg = ctx.op("act", lambda e, p2=p2: e.activation(
                        sgb[p2][:], tgb[p2][:], AF.Silu), deps=[ttg, war_sg[p2]], inc=True)
                    ttu = ctx.op("dve", lambda e, p2=p2, ib=ib, idx=idx: e.tensor_scalar(
                        tub[p2][:], ps_u[idx][:], self.s_cols["u"][:, ib:ib + 1], None, ALU.mult),
                        deps=[tok_u, war_tu[p2], self.s_col_toks["u"]], inc=True)
                    tth = ctx.op("dve", lambda e, p2=p2: e.tensor_mul(
                        hbb[p2][:], sgb[p2][:], tub[p2][:]),
                        deps=[tsg, ttu, war_h[p2]], inc=True)
                    sth = ctx.dma("sp", lambda e, p2=p2, i0=i0, cols=cols: e.dma_start(
                        out=self.h_sp[i0:i0 + 128, cols], in_=hbb[p2][:]), deps=[tth], ch=f"h{p2}")
                    if ib == 0 and tc == 0 and "cg" in self.debug:
                        cgbuf = self.sb(sC, "C_dbgcg", [128, 2 * 512], F32)
                        c1 = ctx.op("dve", lambda e, idx=idx: e.tensor_copy(
                            cgbuf[:, 0:512], ps_g[idx][:]), deps=[tok_g], inc=True)
                        c2 = ctx.op("dve", lambda e, p2=p2: e.tensor_copy(
                            cgbuf[:, 512:1024], tgb[p2][:]), deps=[ttg], inc=True)
                        ctx.dma("sp", lambda e: e.dma_start(
                            out=self.dbg["cg"][:], in_=cgbuf[:]), deps=[c1, c2], ch="dbg")
                        cg2 = self.sb(sC, "C_dbgcg2", [32, 1152], F32)
                        g1 = ctx.op("dve", lambda e, par=par: e.tensor_copy(
                            cg2[:, 0:128], gsl[par][:]), deps=[LSG], inc=True)
                        g2 = ctx.op("dve", lambda e: e.tensor_copy(
                            cg2[:, 128:640], self.xa_g[:, 0:512]), deps=[g1], inc=True)
                        g3 = ctx.op("dve", lambda e: e.tensor_copy(
                            cg2[:, 640:1152], self.r1_2[0:1, 0:512].partition_broadcast(1) if False else self.xa_u[:, 0:512]), deps=[g2], inc=True)
                        ctx.dma("sp", lambda e: e.dma_start(
                            out=self.dbg["cg2"][:], in_=cg2[:]), deps=[g3], ch="dbg")
                    war_psg[idx] = [ttg]
                    war_psu[idx] = [ttu]
                    war_tg[p2] = tsg
                    war_sg[p2] = tth
                    war_tu[p2] = tth
                    war_h[p2] = sth
                    h_states.append(sth)
                war_g[par] = (g_last,)
                war_u[par] = (u_last,)
                war_gsl[par] = (g_last,)
                war_usl[par] = (u_last,)
            self.h_done = h_states[-8:]
        self.c_end = ctx.barrier()

    def phase_D(self, st):
        nc, ctx = self.nc, self.ctx
        dbar = ctx.barrier()
        with ExitStack() as sD:
            hT = self.sb(sD, "D_hT", [128, IB * 512], BF16)
            ostrip = self.sb(sD, "D_o", [128, HB * 512], BF16)
            wdb = [self.sb(sD, f"D_wd{p}", [128, 43 * 256], BF16) for p in range(2)]
            wob = [self.sb(sD, f"D_wo{p}", [128, 16 * 256], BF16) for p in range(2)]
            dao = self.sb(sD, "D_dao", [128, IB * 33], BF16)
            dbsl = [self.sb(sD, f"D_db{p}", [32, 256], BF16) for p in range(2)]
            dbf32 = [self.sb(sD, f"D_dbf{p}", [32, 256], F32) for p in range(2)]
            dscrow = [self.sb(sD, f"D_dsc{p}", [1, 256], F32) for p in range(2)]
            sbdsl = [self.sb(sD, f"D_sbd{p}", [128, 256], F32) for p in range(2)]
            onesrf = self.sb(sD, "D_orf", [1, 128], F32)
            ones64 = self.sb(sD, "D_o64", [64, 1], F32)
            hdaf = self.sb(sD, "D_hdaf", [33, 512], F32)
            hdab = self.sb(sD, "D_hdab", [33, 512], BF16)
            hnegq = self.sb(sD, "D_hneg", [128, 4], F32)
            t1b = [self.sb(sD, f"D_t1{p}", [128, 256], F32) for p in range(4)]
            outb = [self.sb(sD, f"D_out{p}", [128, 256], F32) for p in range(2)]
            ps_aa = [self.ps(sD, f"D_psaa{j}", [128, 512], F32) for j in range(2)]
            ps_bb = [self.ps(sD, f"D_psbb{j}", [128, 512], F32) for j in range(2)]
            ps_a = [ps_aa[0][:, 0:256], ps_aa[1][:, 0:256],
                    ps_aa[0][:, 256:512], ps_aa[1][:, 256:512]]
            ps_b = [ps_bb[0][:, 0:256], ps_bb[1][:, 0:256],
                    ps_bb[0][:, 256:512], ps_bb[1][:, 256:512]]
            self._ps_hda = self.ps(sD, "D_pshda", [33, 512], F32)
            self._ps_hs = self.ps(sD, "D_pshs", [128, 4], F32)
            self._ps_sbd = self.ps(sD, "D_pssbd", [128, 256], F32)

            t_orf = ctx.dma("sp", lambda e: e.dma_start(
                out=onesrf[:], in_=self.cin["ones_row_f32"][:]), deps=dbar, ch="const")
            tm64 = ctx.op("dve", lambda e: e.memset(ones64[:], 1.0), deps=dbar, inc=True)
            daod = dao[:].rearrange("p (ib c) -> p ib c", c=33)
            tda = ctx.dma("pool", lambda e: e.dma_start(out=daod[:, :, 0:32], in_=self.da[:]), deps=dbar, ch="dao")
            tdo = ctx.op("dve", lambda e: e.memset(daod[:, :, 32:33], 1.0), deps=[tda], inc=True)

            dqr = self.dq_bf[:].rearrange("(ib p) o -> p ib o", p=128)
            wor = self.wo_bf[:].rearrange("(hb p) o -> p hb o", p=128)
            osr = self.o_sp[:].rearrange("(hb p) t -> p hb t", p=128)
            hspr = self.h_sp[:].rearrange("(ib p) t -> p ib t", p=128)
            hTd = hT[:].rearrange("p (ib t) -> p ib t", ib=IB)
            wdbr = [w[:].rearrange("p (ib o) -> p ib o", o=256) for w in wdb]
            wobr = [w[:].rearrange("p (hb o) -> p hb o", o=256) for w in wob]

            war_hT = list(dbar)
            war_ostrip = list(dbar)
            war_wd = [list(dbar), list(dbar)]
            war_wo = [list(dbar), list(dbar)]
            war_db = [list(dbar), list(dbar)]
            war_sbd = [[], []]
            war_hda = []
            war_hs = []
            war_t1 = [None] * 4
            war_out = [None] * 2
            war_psa = [[] for _ in range(4)]
            war_psb = [[] for _ in range(4)]
            for Q in range(4):
                q0 = Q * 512
                qcols = slice(q0, q0 + 512)
                hl = []
                for piece in range(4):
                    ibs = slice(piece * 22, min((piece + 1) * 22, IB))
                    hl.append(ctx.dma("sp", lambda e, ibs=ibs, qcols=qcols: e.dma_start(
                        out=hTd[:, ibs, :], in_=hspr[:, ibs, qcols]),
                        deps=war_hT + self.c_end, ch="hT"))
                ol = ctx.dma("sp", lambda e, qcols=qcols: e.dma_start(
                    out=ostrip[:].rearrange("p (hb t) -> p hb t", hb=HB),
                    in_=osr[:, :, qcols]), deps=war_ostrip + self.o_done, ch="ostrip")
                mm = None
                for ib in range(IB):
                    piece_dep = []
                    if ib == 0:
                        piece_dep = [tdo, hl[0]] + war_hda
                    elif ib % 22 == 0:
                        piece_dep = [hl[ib // 22]]
                    mm = ctx.op("pe", lambda e, ib=ib: e.matmul(
                        self._ps_hda[:], dao[:, ib * 33:(ib + 1) * 33],
                        hT[:, ib * 512:(ib + 1) * 512],
                        start=(ib == 0), stop=(ib == IB - 1)),
                        deps=piece_dep, inc=(ib == IB - 1))
                ehf = ctx.op("dve", lambda e: e.tensor_copy(hdaf[:], self._ps_hda[:]),
                             deps=[mm] + war_hs, inc=True)
                war_hda = [ehf]
                ehb = ctx.op("dve", lambda e: e.tensor_copy(hdab[:], hdaf[:]), deps=[ehf], inc=True)
                hsm = None
                for tt in range(4):
                    hsm = ctx.op("pe", lambda e, tt=tt: e.matmul(
                        self._ps_hs[:, tt:tt + 1], hdaf[32:33, tt * 128:(tt + 1) * 128],
                        ones64[32:33, 0:1], start=True, stop=True),
                        deps=([ehf, tm64] + war_hs + war_sbd[0] + war_sbd[1]) if tt == 0 else (),
                        inc=(tt == 3))
                ehn = ctx.op("dve", lambda e: e.tensor_scalar(
                    hnegq[:], self._ps_hs[:], -7.5, None, ALU.mult), deps=[hsm], inc=True)
                war_hs = [ehn]

                for oc in range(16):
                    o0 = oc * 256
                    ocs = slice(o0, o0 + 256)
                    wdt = []
                    for piece in range(2):
                        pibs = slice(piece * 43, (piece + 1) * 43)
                        wdt.append(ctx.dma("sp", lambda e, pibs=pibs, ocs=ocs, piece=piece: e.dma_start(
                            out=wdbr[piece], in_=dqr[:, pibs, ocs]),
                            deps=war_wd[piece] + (self.dq_bf_done if Q == 0 and oc == 0 else []),
                            ch=f"wd{piece}"))
                    wot = []
                    for piece in range(2):
                        phbs = slice(piece * 16, (piece + 1) * 16)
                        wot.append(ctx.dma("act", lambda e, phbs=phbs, ocs=ocs, piece=piece: e.dma_start(
                            out=wobr[piece], in_=wor[:, phbs, ocs]),
                            deps=war_wo[piece] + (self.wo_bf_done if Q == 0 and oc == 0 else []),
                            ch=f"wo{piece}"))
                    pdb = oc % 2
                    ldbf = ctx.dma("sp", lambda e, ocs=ocs, pdb=pdb: e.dma_start(
                        out=dbf32[pdb][:], in_=self.dbm[:, ocs]), deps=war_db[pdb], ch=f"db{pdb}")
                    tdb = ctx.op("dve", lambda e, pdb=pdb: e.tensor_scalar(
                        dbsl[pdb][:], dbf32[pdb][:], 0.02, None, ALU.mult), deps=[ldbf], inc=True)
                    ldsc = ctx.dma("sp", lambda e, ocs=ocs, pdb=pdb: e.dma_start(
                        out=dscrow[pdb][0:1, :], in_=self.dsc[0:1, ocs]), deps=war_sbd[pdb], ch=f"dsc{pdb}")
                    tds = ctx.op("dve", lambda e, pdb=pdb: e.tensor_scalar(
                        dscrow[pdb][0:1, :], dscrow[pdb][0:1, :], 0.005, None, ALU.mult),
                        deps=[ldsc], inc=True)
                    msb = ctx.op("pe", lambda e, pdb=pdb: e.matmul(
                        self._ps_sbd[:], onesrf[:], dscrow[pdb][0:1, :], start=True, stop=True),
                        deps=[tds, t_orf] + war_sbd[pdb] + war_hs, inc=True)
                    esb = ctx.op("dve", lambda e, pdb=pdb: e.tensor_copy(
                        sbdsl[pdb][:], self._ps_sbd[:]), deps=[msb], inc=True)
                    war_sbd[pdb] = [esb]

                    tA_mid = None
                    tA = None
                    et1s = []
                    for tt in range(4):
                        for ib in range(IB):
                            piece = 0 if ib < 43 else 1
                            off = ib - 43 * piece
                            dep = ()
                            if ib == 0:
                                dep = war_psa[tt] + ([wdt[0]] if tt == 0 else [])
                                if tt >= 2:
                                    dep = dep + [et1s[tt - 2]]
                            elif ib == 43 and tt == 0:
                                dep = [wdt[1]]
                            tok = ctx.op("pe", lambda e, ib=ib, tt=tt, piece=piece, off=off: e.matmul(
                                ps_a[tt], hT[:, ib * 512 + tt * 128: ib * 512 + (tt + 1) * 128],
                                wdb[piece][:, off * 256:(off + 1) * 256],
                                start=(ib == 0), stop=(ib == IB - 1)),
                                deps=dep, inc=(ib in (42, IB - 1)))
                            if ib == 42 and tt == 3:
                                tA_mid = tok
                            if ib == IB - 1:
                                tA = tok
                        et1 = ctx.op("dve", lambda e, tt=tt, pdb=pdb: e.scalar_tensor_tensor(
                            t1b[tt][:], ps_a[tt], hnegq[:, tt:tt + 1], sbdsl[pdb][:],
                            ALU.add, ALU.mult),
                            deps=[tA, ehn, esb, war_t1[tt]], inc=True)
                        et1s.append(et1)
                        if tt >= 2:
                            war_psa[tt - 2] = [et1s[tt - 2], et1]
                    war_psa[2] = [et1s[0], et1s[2]]
                    war_psa[3] = [et1s[1], et1s[3]]
                    war_wd = [[tA_mid], [tA]]
                    eos = []
                    for tt in range(4):
                        b2 = tt % 2
                        for hb in range(HB):
                            piece = 0 if hb < 16 else 1
                            off = hb - 16 * piece
                            dep = ()
                            if hb == 0:
                                dep = war_psb[tt] + [ol] + ([wot[0]] if tt == 0 else [])
                                if tt >= 2:
                                    dep = dep + [eos[tt - 2]]
                            elif hb == 16 and tt == 0:
                                dep = [wot[1]]
                            ctx.op("pe", lambda e, hb=hb, tt=tt, piece=piece, off=off: e.matmul(
                                ps_b[tt], ostrip[:, hb * 512 + tt * 128: hb * 512 + (tt + 1) * 128],
                                wob[piece][:, off * 256:(off + 1) * 256],
                                start=(hb == 0), stop=False),
                                deps=dep, inc=False)
                        tB = ctx.op("pe", lambda e, tt=tt, pdb=pdb: e.matmul(
                            ps_b[tt], hdab[0:32, tt * 128:(tt + 1) * 128],
                            dbsl[pdb][:], start=False, stop=True),
                            deps=[tdb, ehb] if tt == 0 else (), inc=True)
                        eo = ctx.op("dve", lambda e, tt=tt, b2=b2: e.tensor_add(
                            outb[b2][:], t1b[tt][:], ps_b[tt]),
                            deps=[tB, et1s[tt], war_out[b2]], inc=True)
                        stt = ctx.dma("pool", lambda e, b2=b2, tt=tt, q0=q0, ocs=ocs: e.dma_start(
                            out=self.out[q0 + tt * 128: q0 + (tt + 1) * 128, ocs],
                            in_=outb[b2][:], accum_op=ALU.add),
                            deps=[eo] + self.out_prefill, ch=f"oacc{tt % 2}")
                        eos.append(eo)
                        war_t1[tt] = eo
                        war_out[b2] = stt
                        if tt >= 2:
                            war_psb[tt - 2] = [eos[tt - 2], eo]
                    war_psb[2] = [eos[0], eos[2]]
                    war_psb[3] = [eos[1], eos[3]]
                    tB_mid = tB
                war_hT = [tA]
                war_ostrip = [tB_mid, tB]


def _prep_inputs(inputs):
    x = inputs["L0_attn_hidden_states"]
    common = {
        "wq": np.ascontiguousarray(inputs["L0_attn_Wq"]),
        "wk": np.ascontiguousarray(inputs["L0_attn_Wk"]),
        "wv": np.ascontiguousarray(inputs["L0_attn_Wv"]),
        "wo": np.ascontiguousarray(inputs["L0_attn_Wo"]),
        "gq": np.ascontiguousarray(inputs["L0_ffn_gate_q"].T),
        "uq": np.ascontiguousarray(inputs["L0_ffn_up_q"].T),
        "dq": np.ascontiguousarray(inputs["L0_ffn_down_q"].T),
        "gsc": np.ascontiguousarray(inputs["L0_ffn_gate_scale"].reshape(1, IFF)),
        "usc": np.ascontiguousarray(inputs["L0_ffn_up_scale"].reshape(1, IFF)),
        "dsc": np.ascontiguousarray(inputs["L0_ffn_down_scale"].reshape(1, DM)),
        "ga": np.ascontiguousarray(inputs["L0_ffn_gate_A"]),
        "gbm": np.ascontiguousarray(inputs["L0_ffn_gate_B"]),
        "ua": np.ascontiguousarray(inputs["L0_ffn_up_A"]),
        "ubm": np.ascontiguousarray(inputs["L0_ffn_up_B"]),
        "da": np.ascontiguousarray(inputs["L0_ffn_down_A"].reshape(IB, 128, 32).transpose(1, 0, 2)),
        "dbm": np.ascontiguousarray(inputs["L0_ffn_down_B"]),
    }
    in_maps = []
    for c in range(8):
        m = dict(common)
        m["x"] = np.ascontiguousarray(x[c])
        in_maps.append(m)
    return in_maps


def kernel(**inputs):
    in_maps = _prep_inputs(inputs)
    nc = Builder(phases=("A", "B", "C", "D")).build()
    res = run_bass_kernel_spmd(nc, in_maps, core_ids=list(range(8)))
    outs = [np.asarray(res.results[c]["out"]) for c in range(8)]
    return np.stack(outs, axis=0)
